# version 03
# baseline (speedup 1.0000x reference)
import sys
sys.path.insert(0, "/opt/trn_rl_repo")
import numpy as np
import ml_dtypes
import concourse.bass as bass
import concourse.mybir as mybir
from concourse.bass_utils import run_bass_kernel_spmd
from contextlib import ExitStack

F32 = mybir.dt.float32
BF16 = mybir.dt.bfloat16
AF = mybir.ActivationFunctionType
ALU = mybir.AluOpType

B, L, D, Hn, DK = 16, 4096, 1024, 16, 64
DFF, RK = 4096, 256
NC_, BLOC = 8, 2
TOKT = 512
NTT = (BLOC * L) // TOKT
T2 = 2048
GELU_C = 0.7978845608028654
GELU_A = 0.044715
EPS = 1e-5

ENGS = ["pe", "act", "dve", "sp", "pool"]
BLOCKNAME = {"pe": "tensor", "act": "scalar", "dve": "vector", "sp": "sync", "pool": "gpsimd"}
INC = {"pe": 1, "act": 1, "dve": 1, "sp": 16, "pool": 16}


class View:
    __slots__ = ("h", "p0", "np_", "c0", "nc_", "esz")

    def __init__(self, h, p0, np_, c0, nc_, esz):
        self.h, self.p0, self.np_, self.c0, self.nc_, self.esz = h, p0, np_, c0, nc_, esz

    def ap(self):
        return self.h[self.p0:self.p0 + self.np_, self.c0:self.c0 + self.nc_]

    def reg(self):
        return (id(self.h), self.p0, self.p0 + self.np_,
                self.c0 * self.esz, (self.c0 + self.nc_) * self.esz)

    def sub(self, dp0, np_, dc0, nc_):
        return View(self.h, self.p0 + dp0, np_, self.c0 + dc0, nc_, self.esz)


def dreg(h, r0, r1, rowbytes):
    return (id(h), 0, 1, r0 * rowbytes, r1 * rowbytes)


class Builder:
    def __init__(self):
        self.ops = []

    def op(self, eng, reads, writes, emit):
        self.ops.append((eng, emit, tuple(r.reg() if isinstance(r, View) else r for r in reads),
                         tuple(w.reg() if isinstance(w, View) else w for w in writes)))

    SERIALIZE = False
    SELF_WAITS = True
    NSLOT = 8

    def emit(self, nc, block, sems):
        n = len(self.ops)
        eng_seq = [None] * n
        seqs = {e: 0 for e in ENGS}
        events = {}
        deps = [dict() for _ in range(n)]

        def inter(a, b):
            return a[1] > b[0] and b[1] > a[0]

        for i, (eng, emit, reads, writes) in enumerate(self.ops):
            myseq = seqs[eng]
            seqs[eng] += 1
            eng_seq[i] = (eng, myseq)
            d = deps[i]

            def dep(ev):
                peng, pseq = ev[4], ev[5]
                if peng == eng and not Builder.SELF_WAITS:
                    return
                if peng == eng and pseq == myseq:
                    return
                if d.get(peng, -1) < pseq:
                    d[peng] = pseq

            for r in reads:
                evs = events.get(r[0])
                if evs:
                    pr = (r[1], r[2]); br = (r[3], r[4])
                    for ev in evs:
                        if ev[6] and inter((ev[0], ev[1]), pr) and inter((ev[2], ev[3]), br):
                            dep(ev)
            for w in writes:
                evs = events.get(w[0])
                if evs:
                    pr = (w[1], w[2]); br = (w[3], w[4])
                    for ev in evs:
                        if inter((ev[0], ev[1]), pr) and inter((ev[2], ev[3]), br):
                            dep(ev)
            for r in reads:
                evs = events.setdefault(r[0], [])
                evs[:] = [ev for ev in evs
                          if ev[6] or ev[4] != eng or
                          (ev[0], ev[1], ev[2], ev[3]) != (r[1], r[2], r[3], r[4])]
                evs.append([r[1], r[2], r[3], r[4], eng, myseq, False, i])
            for w in writes:
                evs = events.setdefault(w[0], [])
                pr = (w[1], w[2]); br = (w[3], w[4])
                evs[:] = [ev for ev in evs
                          if not (ev[0] >= w[1] and ev[1] <= w[2] and ev[2] >= w[3] and ev[3] <= w[4])]
                evs.append([w[1], w[2], w[3], w[4], eng, myseq, True, i])

        need_inc = [True] * n
        idx_of = {}
        for i in range(n):
            idx_of[eng_seq[i]] = i
        for i in range(n):
            for peng, pseq in deps[i].items():
                need_inc[idx_of[(peng, pseq)]] = True
        last_sp = None
        for i in range(n):
            if eng_seq[i][0] == "sp":
                last_sp = i
        if last_sp is not None:
            need_inc[last_sp] = True

        cum = {e: [] for e in ENGS}
        cnt = {e: 0 for e in ENGS}
        for i in range(n):
            eng, myseq = eng_seq[i]
            if need_inc[i]:
                cnt[eng] += 1
            cum[eng].append(cnt[eng])

        waits = [[] for _ in range(n)]
        waited = {e: {p: 0 for p in ENGS} for e in ENGS}
        if self.SERIALIZE:
            done = {e: 0 for e in ENGS}
            for i in range(n):
                eng = eng_seq[i][0]
                for peng in ENGS:
                    if peng == eng:
                        continue
                    val = done[peng] * INC[peng]
                    if val > 0 and waited[eng][peng] < val:
                        waited[eng][peng] = val
                        waits[i].append((peng, val))
                if need_inc[i]:
                    done[eng] += 1
        else:
            for i in range(n):
                eng = eng_seq[i][0]
                for peng, pseq in sorted(deps[i].items()):
                    val = cum[peng][pseq] * INC[peng]
                    assert cum[peng][pseq] > 0
                    if waited[eng][peng] < val:
                        waited[eng][peng] = val
                        waits[i].append((peng, val))

        engine_ops = {e: [] for e in ENGS}
        for i in range(n):
            engine_ops[eng_seq[i][0]].append(i)

        total_inc = {e: cnt[e] * INC[e] for e in ENGS}

        NSLOT = Builder.NSLOT

        def sem_of(eng, seq):
            if eng in ("sp", "pool"):
                return (eng, seq % NSLOT)
            return (eng, 0)

        def val_of(eng, seq):
            if eng in ("sp", "pool"):
                return 16 * (seq // NSLOT + 1)
            return cum[eng][seq] * INC[eng]

        waits = [[] for _ in range(n)]
        waited2 = {}
        if self.SERIALIZE:
            last = {}
            for i in range(n):
                eng, myseq = eng_seq[i]
                for key, val in list(last.items()):
                    if key[0] == eng and key[0] not in ("sp", "pool"):
                        continue
                    if sem_of(*eng_seq[i]) == key:
                        continue
                    wk = (eng, key)
                    if waited2.get(wk, 0) < val:
                        waited2[wk] = val
                        waits[i].append((key, val))
                last[sem_of(eng, myseq)] = val_of(eng, myseq)
        else:
            for i in range(n):
                eng = eng_seq[i][0]
                for peng, pseq in sorted(deps[i].items()):
                    key = sem_of(peng, pseq)
                    val = val_of(peng, pseq)
                    wk = (eng, key)
                    if waited2.get(wk, 0) < val:
                        waited2[wk] = val
                        waits[i].append((key, val))

        gate = [None] * n
        for i in range(n):
            eng, myseq = eng_seq[i]
            if eng in ("sp", "pool") and myseq >= NSLOT:
                gate[i] = (sem_of(eng, myseq), 16 * (myseq // NSLOT))

        for eng in ENGS:
            ops_list = engine_ops[eng]
            if not ops_list and eng != "sp":
                continue

            def make_body(eng, ops_list):
                def body(e):
                    for i in ops_list:
                        if gate[i] is not None:
                            key, val = gate[i]
                            e.wait_ge(sems[key], val)
                        for (key, val) in waits[i]:
                            e.wait_ge(sems[key], val)
                        inst = self.ops[i][1](e)
                        myeng, myseq = eng_seq[i]
                        if need_inc[i] or myeng in ("sp", "pool"):
                            inst.then_inc(sems[sem_of(myeng, myseq)], INC[myeng])
                    if eng in ("sp", "pool"):
                        nops = len(ops_list)
                        for sl in range(min(NSLOT, nops)):
                            cnt_s = (nops - sl + NSLOT - 1) // NSLOT
                            e.wait_ge(sems[(eng, sl)], 16 * cnt_s)
                return body

            getattr(block, BLOCKNAME[eng])(make_body(eng, ops_list))


def build_program(dbg=(), phases=tuple(range(1, 14))):
    nc = bass.Bass("TRN2", target_bir_lowering=False)
    dt4, dt2 = 4, 2

    def din(name, shape, dt=F32):
        return nc.dram_tensor(name, shape, dt, kind="ExternalInput")

    xT = din("xT", [BLOC * D, L])
    wqkvT = din("wqkvT", [D, 3 * D])
    woT = din("woT", [D, D])
    u1 = din("u1", [D, RK]); v1 = din("v1", [RK, DFF])
    u2 = din("u2", [DFF, RK]); v2 = din("v2", [RK, D])
    wvgsT = din("wvgsT", [D, Hn])
    b1c_in = din("b1c", [128, 32]); b2c_in = din("b2c", [128, 8])
    wcols_in = din("wcols", [128, 512])
    wltmT = din("wltmT", [Hn, DK, DK])
    sel_in = din("selc", [32, 128, 64], BF16)
    blk_in = din("blkc", [8, 128, 16])
    bc16_in = din("bc16c", [8, 16, 128]); bc16n_in = din("bc16nc", [8, 16, 128])
    onesdiv_in = din("onesdiv", [128, 1])
    bc1_in = din("bc1", [1, 128]); bc1n_in = din("bc1n", [1, 128])
    ones16_in = din("ones16", [1, 16])
    bt_in = din("betatau", [1, 16]); cc_in = din("cch", [1, 16])
    gm_in = din("gammav", [1, 16]); bg_in = din("betagm", [1, 16])
    al_in = din("alphap", [1, 1]); ic_in = din("icth", [1, 1])
    vgsb_in = din("vgsbase", [16]); vt0_in = din("vt0", [16])

    outT = nc.dram_tensor("outT", [BLOC * D, L], F32, kind="ExternalOutput")

    def scr(name, shape, dt=F32):
        kind = "ExternalOutput" if name in dbg else "Internal"
        return nc.dram_tensor(name, shape, dt, kind=kind)

    x1s = scr("x1s", [BLOC * D, L]); x2s = scr("x2s", [BLOC * D, L])
    qs = scr("qs", [BLOC * D, L]); ks = scr("ks", [BLOC * D, L]); vs = scr("vs", [BLOC * D, L])
    ss = scr("ss", [BLOC * Hn, L])
    ys = scr("ys", [BLOC * D, L], BF16)
    xlns = scr("xlns", [BLOC * D, L], BF16)
    p1s = scr("p1s", [BLOC * RK, L], BF16)
    q2s = scr("q2s", [BLOC * RK, L], BF16)
    hs = scr("hs", [BLOC * DFF, L], BF16)
    wb_qkv = scr("wb_qkv", [D, 3 * D], BF16)
    wb_wo = scr("wb_wo", [D, D], BF16)
    wb_u1 = scr("wb_u1", [D, RK], BF16)
    wb_v1 = scr("wb_v1", [RK, DFF], BF16)
    wb_u2h = scr("wb_u2h", [DFF, RK], BF16)
    wb_v2 = scr("wb_v2", [RK, D], BF16)
    wb_vgs = scr("wb_vgs", [D, Hn], BF16)
    cbounce = scr("cbounce", [4, 16])

    bd = Builder()

    with ExitStack() as ctx:
        ARF_h = ctx.enter_context(nc.sbuf_tensor("ARF", [128, 38912], F32))
        ARB_h = ctx.enter_context(nc.sbuf_tensor("ARB", [128, 27648], BF16))
        PS_h = ctx.enter_context(nc.psum_tensor("PS", [128, 4096], F32))
        sems = {}
        for e in ("pe", "act", "dve"):
            sems[(e, 0)] = ctx.enter_context(nc.semaphore("sem_" + e))
        for e in ("sp", "pool"):
            for sl in range(Builder.NSLOT):
                sems[(e, sl)] = ctx.enter_context(nc.semaphore(f"sem_{e}{sl}"))
        block = ctx.enter_context(nc.Block())

        def F(p0, np_, c0, nc_):
            return View(ARF_h, p0, np_, c0, nc_, 4)

        def Bv(p0, np_, c0, nc_):
            return View(ARB_h, p0, np_, c0, nc_, 2)

        def P(p0, np_, c0, nc_):
            return View(PS_h, p0, np_, c0, nc_, 4)

        v_sbc = F(0, 128, 0, 4096)
        v_krep = F(0, 128, 4096, 4096)
        v_qrep = F(0, 128, 8192, 4096)
        v_vh = F(64, 64, 12288, 4096)
        v_bc16 = F(0, 16, 12288, 1024)
        v_bc16n = F(0, 16, 13312, 1024)
        v_blk = F(0, 128, 14336, 128)
        CH = 16384
        v_sp = [F(0, 128, CH, T2), F(0, 128, CH + T2, T2)]
        v_tsig = F(0, 128, CH + 2 * T2, T2)
        v_sgn = F(0, 128, CH + 3 * T2, T2)
        v_R = [F(0, 128, CH + 4 * T2, T2), F(0, 128, CH + 5 * T2, T2)]
        v_evr = F(0, 128, CH + 6 * T2, T2)
        v_g1 = F(0, 128, CH + 7 * T2, T2)
        v_cc = F(0, 128, CH + 8 * T2, T2)
        v_ff = F(0, 128, CH + 9 * T2, T2)
        v_carry = F(0, 128, 36864, 8)
        PC = 36880
        v_wcols = F(0, 128, PC, 512)
        v_w05 = F(0, 128, PC + 512, 512)
        v_cexp = F(0, 128, PC + 1024, 16)
        v_tsc = F(0, 128, PC + 1040, 16)
        v_tsb = F(0, 128, PC + 1056, 16)
        v_b1c = F(0, 128, PC + 1072, 32)
        v_b2hc = F(0, 128, PC + 1104, 8)
        v_onesdiv = F(0, 128, PC + 1112, 1)
        v_bc1 = F(0, 1, PC + 1120, 128)
        v_bc1n = F(0, 1, PC + 1248, 128)
        v_vpreK = F(0, 128, PC + 1376, 8)
        v_sb16 = F(0, 16, PC + 1384, 1)
        v_wtmp = F(0, 128, PC + 1392, 16)
        v_row1 = F(0, 1, PC + 1408, 16)
        v_row2 = F(0, 1, PC + 1424, 16)
        v_row3 = F(0, 1, PC + 1440, 16)
        st_x = F(0, 128, 0, 4096)
        st_x2 = F(0, 128, 4096, 4096)
        st_u = F(0, 128, 24576, 512)
        st_u2 = F(0, 128, 25088, 512)
        st_p1g = F(0, 128, 25600, 512)
        st_p2 = F(0, 128, 26112, 512)
        st_t1 = F(0, 128, 26624, 512)
        st_t2 = F(0, 128, 27136, 512)
        r_mrow = F(0, 1, 27648, 512)
        r_vrow = F(0, 1, 28160, 512)
        r_msq = F(0, 1, 28672, 512)
        r_varr = F(0, 1, 29184, 512)
        r_lnv = F(0, 1, 29696, 512)
        r_rrow = F(0, 1, 30208, 512)
        r_nmrow = F(0, 1, 30720, 512)
        k_m16 = F(0, 16, 31232, 512)
        k_v16 = F(0, 16, 31744, 512)
        k_msq = F(0, 16, 32256, 512)
        k_var = F(0, 16, 32768, 512)
        k_lnv = F(0, 16, 33280, 512)
        k_r16 = F(0, 16, 33792, 512)
        k_nm16 = F(0, 16, 34304, 512)
        st_kraw = F(0, 128, CH, 4096)
        st_sq = F(0, 128, CH + 4096, 4096)
        st_wf = F(0, 128, CH + 8192, 3072)

        b_m = Bv(0, 128, 0, T2)
        b_sel = Bv(0, 128, 2048, 2048)
        b_wq = Bv(0, 64, 4096, 1024)
        b_qbf = Bv(0, 64, 5120, 4096)
        b_ybf = Bv(0, 64, 9216, 4096)
        b_xmov = Bv(0, 128, 13312, 4096)
        b_work2 = Bv(0, 128, 17408, 2048)
        b_slab = Bv(0, 128, 19456, 8192)
        b_wcast = Bv(0, 128, 13312, 3072)

        ps_bank = [P(0, 128, 512 * i, 512) for i in range(8)]
        y_ps = [P(0, 64, 0, 2048), P(0, 64, 2048, 2048)]

        op = bd.op

        def dma(dst_ap, src_ap, reads, writes, eng="sp"):
            op(eng, reads, writes, lambda e, d=dst_ap, s=src_ap: e.dma_start(out=d, in_=s))

        rb = {
            id(xT): L * 4, id(outT): L * 4, id(x1s): L * 4, id(x2s): L * 4,
            id(qs): L * 4, id(ks): L * 4, id(vs): L * 4, id(ss): L * 4,
            id(ys): L * 2, id(xlns): L * 2, id(p1s): L * 2, id(q2s): L * 2, id(hs): L * 2,
        }

        def drg(h, r0, r1):
            return dreg(h, r0, r1, rb[id(h)])

        def wreg(h):
            return (id(h), 0, 1, 0, 1 << 40)

        v_eps16 = F(0, 16, PC + 1600, 1)
        op("dve", [], [v_eps16],
           lambda e: e.memset(v_eps16.ap(), EPS))
        dma(v_onesdiv.ap(), onesdiv_in[:, :], [wreg(onesdiv_in)], [v_onesdiv])
        dma(v_bc1.ap(), bc1_in[:, :], [wreg(bc1_in)], [v_bc1])
        dma(v_bc1n.ap(), bc1n_in[:, :], [wreg(bc1n_in)], [v_bc1n])
        dma(v_b1c.ap(), b1c_in[:, :], [wreg(b1c_in)], [v_b1c])
        dma(v_wcols.ap(), wcols_in[:, :], [wreg(wcols_in)], [v_wcols])
        op("dve", [v_wcols], [v_w05],
           lambda e: e.tensor_scalar(out=v_w05.ap(), in0=v_wcols.ap(), scalar1=0.5,
                                     scalar2=None, op0=ALU.mult))
        tmpb2 = F(0, 128, PC + 1456, 8)
        dma(tmpb2.ap(), b2c_in[:, :], [wreg(b2c_in)], [tmpb2])
        op("dve", [tmpb2], [v_b2hc],
           lambda e: e.tensor_scalar(out=v_b2hc.ap(), in0=tmpb2.ap(), scalar1=0.5,
                                     scalar2=None, op0=ALU.mult))
        src = bass.AP(bc16_in, 0, [[128, 16], [2048, 8], [1, 128]])
        dma(v_bc16.ap(), src, [wreg(bc16_in)], [v_bc16])
        src = bass.AP(bc16n_in, 0, [[128, 16], [2048, 8], [1, 128]])
        dma(v_bc16n.ap(), src, [wreg(bc16n_in)], [v_bc16n])
        src = bass.AP(blk_in, 0, [[16, 128], [2048, 8], [1, 16]])
        dma(v_blk.ap(), src, [wreg(blk_in)], [v_blk])
        src = bass.AP(sel_in, 0, [[64, 128], [8192, 32], [1, 64]])
        dma(b_sel.ap(), src, [wreg(sel_in)], [b_sel])
        wq_f = F(0, 64, CH, 1024)
        src = bass.AP(wltmT, 0, [[64, 64], [4096, 16], [1, 64]])
        dma(wq_f.ap(), src, [wreg(wltmT)], [wq_f])
        op("dve", [wq_f], [b_wq],
           lambda e: e.tensor_copy(out=b_wq.ap(), in_=wq_f.ap()))
        rbt = F(0, 1, PC + 1472, 16); rcc = F(0, 1, PC + 1488, 16)
        rgm = F(0, 1, PC + 1504, 16); rbg = F(0, 1, PC + 1520, 16)
        ral = F(0, 1, PC + 1536, 1); ric = F(0, 1, PC + 1537, 1)
        r16o = F(0, 1, PC + 1538, 16)
        rt1 = F(0, 1, PC + 1554, 16); rt2 = F(0, 1, PC + 1570, 16)
        dma(rbt.ap(), bt_in[:, :], [wreg(bt_in)], [rbt])
        dma(rcc.ap(), cc_in[:, :], [wreg(cc_in)], [rcc])
        dma(rgm.ap(), gm_in[:, :], [wreg(gm_in)], [rgm])
        dma(rbg.ap(), bg_in[:, :], [wreg(bg_in)], [rbg])
        dma(ral.ap(), al_in[:, :], [wreg(al_in)], [ral])
        dma(ric.ap(), ic_in[:, :], [wreg(ic_in)], [ric])
        dma(r16o.ap(), ones16_in[:, :], [wreg(ones16_in)], [r16o])
        op("dve", [rcc], [rt1], lambda e: e.reciprocal(out=rt1.sub(0, 1, 0, 16).ap(), in_=rcc.ap()))
        op("dve", [rbt, rt1], [rt2],
           lambda e: e.tensor_tensor(out=rt2.ap(), in0=rbt.ap(), in1=rt1.ap(), op=ALU.mult))
        op("dve", [rt2], [rt2],
           lambda e: e.tensor_scalar(out=rt2.ap(), in0=rt2.ap(), scalar1=-1.0,
                                     scalar2=None, op0=ALU.mult))
        dma(cbounce[0:1, :], rt2.ap(), [rt2], [wreg(cbounce)])
        op("dve", [rbt, ral], [rt1],
           lambda e: e.tensor_scalar(out=rt1.ap(), in0=rbt.ap(), scalar1=ral.ap(),
                                     scalar2=None, op0=ALU.mult))
        dma(cbounce[1:2, :], rt1.ap(), [rt1, wreg(cbounce)], [wreg(cbounce)])
        op("dve", [r16o, ral, ric], [rt1],
           lambda e: e.tensor_scalar(out=rt1.ap(), in0=r16o.ap(), scalar1=ral.ap(),
                                     scalar2=ric.ap(), op0=ALU.mult, op1=ALU.mult))
        op("dve", [rt1], [rt1],
           lambda e: e.tensor_scalar(out=rt1.ap(), in0=rt1.ap(), scalar1=-1.0,
                                     scalar2=None, op0=ALU.mult))
        dma(cbounce[2:3, :], rt1.ap(), [rt1, wreg(cbounce)], [wreg(cbounce)])
        op("dve", [rgm, rbg], [rt1],
           lambda e: e.tensor_tensor(out=rt1.ap(), in0=rgm.ap(), in1=rbg.ap(), op=ALU.mult))
        op("dve", [rt1], [rt1],
           lambda e: e.tensor_scalar(out=rt1.ap(), in0=rt1.ap(), scalar1=0.5,
                                     scalar2=None, op0=ALU.mult))
        dma(cbounce[3:4, :], rt1.ap(), [rt1, wreg(cbounce)], [wreg(cbounce)])
        dma(v_cexp.ap(), cbounce[0:1, :].partition_broadcast(128), [wreg(cbounce)], [v_cexp])
        dma(v_tsc.ap(), cbounce[1:2, :].partition_broadcast(128), [wreg(cbounce)], [v_tsc])
        dma(v_tsb.ap(), cbounce[2:3, :].partition_broadcast(128), [wreg(cbounce)], [v_tsb])
        for dd in range(2):
            dst_ap = v_vpreK.sub(64 * dd, 64, 0, 8).ap()
            src_ap = bass.AP(cbounce, 3 * 16 + dd, [[0, 64], [2, 8]])
            op("sp", [wreg(cbounce)], [v_vpreK],
               lambda e, d=dst_ap, s=src_ap: e.dma_start(
                   out=d, in_=s, allow_slow_non_contiguous=True))
        sb_a = F(0, 16, PC + 1586, 1); sb_b = F(0, 16, PC + 1587, 1)
        dma(sb_a.ap(), bass.AP(vgsb_in, 0, [[1, 16], [1, 1]]), [wreg(vgsb_in)], [sb_a])
        dma(sb_b.ap(), bass.AP(vt0_in, 0, [[1, 16], [1, 1]]), [wreg(vt0_in)], [sb_b])
        op("dve", [sb_a, sb_b], [v_sb16],
           lambda e: e.tensor_tensor(out=v_sb16.ap(), in0=sb_a.ap(), in1=sb_b.ap(),
                                     op=ALU.subtract))

        def wcast(src_h, dst_h, rows, cols, scale=None):
            for r0 in range(0, rows, 128):
                for c0 in range(0, cols, 3072):
                    cw = min(3072, cols - c0)
                    fsrc = st_wf.sub(0, 128, 0, cw)
                    bdst = b_wcast.sub(0, 128, 0, cw)
                    dma(fsrc.ap(), src_h[r0:r0 + 128, c0:c0 + cw], [wreg(src_h)], [fsrc])
                    if scale is None:
                        op("dve", [fsrc], [bdst],
                           lambda e, a=bdst.ap(), b_=fsrc.ap(): e.tensor_copy(out=a, in_=b_))
                    else:
                        op("dve", [fsrc], [bdst],
                           lambda e, a=bdst.ap(), b_=fsrc.ap(), s=scale: e.tensor_scalar(
                               out=a, in0=b_, scalar1=s, scalar2=None, op0=ALU.mult))
                    dma(dst_h[r0:r0 + 128, c0:c0 + cw], bdst.ap(), [bdst], [wreg(dst_h)])

        wcast(wqkvT, wb_qkv, D, 3 * D)
        wcast(woT, wb_wo, D, D)
        wcast(u1, wb_u1, D, RK)
        wcast(v1, wb_v1, RK, DFF)
        wcast(u2, wb_u2h, DFF, RK, scale=0.25)
        wcast(v2, wb_v2, RK, D)
        wcast(wvgsT, wb_vgs, D, Hn)

        def load_slab(src_h, col0, nk, nm, src_cols):
            for kc in range(nk):
                dstv = b_slab.sub(0, 128, kc * nm * 128, nm * 128)
                dma(dstv.ap(), src_h[kc * 128:(kc + 1) * 128, col0:col0 + nm * 128],
                    [wreg(src_h)], [dstv])

        def slab_tile(kc, ml, nm):
            return b_slab.sub(0, 128, (kc * nm + ml) * 128, 128)

        def mm(psv, lhv, rhv, start, stop, extra_reads=()):
            op("pe", [lhv, rhv] + list(extra_reads), [psv],
               lambda e, o=psv.ap(), l=lhv.ap(), r=rhv.ap(), st=start, sp=stop:
               e.matmul(out=o, lhsT=l, rhs=r, start=st, stop=sp))

        def act(outv, inv, func, bias=0.0, scale=1.0, extra_reads=()):
            br = [bias] if isinstance(bias, View) else []
            sr = [scale] if isinstance(scale, View) else []
            bap = bias.ap() if isinstance(bias, View) else bias
            sap = scale.ap() if isinstance(scale, View) else scale
            if func == AF.Copy and br:
                assert not sr and scale == 1.0
                op("act", [inv] + br + list(extra_reads), [outv],
                   lambda e, o=outv.ap(), i=inv.ap(), b_=bap:
                   e.add(out=o, in_=i, add=b_))
            else:
                op("act", [inv] + br + sr + list(extra_reads), [outv],
                   lambda e, o=outv.ap(), i=inv.ap(), f=func, b_=bap, s=sap:
                   e.activation(out=o, in_=i, func=f, bias=b_, scale=s))

        def tt(outv, av, bv, alu):
            op("dve", [av, bv], [outv],
               lambda e, o=outv.ap(), a=av.ap(), b_=bv.ap(), u=alu:
               e.tensor_tensor(out=o, in0=a, in1=b_, op=u))

        def ts(outv, av, s1, s2, op0, op1=None):
            s1r = [s1] if isinstance(s1, View) else []
            s1ap = s1.ap() if isinstance(s1, View) else s1
            if op1 is None:
                op("dve", [av] + s1r, [outv],
                   lambda e, o=outv.ap(), a=av.ap(), s=s1ap, u=op0:
                   e.tensor_scalar(out=o, in0=a, scalar1=s, scalar2=None, op0=u))
            else:
                op("dve", [av] + s1r, [outv],
                   lambda e, o=outv.ap(), a=av.ap(), s=s1ap, s2_=s2, u0=op0, u1=op1:
                   e.tensor_scalar(out=o, in0=a, scalar1=s, scalar2=s2_, op0=u0, op1=u1))

        def stt(outv, av, sc, bv, op0, op1):
            op("dve", [av, bv], [outv],
               lambda e, o=outv.ap(), a=av.ap(), s=sc, b_=bv.ap(), u0=op0, u1=op1:
               e.scalar_tensor_tensor(out=o, in0=a, scalar=s, in1=b_, op0=u0, op1=u1))

        def layernorm(xtiles, sqv, xlnouts):
            whole = xtiles[0].sub(0, 128, 0, 4096)
            act(sqv, whole, AF.Square)
            for kc in range(8):
                mm(ps_bank[6].sub(0, 1, 0, 512), v_onesdiv, xtiles[kc], kc == 0, kc == 7)
            for kc in range(8):
                mm(ps_bank[7].sub(0, 1, 0, 512), v_onesdiv, sqv.sub(0, 128, kc * 512, 512),
                   kc == 0, kc == 7)
            act(r_mrow, ps_bank[6].sub(0, 1, 0, 512), AF.Copy)
            act(r_vrow, ps_bank[7].sub(0, 1, 0, 512), AF.Copy)
            act(r_msq, r_mrow, AF.Square)
            tt(r_varr, r_vrow, r_msq, ALU.subtract)
            act(r_lnv, r_varr, AF.Ln, bias=v_eps16.sub(0, 1, 0, 1))
            act(r_rrow, r_lnv, AF.Exp, scale=-0.5)
            tt(r_nmrow, r_mrow, r_rrow, ALU.mult)
            mm(ps_bank[4], v_bc1, r_rrow, True, True)
            mm(ps_bank[5], v_bc1n, r_nmrow, True, True)
            for kc in range(8):
                tt(st_t1, xtiles[kc], ps_bank[4], ALU.mult)
                tt(xlnouts[kc], st_t1, ps_bank[5], ALU.add)

        def ph_u1(with_ln):
            load_slab(wb_u1, 0, 8, 2, RK)
            for b in range(BLOC):
                for tt_i in range(L // TOKT):
                    t0 = tt_i * TOKT
                    xm = [b_xmov.sub(0, 128, kc * 512, 512) for kc in range(8)]
                    if with_ln:
                        xts = [st_x.sub(0, 128, kc * 512, 512) for kc in range(8)]
                        for kc in range(8):
                            dma(xts[kc].ap(), xT[b * D + kc * 128:b * D + (kc + 1) * 128,
                                                 t0:t0 + TOKT],
                                [wreg(xT)], [xts[kc]])
                        layernorm(xts, st_sq, xm)
                    else:
                        for kc in range(8):
                            dma(xm[kc].ap(), xlns[b * D + kc * 128:b * D + (kc + 1) * 128,
                                                  t0:t0 + TOKT],
                                [drg(xlns, b * D + kc * 128, b * D + (kc + 1) * 128)], [xm[kc]])
                    for m2 in range(2):
                        psv = ps_bank[m2]
                        for kc in range(8):
                            mm(psv, slab_tile(kc, m2, 2), xm[kc], kc == 0, kc == 7)
                        ov = b_work2.sub(0, 128, m2 * 512, 512)
                        op("dve", [psv], [ov],
                           lambda e, o=ov.ap(), i=psv.ap(): e.tensor_copy(out=o, in_=i))
                        dma(p1s[b * RK + m2 * 128:b * RK + (m2 + 1) * 128, t0:t0 + TOKT],
                            ov.ap(), [ov], [drg(p1s, b * RK + m2 * 128, b * RK + (m2 + 1) * 128)])

        def ph_v1gelu():
            load_slab(wb_v1, 0, 2, 32, DFF)
            for b in range(BLOC):
                for tt_i in range(L // TOKT):
                    t0 = tt_i * TOKT
                    pm = [b_xmov.sub(0, 128, r2 * 512, 512) for r2 in range(2)]
                    for r2 in range(2):
                        dma(pm[r2].ap(), p1s[b * RK + r2 * 128:b * RK + (r2 + 1) * 128,
                                             t0:t0 + TOKT],
                            [drg(p1s, b * RK + r2 * 128, b * RK + (r2 + 1) * 128)], [pm[r2]])
                    for fc in range(32):
                        psv = ps_bank[fc % 2]
                        for r2 in range(2):
                            mm(psv, slab_tile(r2, fc, 32), pm[r2], r2 == 0, r2 == 1)
                        act(st_u, psv, AF.Copy, bias=v_b1c.sub(0, 128, fc, 1))
                        act(st_u2, st_u, AF.Square)
                        ts(st_p1g, st_u2, GELU_A, 1.0, ALU.mult, ALU.add)
                        tt(st_p2, st_p1g, st_u, ALU.mult)
                        act(st_u2, st_p2, AF.Tanh, scale=GELU_C)
                        hv = b_work2.sub(0, 128, (fc % 4) * 512, 512)
                        stt(hv, st_u2, 1.0, st_u, ALU.add, ALU.mult)
                        dma(hs[b * DFF + fc * 128:b * DFF + (fc + 1) * 128, t0:t0 + TOKT],
                            hv.ap(), [hv], [drg(hs, b * DFF + fc * 128, b * DFF + (fc + 1) * 128)])

        def ph_u2():
            for kc in range(32):
                dstv = b_slab.sub(0, 128, kc * 256, 256)
                dma(dstv.ap(), wb_u2h[kc * 128:(kc + 1) * 128, :], [wreg(wb_u2h)], [dstv])
            for b in range(BLOC):
                for tt_i in range(L // TOKT):
                    t0 = tt_i * TOKT
                    for fg in range(4):
                        hm = [b_xmov.sub(0, 128, i * 512, 512) for i in range(8)]
                        for i in range(8):
                            fc = fg * 8 + i
                            dma(hm[i].ap(), hs[b * DFF + fc * 128:b * DFF + (fc + 1) * 128,
                                               t0:t0 + TOKT],
                                [drg(hs, b * DFF + fc * 128, b * DFF + (fc + 1) * 128)], [hm[i]])
                        for i in range(8):
                            fc = fg * 8 + i
                            for r2 in range(2):
                                mm(ps_bank[2 + r2], b_slab.sub(0, 128, fc * 256 + r2 * 128, 128),
                                   hm[i], fc == 0, fc == 31)
                    for r2 in range(2):
                        ov = b_work2.sub(0, 128, r2 * 512, 512)
                        op("dve", [ps_bank[2 + r2]], [ov],
                           lambda e, o=ov.ap(), i=ps_bank[2 + r2].ap(): e.tensor_copy(out=o, in_=i))
                        dma(q2s[b * RK + r2 * 128:b * RK + (r2 + 1) * 128, t0:t0 + TOKT],
                            ov.ap(), [ov], [drg(q2s, b * RK + r2 * 128, b * RK + (r2 + 1) * 128)])

        def ph_v2res(res_h, out_h, with_ln):
            load_slab(wb_v2, 0, 2, 8, D)
            for b in range(BLOC):
                for tt_i in range(L // TOKT):
                    t0 = tt_i * TOKT
                    qm = [b_xmov.sub(0, 128, r2 * 512, 512) for r2 in range(2)]
                    for r2 in range(2):
                        dma(qm[r2].ap(), q2s[b * RK + r2 * 128:b * RK + (r2 + 1) * 128,
                                             t0:t0 + TOKT],
                            [drg(q2s, b * RK + r2 * 128, b * RK + (r2 + 1) * 128)], [qm[r2]])
                    outs = [st_x.sub(0, 128, mc * 512, 512) for mc in range(8)]
                    for mc in range(8):
                        psv = ps_bank[mc % 2]
                        for r2 in range(2):
                            mm(psv, slab_tile(r2, mc, 8), qm[r2], r2 == 0, r2 == 1)
                        act(st_u, psv, AF.Copy, bias=v_b2hc.sub(0, 128, mc, 1))
                        rv = st_u2
                        dma(rv.ap(), res_h[b * D + mc * 128:b * D + (mc + 1) * 128, t0:t0 + TOKT],
                            [wreg(res_h) if id(res_h) not in rb else
                             drg(res_h, b * D + mc * 128, b * D + (mc + 1) * 128)], [rv])
                        tt(outs[mc], st_u, rv, ALU.add)
                        dma(out_h[b * D + mc * 128:b * D + (mc + 1) * 128, t0:t0 + TOKT],
                            outs[mc].ap(), [outs[mc]],
                            [drg(out_h, b * D + mc * 128, b * D + (mc + 1) * 128)
                             if id(out_h) in rb else wreg(out_h)])
                    if with_ln:
                        xm = [b_xmov.sub(0, 128, 2048 + kc * 512, 512) for kc in range(4)] + \
                             [b_work2.sub(0, 128, kc * 512, 512) for kc in range(4)]
                        layernorm(outs, st_sq, xm)
                        for kc in range(8):
                            dma(xlns[b * D + kc * 128:b * D + (kc + 1) * 128, t0:t0 + TOKT],
                                xm[kc].ap(), [xm[kc]],
                                [drg(xlns, b * D + kc * 128, b * D + (kc + 1) * 128)])

        def ph_qkv(mg):
            load_slab(wb_qkv, mg * 1024, 8, 8, 3 * D)
            if mg == 2:
                for kc in range(8):
                    dstv = b_work2.sub(0, 128, kc * 16, 16)
                    dma(dstv.ap(), wb_vgs[kc * 128:(kc + 1) * 128, :], [wreg(wb_vgs)], [dstv])
            for b in range(BLOC):
                for tt_i in range(L // TOKT):
                    t0 = tt_i * TOKT
                    xm = [b_xmov.sub(0, 128, kc * 512, 512) for kc in range(8)]
                    for kc in range(8):
                        dma(xm[kc].ap(), xlns[b * D + kc * 128:b * D + (kc + 1) * 128,
                                              t0:t0 + TOKT],
                            [drg(xlns, b * D + kc * 128, b * D + (kc + 1) * 128)], [xm[kc]])
                    if mg == 0:
                        for mc in range(8):
                            psv = ps_bank[mc % 2]
                            for kc in range(8):
                                mm(psv, slab_tile(kc, mc, 8), xm[kc], kc == 0, kc == 7)
                            act(st_u, psv, AF.Copy)
                            dma(qs[b * D + mc * 128:b * D + (mc + 1) * 128, t0:t0 + TOKT],
                                st_u.ap(), [st_u],
                                [drg(qs, b * D + mc * 128, b * D + (mc + 1) * 128)])
                    elif mg == 1:
                        kraw = [st_kraw.sub(0, 128, mc * 512, 512) for mc in range(8)]
                        for mc in range(8):
                            psv = ps_bank[mc % 2]
                            for kc in range(8):
                                mm(psv, slab_tile(kc, mc, 8), xm[kc], kc == 0, kc == 7)
                            act(kraw[mc], psv, AF.Copy)
                        act(st_sq, st_kraw, AF.Square)
                        for jk in range(8):
                            mm(ps_bank[6].sub(0, 16, 0, 512),
                               v_blk.sub(0, 128, jk * 16, 16), kraw[jk], jk == 0, jk == 7)
                        for jk in range(8):
                            mm(ps_bank[7].sub(0, 16, 0, 512),
                               v_blk.sub(0, 128, jk * 16, 16),
                               st_sq.sub(0, 128, jk * 512, 512), jk == 0, jk == 7)
                        act(k_m16, ps_bank[6].sub(0, 16, 0, 512), AF.Copy)
                        act(k_v16, ps_bank[7].sub(0, 16, 0, 512), AF.Copy)
                        act(k_msq, k_m16, AF.Square)
                        tt(k_var, k_v16, k_msq, ALU.subtract)
                        act(k_lnv, k_var, AF.Ln, bias=v_eps16)
                        act(k_r16, k_lnv, AF.Exp, scale=-0.5)
                        tt(k_nm16, k_m16, k_r16, ALU.mult)
                        for jk in range(8):
                            pr = ps_bank[4 if jk % 2 == 0 else 6]
                            pn = ps_bank[5 if jk % 2 == 0 else 7]
                            mm(pr, v_bc16.sub(0, 16, jk * 128, 128), k_r16, True, True)
                            mm(pn, v_bc16n.sub(0, 16, jk * 128, 128), k_nm16, True, True)
                            klv = st_x.sub(0, 128, 0, 512)
                            tt(st_t1, kraw[jk], pr, ALU.mult)
                            tt(klv, st_t1, pn, ALU.add)
                            dma(ks[b * D + jk * 128:b * D + (jk + 1) * 128, t0:t0 + TOKT],
                                klv.ap(), [klv],
                                [drg(ks, b * D + jk * 128, b * D + (jk + 1) * 128)])
                    else:
                        for mc in range(8):
                            psv = ps_bank[mc % 2]
                            for kc in range(8):
                                mm(psv, slab_tile(kc, mc, 8), xm[kc], kc == 0, kc == 7)
                            ts(st_u, psv, v_vpreK.sub(0, 128, mc, 1), None, ALU.mult)
                            dma(vs[b * D + mc * 128:b * D + (mc + 1) * 128, t0:t0 + TOKT],
                                st_u.ap(), [st_u],
                                [drg(vs, b * D + mc * 128, b * D + (mc + 1) * 128)])
                        for kc in range(8):
                            mm(ps_bank[6].sub(0, 16, 0, 512),
                               b_work2.sub(0, 128, kc * 16, 16), xm[kc], kc == 0, kc == 7)
                        sv = st_x.sub(0, 16, 512, 512)
                        act(sv, ps_bank[6].sub(0, 16, 0, 512), AF.Copy, bias=v_sb16)
                        dma(ss[b * Hn:(b + 1) * Hn, t0:t0 + TOKT], sv.ap(), [sv],
                            [drg(ss, b * Hn, (b + 1) * Hn)])

        def ph_attn():
            for bh in range(BLOC * Hn):
                b, h = bh // Hn, bh % Hn
                r0 = b * D + h * DK
                dma(v_krep.sub(0, 64, 0, 4096).ap(), ks[r0:r0 + 64, :],
                    [drg(ks, r0, r0 + 64)], [v_krep])
                dma(v_krep.sub(64, 64, 0, 4096).ap(), ks[r0:r0 + 64, :],
                    [drg(ks, r0, r0 + 64)], [v_krep])
                dma(v_qrep.sub(0, 64, 0, 4096).ap(), qs[r0:r0 + 64, :],
                    [drg(qs, r0, r0 + 64)], [v_qrep])
                dma(v_qrep.sub(64, 64, 0, 4096).ap(), qs[r0:r0 + 64, :],
                    [drg(qs, r0, r0 + 64)], [v_qrep])
                dma(v_sbc.ap(), ss[b * Hn + h:b * Hn + h + 1, :].partition_broadcast(128),
                    [drg(ss, b * Hn + h, b * Hn + h + 1)], [v_sbc])
                op("dve", [v_qrep], [b_qbf],
                   lambda e, o=b_qbf.ap(), i=v_qrep.sub(0, 64, 0, 4096).ap():
                   e.tensor_copy(out=o, in_=i))
                cexp_h = v_cexp.sub(0, 128, h, 1)
                tsc_h = v_tsc.sub(0, 128, h, 1)
                tsb_h = v_tsb.sub(0, 128, h, 1)
                wq_h = b_wq.sub(0, 64, h * 64, 64)
                it = 0
                for c in range(32):
                    wcol = v_wcols.sub(0, 128, h * 32 + c, 1)
                    w05 = v_w05.sub(0, 128, h * 32 + c, 1)
                    for th in range(2):
                        par = it % 2
                        sbc_t = v_sbc.sub(0, 128, th * T2, T2)
                        spv = v_sp[par]
                        Rv = v_R[par]
                        act(v_evr, sbc_t, AF.Exp, bias=wcol)
                        act(spv, v_evr, AF.Ln, bias=1.0)
                        act(v_tsig, sbc_t, AF.Tanh, bias=w05, scale=0.5)
                        act(v_sgn, spv, AF.Tanh, bias=tsb_h, scale=tsc_h)
                        act(Rv, spv, AF.Exp, scale=cexp_h)
                        for dd in range(2):
                            vrow = r0 + 2 * c + dd
                            dma(v_evr.sub(64 * dd, 64, 0, T2).ap(),
                                vs[vrow:vrow + 1,
                                   th * T2:th * T2 + T2].partition_broadcast(64),
                                [drg(vs, vrow, vrow + 1)], [v_evr])
                        stt(v_g1, v_tsig, 1.0, v_sgn, ALU.add, ALU.mult)
                        tt(v_cc, v_g1, spv, ALU.mult)
                        krep_t = v_krep.sub(0, 128, th * T2, T2)
                        qrep_t = v_qrep.sub(0, 128, th * T2, T2)
                        tt(v_g1, krep_t, v_evr, ALU.mult)
                        tt(v_tsig, v_cc, v_g1, ALU.mult)
                        if th == 0:
                            op("dve", [Rv, v_tsig], [v_ff],
                               lambda e, o=v_ff.ap(), r=Rv.ap(), z=v_tsig.ap():
                               e.tensor_tensor_scan(out=o, data0=r, data1=z, initial=0.0,
                                                    op0=ALU.mult, op1=ALU.add))
                            op("dve", [v_ff], [v_carry],
                               lambda e, o=v_carry.sub(0, 128, 0, 1).ap(),
                               i=v_ff.sub(0, 128, T2 - 1, 1).ap(): e.tensor_copy(out=o, in_=i))
                        else:
                            op("dve", [Rv, v_tsig, v_carry], [v_ff],
                               lambda e, o=v_ff.ap(), r=Rv.ap(), z=v_tsig.ap(),
                               ini=v_carry.sub(0, 128, 0, 1).ap():
                               e.tensor_tensor_scan(out=o, data0=r, data1=z, initial=ini,
                                                    op0=ALU.mult, op1=ALU.add))
                        tt(b_m, v_ff, qrep_t, ALU.mult)
                        if c == 0:
                            for s4 in range(4):
                                mm(y_ps[th].sub(0, 64, s4 * 512, 512), wq_h,
                                   b_qbf.sub(0, 64, th * T2 + s4 * 512, 512), True, False)
                        sel_c = b_sel.sub(0, 128, c * 64, 64)
                        for s4 in range(4):
                            mm(y_ps[th].sub(0, 64, s4 * 512, 512), sel_c,
                               b_m.sub(0, 128, s4 * 512, 512), False, c == 31)
                        if c == 31:
                            ybv = b_ybf.sub(0, 64, th * T2, T2)
                            act(ybv, y_ps[th], AF.Copy)
                        it += 1
                dma(ys[r0:r0 + 64, :], b_ybf.ap(), [b_ybf], [drg(ys, r0, r0 + 64)])

        def ph_wo():
            for kc in range(8):
                dstv = b_slab.sub(0, 128, kc * 1024, 1024)
                dma(dstv.ap(), wb_wo[kc * 128:(kc + 1) * 128, :], [wreg(wb_wo)], [dstv])
            for b in range(BLOC):
                for tt_i in range(L // TOKT):
                    t0 = tt_i * TOKT
                    ym = [b_xmov.sub(0, 128, kc * 512, 512) for kc in range(8)]
                    for kc in range(8):
                        dma(ym[kc].ap(), ys[b * D + kc * 128:b * D + (kc + 1) * 128, t0:t0 + TOKT],
                            [drg(ys, b * D + kc * 128, b * D + (kc + 1) * 128)], [ym[kc]])
                    outs = [st_x2.sub(0, 128, mc * 512, 512) for mc in range(8)]
                    for mc in range(8):
                        psv = ps_bank[mc % 2]
                        for kc in range(8):
                            mm(psv, b_slab.sub(0, 128, kc * 1024 + mc * 128, 128), ym[kc],
                               kc == 0, kc == 7)
                        rv = st_u
                        dma(rv.ap(), x1s[b * D + mc * 128:b * D + (mc + 1) * 128, t0:t0 + TOKT],
                            [drg(x1s, b * D + mc * 128, b * D + (mc + 1) * 128)], [rv])
                        tt(outs[mc], rv, psv, ALU.add)
                        dma(x2s[b * D + mc * 128:b * D + (mc + 1) * 128, t0:t0 + TOKT],
                            outs[mc].ap(), [outs[mc]],
                            [drg(x2s, b * D + mc * 128, b * D + (mc + 1) * 128)])
                    xm = [b_xmov.sub(0, 128, 4096 - 4096 + kc * 512, 512) for kc in range(8)]
                    layernorm(outs, st_sq, xm)
                    for kc in range(8):
                        dma(xlns[b * D + kc * 128:b * D + (kc + 1) * 128, t0:t0 + TOKT],
                            xm[kc].ap(), [xm[kc]],
                            [drg(xlns, b * D + kc * 128, b * D + (kc + 1) * 128)])

        phset = set(phases)
        if 1 in phset: ph_u1(with_ln=True)
        if 2 in phset: ph_v1gelu()
        if 3 in phset: ph_u2()
        if 4 in phset: ph_v2res(xT, x1s, True)
        if 5 in phset: ph_qkv(0)
        if 6 in phset: ph_qkv(1)
        if 7 in phset: ph_qkv(2)
        if 8 in phset: ph_attn()
        if 9 in phset: ph_wo()
        if 10 in phset: ph_u1(with_ln=False)
        if 11 in phset: ph_v1gelu()
        if 12 in phset: ph_u2()
        if 13 in phset: ph_v2res(x2s, outT, False)

        bd.emit(nc, block, sems)

    return nc


_host_cache = {}


def kernel(**inputs):
    x = np.asarray(inputs["L0_attn_x"], np.float32)
    W_qkv = np.asarray(inputs["L0_attn_W_qkv"], np.float32)
    W_o = np.asarray(inputs["L0_attn_W_o"], np.float32)
    W_vgs = np.asarray(inputs["L0_attn_W_vgs"], np.float32)
    W_LTM = np.asarray(inputs["L0_attn_W_LTM"], np.float32)
    V_gs_base = np.asarray(inputs["L0_attn_V_gs_base"], np.float32)
    V_T0 = np.asarray(inputs["L0_attn_V_T0"], np.float32)
    beta_tau = np.asarray(inputs["L0_attn_beta_tau"], np.float32)
    beta_gm = np.asarray(inputs["L0_attn_beta_gm"], np.float32)
    C_ch = np.asarray(inputs["L0_attn_C_ch"], np.float32)
    gamma = np.asarray(inputs["L0_attn_gamma"], np.float32)
    alpha = np.asarray(inputs["L0_attn_alpha_ppd"], np.float32)
    ic = np.asarray(inputs["L0_attn_IC_threshold"], np.float32)
    U1 = np.asarray(inputs["L0_ffn_U1"], np.float32)
    V1 = np.asarray(inputs["L0_ffn_V1"], np.float32)
    b1 = np.asarray(inputs["L0_ffn_b1"], np.float32)
    U2 = np.asarray(inputs["L0_ffn_U2"], np.float32)
    V2 = np.asarray(inputs["L0_ffn_V2"], np.float32)
    b2 = np.asarray(inputs["L0_ffn_b2"], np.float32)

    selc = np.zeros((32, 128, 64), ml_dtypes.bfloat16)
    for c in range(32):
        for p in range(128):
            selc[c, p, 2 * c + p // 64] = 1.0
    blkc = np.zeros((8, 128, 16), np.float32)
    bc16c = np.zeros((8, 16, 128), np.float32)
    for jk in range(8):
        for p in range(128):
            hh = 2 * jk + p // 64
            blkc[jk, p, hh] = 1.0 / 64.0
            bc16c[jk, hh, p] = 1.0
    bc16nc = -bc16c
    common = dict(
        wqkvT=np.ascontiguousarray(W_qkv.T),
        woT=np.ascontiguousarray(W_o.T),
        u1=U1, v1=V1, u2=U2, v2=V2,
        wvgsT=np.ascontiguousarray(W_vgs.T),
        b1c=np.ascontiguousarray(b1.reshape(32, 128).T),
        b2c=np.ascontiguousarray(b2.reshape(8, 128).T),
        wcols=np.ascontiguousarray(
            W_LTM.reshape(Hn, 32, 128).transpose(2, 0, 1).reshape(128, 512)),
        wltmT=np.ascontiguousarray(W_LTM.transpose(0, 2, 1)),
        selc=selc, blkc=blkc, bc16c=bc16c, bc16nc=bc16nc,
        onesdiv=np.full((128, 1), 1.0 / 1024.0, np.float32),
        bc1=np.ones((1, 128), np.float32),
        bc1n=-np.ones((1, 128), np.float32),
        ones16=np.ones((1, 16), np.float32),
        betatau=beta_tau.reshape(1, 16), cch=C_ch.reshape(1, 16),
        gammav=gamma.reshape(1, 16), betagm=beta_gm.reshape(1, 16),
        alphap=alpha.reshape(1, 1), icth=ic.reshape(1, 1),
        vgsbase=V_gs_base, vt0=V_T0,
    )
    in_maps = []
    for ci in range(NC_):
        xs = np.ascontiguousarray(
            x[ci * BLOC:(ci + 1) * BLOC].transpose(0, 2, 1).reshape(BLOC * D, L))
        m = dict(common)
        m["xT"] = xs
        in_maps.append(m)

    nc = build_program()
    res = run_bass_kernel_spmd(nc, in_maps, core_ids=list(range(NC_)))
    outs = []
    for ci in range(NC_):
        o = res.results[ci]["outT"].reshape(BLOC, D, L).transpose(0, 2, 1)
        outs.append(o)
    return np.ascontiguousarray(np.concatenate(outs, axis=0)).astype(np.float32)


# version 04
# speedup vs baseline: 1.0835x; 1.0835x over previous
import sys
sys.path.insert(0, "/opt/trn_rl_repo")
import numpy as np
import ml_dtypes
import concourse.bass as bass
import concourse.mybir as mybir
from concourse.bass_utils import run_bass_kernel_spmd
from contextlib import ExitStack

F32 = mybir.dt.float32
BF16 = mybir.dt.bfloat16
AF = mybir.ActivationFunctionType
ALU = mybir.AluOpType

B, L, D, Hn, DK = 16, 4096, 1024, 16, 64
DFF, RK = 4096, 256
NC_, BLOC = 8, 2
TOKT = 512
NTT = (BLOC * L) // TOKT
T2 = 2048
GELU_C = 0.7978845608028654
GELU_A = 0.044715
EPS = 1e-5

ENGS = ["pe", "act", "dve", "sp", "pool"]
BLOCKNAME = {"pe": "tensor", "act": "scalar", "dve": "vector", "sp": "sync", "pool": "gpsimd"}
INC = {"pe": 1, "act": 1, "dve": 1, "sp": 16, "pool": 16}


class View:
    __slots__ = ("h", "p0", "np_", "c0", "nc_", "esz")

    def __init__(self, h, p0, np_, c0, nc_, esz):
        self.h, self.p0, self.np_, self.c0, self.nc_, self.esz = h, p0, np_, c0, nc_, esz

    def ap(self):
        return self.h[self.p0:self.p0 + self.np_, self.c0:self.c0 + self.nc_]

    def reg(self):
        return (id(self.h), self.p0, self.p0 + self.np_,
                self.c0 * self.esz, (self.c0 + self.nc_) * self.esz)

    def sub(self, dp0, np_, dc0, nc_):
        return View(self.h, self.p0 + dp0, np_, self.c0 + dc0, nc_, self.esz)


def dreg(h, r0, r1, rowbytes):
    return (id(h), 0, 1, r0 * rowbytes, r1 * rowbytes)


class Builder:
    def __init__(self):
        self.ops = []

    def op(self, eng, reads, writes, emit):
        self.ops.append((eng, emit, tuple(r.reg() if isinstance(r, View) else r for r in reads),
                         tuple(w.reg() if isinstance(w, View) else w for w in writes)))

    SERIALIZE = False
    SELF_WAITS = True
    NSLOT = 8

    def emit(self, nc, block, sems):
        n = len(self.ops)
        eng_seq = [None] * n
        seqs = {e: 0 for e in ENGS}
        events = {}
        deps = [dict() for _ in range(n)]

        def inter(a, b):
            return a[1] > b[0] and b[1] > a[0]

        for i, (eng, emit, reads, writes) in enumerate(self.ops):
            myseq = seqs[eng]
            seqs[eng] += 1
            eng_seq[i] = (eng, myseq)
            d = deps[i]

            def dep(ev):
                peng, pseq = ev[4], ev[5]
                if peng == eng and not Builder.SELF_WAITS:
                    return
                if peng == eng and pseq == myseq:
                    return
                if d.get(peng, -1) < pseq:
                    d[peng] = pseq

            for r in reads:
                evs = events.get(r[0])
                if evs:
                    pr = (r[1], r[2]); br = (r[3], r[4])
                    for ev in evs:
                        if ev[6] and inter((ev[0], ev[1]), pr) and inter((ev[2], ev[3]), br):
                            dep(ev)
            for w in writes:
                evs = events.get(w[0])
                if evs:
                    pr = (w[1], w[2]); br = (w[3], w[4])
                    for ev in evs:
                        if inter((ev[0], ev[1]), pr) and inter((ev[2], ev[3]), br):
                            dep(ev)
            for r in reads:
                evs = events.setdefault(r[0], [])
                evs[:] = [ev for ev in evs
                          if ev[6] or ev[4] != eng or
                          (ev[0], ev[1], ev[2], ev[3]) != (r[1], r[2], r[3], r[4])]
                evs.append([r[1], r[2], r[3], r[4], eng, myseq, False, i])
            for w in writes:
                evs = events.setdefault(w[0], [])
                pr = (w[1], w[2]); br = (w[3], w[4])
                evs[:] = [ev for ev in evs
                          if not (ev[0] >= w[1] and ev[1] <= w[2] and ev[2] >= w[3] and ev[3] <= w[4])]
                evs.append([w[1], w[2], w[3], w[4], eng, myseq, True, i])

        need_inc = [True] * n
        idx_of = {}
        for i in range(n):
            idx_of[eng_seq[i]] = i
        for i in range(n):
            for peng, pseq in deps[i].items():
                need_inc[idx_of[(peng, pseq)]] = True
        last_sp = None
        for i in range(n):
            if eng_seq[i][0] == "sp":
                last_sp = i
        if last_sp is not None:
            need_inc[last_sp] = True

        cum = {e: [] for e in ENGS}
        cnt = {e: 0 for e in ENGS}
        for i in range(n):
            eng, myseq = eng_seq[i]
            if need_inc[i]:
                cnt[eng] += 1
            cum[eng].append(cnt[eng])

        waits = [[] for _ in range(n)]
        waited = {e: {p: 0 for p in ENGS} for e in ENGS}
        if self.SERIALIZE:
            done = {e: 0 for e in ENGS}
            for i in range(n):
                eng = eng_seq[i][0]
                for peng in ENGS:
                    if peng == eng:
                        continue
                    val = done[peng] * INC[peng]
                    if val > 0 and waited[eng][peng] < val:
                        waited[eng][peng] = val
                        waits[i].append((peng, val))
                if need_inc[i]:
                    done[eng] += 1
        else:
            for i in range(n):
                eng = eng_seq[i][0]
                for peng, pseq in sorted(deps[i].items()):
                    val = cum[peng][pseq] * INC[peng]
                    assert cum[peng][pseq] > 0
                    if waited[eng][peng] < val:
                        waited[eng][peng] = val
                        waits[i].append((peng, val))

        engine_ops = {e: [] for e in ENGS}
        for i in range(n):
            engine_ops[eng_seq[i][0]].append(i)

        total_inc = {e: cnt[e] * INC[e] for e in ENGS}

        NSLOT = Builder.NSLOT

        def sem_of(eng, seq):
            if eng in ("sp", "pool"):
                return (eng, seq % NSLOT)
            return (eng, 0)

        def val_of(eng, seq):
            if eng in ("sp", "pool"):
                return 16 * (seq // NSLOT + 1)
            return cum[eng][seq] * INC[eng]

        waits = [[] for _ in range(n)]
        waited2 = {}
        if self.SERIALIZE:
            last = {}
            for i in range(n):
                eng, myseq = eng_seq[i]
                for key, val in list(last.items()):
                    if key[0] == eng and key[0] not in ("sp", "pool"):
                        continue
                    if sem_of(*eng_seq[i]) == key:
                        continue
                    wk = (eng, key)
                    if waited2.get(wk, 0) < val:
                        waited2[wk] = val
                        waits[i].append((key, val))
                last[sem_of(eng, myseq)] = val_of(eng, myseq)
        else:
            for i in range(n):
                eng = eng_seq[i][0]
                for peng, pseq in sorted(deps[i].items()):
                    key = sem_of(peng, pseq)
                    val = val_of(peng, pseq)
                    wk = (eng, key)
                    if waited2.get(wk, 0) < val:
                        waited2[wk] = val
                        waits[i].append((key, val))

        gate = [None] * n
        for i in range(n):
            eng, myseq = eng_seq[i]
            if eng in ("sp", "pool") and myseq >= NSLOT:
                gate[i] = (sem_of(eng, myseq), 16 * (myseq // NSLOT))

        for eng in ENGS:
            ops_list = engine_ops[eng]
            if not ops_list and eng != "sp":
                continue

            def make_body(eng, ops_list):
                def body(e):
                    for i in ops_list:
                        if gate[i] is not None:
                            key, val = gate[i]
                            e.wait_ge(sems[key], val)
                        for (key, val) in waits[i]:
                            e.wait_ge(sems[key], val)
                        inst = self.ops[i][1](e)
                        myeng, myseq = eng_seq[i]
                        if need_inc[i] or myeng in ("sp", "pool"):
                            inst.then_inc(sems[sem_of(myeng, myseq)], INC[myeng])
                    if eng in ("sp", "pool"):
                        nops = len(ops_list)
                        for sl in range(min(NSLOT, nops)):
                            cnt_s = (nops - sl + NSLOT - 1) // NSLOT
                            e.wait_ge(sems[(eng, sl)], 16 * cnt_s)
                return body

            getattr(block, BLOCKNAME[eng])(make_body(eng, ops_list))


def build_program(dbg=(), phases=tuple(range(1, 14))):
    nc = bass.Bass("TRN2", target_bir_lowering=False)
    dt4, dt2 = 4, 2

    def din(name, shape, dt=F32):
        return nc.dram_tensor(name, shape, dt, kind="ExternalInput")

    xT = din("xT", [BLOC * D, L])
    wqkvT = din("wqkvT", [D, 3 * D])
    woT = din("woT", [D, D])
    u1 = din("u1", [D, RK]); v1 = din("v1", [RK, DFF])
    u2 = din("u2", [DFF, RK]); v2 = din("v2", [RK, D])
    wvgsT = din("wvgsT", [D, Hn])
    b1c_in = din("b1c", [128, 32]); b2c_in = din("b2c", [128, 8])
    wcols_in = din("wcols", [128, 512])
    wltmT = din("wltmT", [Hn, DK, DK])
    sel_in = din("selc", [32, 128, 64], BF16)
    blk_in = din("blkc", [8, 128, 16])
    bc16_in = din("bc16c", [8, 16, 128]); bc16n_in = din("bc16nc", [8, 16, 128])
    onesdiv_in = din("onesdiv", [128, 1])
    bc1_in = din("bc1", [1, 128]); bc1n_in = din("bc1n", [1, 128])
    ones16_in = din("ones16", [1, 16])
    bt_in = din("betatau", [1, 16]); cc_in = din("cch", [1, 16])
    gm_in = din("gammav", [1, 16]); bg_in = din("betagm", [1, 16])
    al_in = din("alphap", [1, 1]); ic_in = din("icth", [1, 1])
    vgsb_in = din("vgsbase", [16]); vt0_in = din("vt0", [16])

    outT = nc.dram_tensor("outT", [BLOC * D, L], F32, kind="ExternalOutput")

    def scr(name, shape, dt=F32):
        kind = "ExternalOutput" if name in dbg else "Internal"
        return nc.dram_tensor(name, shape, dt, kind=kind)

    x1s = scr("x1s", [BLOC * D, L]); x2s = scr("x2s", [BLOC * D, L])
    qs = scr("qs", [BLOC * D, L]); ks = scr("ks", [BLOC * D, L]); vs = scr("vs", [BLOC * D, L])
    ss = scr("ss", [BLOC * Hn, L])
    ys = scr("ys", [BLOC * D, L], BF16)
    xlns = scr("xlns", [BLOC * D, L], BF16)
    p1s = scr("p1s", [BLOC * RK, L], BF16)
    q2s = scr("q2s", [BLOC * RK, L], BF16)
    hs = scr("hs", [BLOC * DFF, L], BF16)
    wb_qkv = scr("wb_qkv", [D, 3 * D], BF16)
    wb_wo = scr("wb_wo", [D, D], BF16)
    wb_u1 = scr("wb_u1", [D, RK], BF16)
    wb_v1 = scr("wb_v1", [RK, DFF], BF16)
    wb_u2h = scr("wb_u2h", [DFF, RK], BF16)
    wb_v2 = scr("wb_v2", [RK, D], BF16)
    wb_vgs = scr("wb_vgs", [D, Hn], BF16)
    cbounce = scr("cbounce", [4, 16])

    bd = Builder()

    with ExitStack() as ctx:
        ARF_h = ctx.enter_context(nc.sbuf_tensor("ARF", [128, 38912], F32))
        ARB_h = ctx.enter_context(nc.sbuf_tensor("ARB", [128, 27648], BF16))
        PS_h = ctx.enter_context(nc.psum_tensor("PS", [128, 4096], F32))
        sems = {}
        for e in ("pe", "act", "dve"):
            sems[(e, 0)] = ctx.enter_context(nc.semaphore("sem_" + e))
        for e in ("sp", "pool"):
            for sl in range(Builder.NSLOT):
                sems[(e, sl)] = ctx.enter_context(nc.semaphore(f"sem_{e}{sl}"))
        block = ctx.enter_context(nc.Block())

        def F(p0, np_, c0, nc_):
            return View(ARF_h, p0, np_, c0, nc_, 4)

        def Bv(p0, np_, c0, nc_):
            return View(ARB_h, p0, np_, c0, nc_, 2)

        def P(p0, np_, c0, nc_):
            return View(PS_h, p0, np_, c0, nc_, 4)

        v_sbc = F(0, 128, 0, 4096)
        v_krep = F(0, 128, 4096, 4096)
        v_qrep = F(0, 128, 8192, 4096)
        v_vh = F(64, 64, 12288, 4096)
        v_bc16 = F(0, 16, 12288, 1024)
        v_bc16n = F(0, 16, 13312, 1024)
        v_blk = F(0, 128, 14336, 128)
        CH = 16384
        v_sp = [F(0, 128, CH, T2), F(0, 128, CH + T2, T2)]
        v_tsig = F(0, 128, CH + 2 * T2, T2)
        v_sgn = F(0, 128, CH + 3 * T2, T2)
        v_R = [F(0, 128, CH + 4 * T2, T2), F(0, 128, CH + 5 * T2, T2)]
        v_evr = F(0, 128, CH + 6 * T2, T2)
        v_g1 = F(0, 128, CH + 7 * T2, T2)
        v_cc = F(0, 128, CH + 8 * T2, T2)
        v_ff = F(0, 128, CH + 9 * T2, T2)
        v_carry = F(0, 128, 36864, 8)
        PC = 36880
        v_wcols = F(0, 128, PC, 512)
        v_w05 = F(0, 128, PC + 512, 512)
        v_cexp = F(0, 128, PC + 1024, 16)
        v_tsc = F(0, 128, PC + 1040, 16)
        v_tsb = F(0, 128, PC + 1056, 16)
        v_b1c = F(0, 128, PC + 1072, 32)
        v_b2hc = F(0, 128, PC + 1104, 8)
        v_onesdiv = F(0, 128, PC + 1112, 1)
        v_bc1 = F(0, 1, PC + 1120, 128)
        v_bc1n = F(0, 1, PC + 1248, 128)
        v_vpreK = F(0, 128, PC + 1376, 8)
        v_sb16 = F(0, 16, PC + 1384, 1)
        v_wtmp = F(0, 128, PC + 1392, 16)
        v_row1 = F(0, 1, PC + 1408, 16)
        v_row2 = F(0, 1, PC + 1424, 16)
        v_row3 = F(0, 1, PC + 1440, 16)
        st_x = F(0, 128, 0, 4096)
        st_x2 = F(0, 128, 4096, 4096)
        st_u = F(0, 128, 24576, 512)
        st_u2 = F(0, 128, 25088, 512)
        st_p1g = F(0, 128, 25600, 512)
        st_p2 = F(0, 128, 26112, 512)
        st_t1 = F(0, 128, 26624, 512)
        st_t2 = F(0, 128, 27136, 512)
        r_mrow = F(0, 1, 27648, 512)
        r_vrow = F(0, 1, 28160, 512)
        r_msq = F(0, 1, 28672, 512)
        r_varr = F(0, 1, 29184, 512)
        r_lnv = F(0, 1, 29696, 512)
        r_rrow = F(0, 1, 30208, 512)
        r_nmrow = F(0, 1, 30720, 512)
        k_m16 = F(0, 16, 31232, 512)
        k_v16 = F(0, 16, 31744, 512)
        k_msq = F(0, 16, 32256, 512)
        k_var = F(0, 16, 32768, 512)
        k_lnv = F(0, 16, 33280, 512)
        k_r16 = F(0, 16, 33792, 512)
        k_nm16 = F(0, 16, 34304, 512)
        st_kraw = F(0, 128, CH, 4096)
        st_sq = F(0, 128, CH + 4096, 4096)
        st_wf = F(0, 128, CH + 8192, 3072)

        b_m = Bv(0, 128, 0, T2)
        b_sel = Bv(0, 128, 2048, 2048)
        b_wq = Bv(0, 64, 4096, 1024)
        b_qbf = Bv(0, 64, 5120, 4096)
        b_ybf = Bv(0, 64, 9216, 4096)
        b_xmov = Bv(0, 128, 13312, 4096)
        b_work2 = Bv(0, 128, 17408, 2048)
        b_slab = Bv(0, 128, 19456, 8192)
        b_wcast = Bv(0, 128, 13312, 3072)

        ps_bank = [P(0, 128, 512 * i, 512) for i in range(8)]
        y_ps = [P(0, 64, 0, 2048), P(0, 64, 2048, 2048)]

        op = bd.op

        def dma(dst_ap, src_ap, reads, writes, eng="sp"):
            op(eng, reads, writes, lambda e, d=dst_ap, s=src_ap: e.dma_start(out=d, in_=s))

        rb = {
            id(xT): L * 4, id(outT): L * 4, id(x1s): L * 4, id(x2s): L * 4,
            id(qs): L * 4, id(ks): L * 4, id(vs): L * 4, id(ss): L * 4,
            id(ys): L * 2, id(xlns): L * 2, id(p1s): L * 2, id(q2s): L * 2, id(hs): L * 2,
        }

        def drg(h, r0, r1):
            return dreg(h, r0, r1, rb[id(h)])

        def wreg(h):
            return (id(h), 0, 1, 0, 1 << 40)

        v_eps16 = F(0, 16, PC + 1600, 1)
        op("dve", [], [v_eps16],
           lambda e: e.memset(v_eps16.ap(), EPS))
        dma(v_onesdiv.ap(), onesdiv_in[:, :], [wreg(onesdiv_in)], [v_onesdiv])
        dma(v_bc1.ap(), bc1_in[:, :], [wreg(bc1_in)], [v_bc1])
        dma(v_bc1n.ap(), bc1n_in[:, :], [wreg(bc1n_in)], [v_bc1n])
        dma(v_b1c.ap(), b1c_in[:, :], [wreg(b1c_in)], [v_b1c])
        dma(v_wcols.ap(), wcols_in[:, :], [wreg(wcols_in)], [v_wcols])
        op("dve", [v_wcols], [v_w05],
           lambda e: e.tensor_scalar(out=v_w05.ap(), in0=v_wcols.ap(), scalar1=0.5,
                                     scalar2=None, op0=ALU.mult))
        tmpb2 = F(0, 128, PC + 1456, 8)
        dma(tmpb2.ap(), b2c_in[:, :], [wreg(b2c_in)], [tmpb2])
        op("dve", [tmpb2], [v_b2hc],
           lambda e: e.tensor_scalar(out=v_b2hc.ap(), in0=tmpb2.ap(), scalar1=0.5,
                                     scalar2=None, op0=ALU.mult))
        src = bass.AP(bc16_in, 0, [[128, 16], [2048, 8], [1, 128]])
        dma(v_bc16.ap(), src, [wreg(bc16_in)], [v_bc16])
        src = bass.AP(bc16n_in, 0, [[128, 16], [2048, 8], [1, 128]])
        dma(v_bc16n.ap(), src, [wreg(bc16n_in)], [v_bc16n])
        src = bass.AP(blk_in, 0, [[16, 128], [2048, 8], [1, 16]])
        dma(v_blk.ap(), src, [wreg(blk_in)], [v_blk])
        src = bass.AP(sel_in, 0, [[64, 128], [8192, 32], [1, 64]])
        dma(b_sel.ap(), src, [wreg(sel_in)], [b_sel])
        wq_f = F(0, 64, CH, 1024)
        src = bass.AP(wltmT, 0, [[64, 64], [4096, 16], [1, 64]])
        dma(wq_f.ap(), src, [wreg(wltmT)], [wq_f])
        op("dve", [wq_f], [b_wq],
           lambda e: e.tensor_copy(out=b_wq.ap(), in_=wq_f.ap()))
        rbt = F(0, 1, PC + 1472, 16); rcc = F(0, 1, PC + 1488, 16)
        rgm = F(0, 1, PC + 1504, 16); rbg = F(0, 1, PC + 1520, 16)
        ral = F(0, 1, PC + 1536, 1); ric = F(0, 1, PC + 1537, 1)
        r16o = F(0, 1, PC + 1538, 16)
        rt1 = F(0, 1, PC + 1554, 16); rt2 = F(0, 1, PC + 1570, 16)
        dma(rbt.ap(), bt_in[:, :], [wreg(bt_in)], [rbt])
        dma(rcc.ap(), cc_in[:, :], [wreg(cc_in)], [rcc])
        dma(rgm.ap(), gm_in[:, :], [wreg(gm_in)], [rgm])
        dma(rbg.ap(), bg_in[:, :], [wreg(bg_in)], [rbg])
        dma(ral.ap(), al_in[:, :], [wreg(al_in)], [ral])
        dma(ric.ap(), ic_in[:, :], [wreg(ic_in)], [ric])
        dma(r16o.ap(), ones16_in[:, :], [wreg(ones16_in)], [r16o])
        op("dve", [rcc], [rt1], lambda e: e.reciprocal(out=rt1.sub(0, 1, 0, 16).ap(), in_=rcc.ap()))
        op("dve", [rbt, rt1], [rt2],
           lambda e: e.tensor_tensor(out=rt2.ap(), in0=rbt.ap(), in1=rt1.ap(), op=ALU.mult))
        op("dve", [rt2], [rt2],
           lambda e: e.tensor_scalar(out=rt2.ap(), in0=rt2.ap(), scalar1=-1.0,
                                     scalar2=None, op0=ALU.mult))
        dma(cbounce[0:1, :], rt2.ap(), [rt2], [wreg(cbounce)])
        op("dve", [rbt, ral], [rt1],
           lambda e: e.tensor_scalar(out=rt1.ap(), in0=rbt.ap(), scalar1=ral.ap(),
                                     scalar2=None, op0=ALU.mult))
        dma(cbounce[1:2, :], rt1.ap(), [rt1, wreg(cbounce)], [wreg(cbounce)])
        op("dve", [r16o, ral, ric], [rt1],
           lambda e: e.tensor_scalar(out=rt1.ap(), in0=r16o.ap(), scalar1=ral.ap(),
                                     scalar2=ric.ap(), op0=ALU.mult, op1=ALU.mult))
        op("dve", [rt1], [rt1],
           lambda e: e.tensor_scalar(out=rt1.ap(), in0=rt1.ap(), scalar1=-1.0,
                                     scalar2=None, op0=ALU.mult))
        dma(cbounce[2:3, :], rt1.ap(), [rt1, wreg(cbounce)], [wreg(cbounce)])
        op("dve", [rgm, rbg], [rt1],
           lambda e: e.tensor_tensor(out=rt1.ap(), in0=rgm.ap(), in1=rbg.ap(), op=ALU.mult))
        op("dve", [rt1], [rt1],
           lambda e: e.tensor_scalar(out=rt1.ap(), in0=rt1.ap(), scalar1=0.5,
                                     scalar2=None, op0=ALU.mult))
        dma(cbounce[3:4, :], rt1.ap(), [rt1, wreg(cbounce)], [wreg(cbounce)])
        dma(v_cexp.ap(), cbounce[0:1, :].partition_broadcast(128), [wreg(cbounce)], [v_cexp])
        dma(v_tsc.ap(), cbounce[1:2, :].partition_broadcast(128), [wreg(cbounce)], [v_tsc])
        dma(v_tsb.ap(), cbounce[2:3, :].partition_broadcast(128), [wreg(cbounce)], [v_tsb])
        for dd in range(2):
            dst_ap = v_vpreK.sub(64 * dd, 64, 0, 8).ap()
            src_ap = bass.AP(cbounce, 3 * 16 + dd, [[0, 64], [2, 8]])
            op("sp", [wreg(cbounce)], [v_vpreK],
               lambda e, d=dst_ap, s=src_ap: e.dma_start(
                   out=d, in_=s, allow_slow_non_contiguous=True))
        sb_a = F(0, 16, PC + 1586, 1); sb_b = F(0, 16, PC + 1587, 1)
        dma(sb_a.ap(), bass.AP(vgsb_in, 0, [[1, 16], [1, 1]]), [wreg(vgsb_in)], [sb_a])
        dma(sb_b.ap(), bass.AP(vt0_in, 0, [[1, 16], [1, 1]]), [wreg(vt0_in)], [sb_b])
        op("dve", [sb_a, sb_b], [v_sb16],
           lambda e: e.tensor_tensor(out=v_sb16.ap(), in0=sb_a.ap(), in1=sb_b.ap(),
                                     op=ALU.subtract))

        def wcast(src_h, dst_h, rows, cols, scale=None):
            for r0 in range(0, rows, 128):
                for c0 in range(0, cols, 3072):
                    cw = min(3072, cols - c0)
                    fsrc = st_wf.sub(0, 128, 0, cw)
                    bdst = b_wcast.sub(0, 128, 0, cw)
                    dma(fsrc.ap(), src_h[r0:r0 + 128, c0:c0 + cw], [wreg(src_h)], [fsrc])
                    if scale is None:
                        op("dve", [fsrc], [bdst],
                           lambda e, a=bdst.ap(), b_=fsrc.ap(): e.tensor_copy(out=a, in_=b_))
                    else:
                        op("dve", [fsrc], [bdst],
                           lambda e, a=bdst.ap(), b_=fsrc.ap(), s=scale: e.tensor_scalar(
                               out=a, in0=b_, scalar1=s, scalar2=None, op0=ALU.mult))
                    dma(dst_h[r0:r0 + 128, c0:c0 + cw], bdst.ap(), [bdst], [wreg(dst_h)])

        wcast(wqkvT, wb_qkv, D, 3 * D)
        wcast(woT, wb_wo, D, D)
        wcast(u1, wb_u1, D, RK)
        wcast(v1, wb_v1, RK, DFF)
        wcast(u2, wb_u2h, DFF, RK, scale=0.25)
        wcast(v2, wb_v2, RK, D)
        wcast(wvgsT, wb_vgs, D, Hn)

        def load_slab(src_h, col0, nk, nm, src_cols):
            for kc in range(nk):
                dstv = b_slab.sub(0, 128, kc * nm * 128, nm * 128)
                dma(dstv.ap(), src_h[kc * 128:(kc + 1) * 128, col0:col0 + nm * 128],
                    [wreg(src_h)], [dstv])

        def slab_tile(kc, ml, nm):
            return b_slab.sub(0, 128, (kc * nm + ml) * 128, 128)

        def mm(psv, lhv, rhv, start, stop, extra_reads=()):
            op("pe", [lhv, rhv] + list(extra_reads), [psv],
               lambda e, o=psv.ap(), l=lhv.ap(), r=rhv.ap(), st=start, sp=stop:
               e.matmul(out=o, lhsT=l, rhs=r, start=st, stop=sp))

        def act(outv, inv, func, bias=0.0, scale=1.0, extra_reads=()):
            br = [bias] if isinstance(bias, View) else []
            sr = [scale] if isinstance(scale, View) else []
            bap = bias.ap() if isinstance(bias, View) else bias
            sap = scale.ap() if isinstance(scale, View) else scale
            if func == AF.Copy and br:
                assert not sr and scale == 1.0
                op("act", [inv] + br + list(extra_reads), [outv],
                   lambda e, o=outv.ap(), i=inv.ap(), b_=bap:
                   e.add(out=o, in_=i, add=b_))
            else:
                op("act", [inv] + br + sr + list(extra_reads), [outv],
                   lambda e, o=outv.ap(), i=inv.ap(), f=func, b_=bap, s=sap:
                   e.activation(out=o, in_=i, func=f, bias=b_, scale=s))

        def tt(outv, av, bv, alu):
            op("dve", [av, bv], [outv],
               lambda e, o=outv.ap(), a=av.ap(), b_=bv.ap(), u=alu:
               e.tensor_tensor(out=o, in0=a, in1=b_, op=u))

        def ts(outv, av, s1, s2, op0, op1=None):
            s1r = [s1] if isinstance(s1, View) else []
            s1ap = s1.ap() if isinstance(s1, View) else s1
            if op1 is None:
                op("dve", [av] + s1r, [outv],
                   lambda e, o=outv.ap(), a=av.ap(), s=s1ap, u=op0:
                   e.tensor_scalar(out=o, in0=a, scalar1=s, scalar2=None, op0=u))
            else:
                op("dve", [av] + s1r, [outv],
                   lambda e, o=outv.ap(), a=av.ap(), s=s1ap, s2_=s2, u0=op0, u1=op1:
                   e.tensor_scalar(out=o, in0=a, scalar1=s, scalar2=s2_, op0=u0, op1=u1))

        def stt(outv, av, sc, bv, op0, op1):
            op("dve", [av, bv], [outv],
               lambda e, o=outv.ap(), a=av.ap(), s=sc, b_=bv.ap(), u0=op0, u1=op1:
               e.scalar_tensor_tensor(out=o, in0=a, scalar=s, in1=b_, op0=u0, op1=u1))

        def layernorm(xtiles, sqv, xlnouts):
            whole = xtiles[0].sub(0, 128, 0, 4096)
            act(sqv, whole, AF.Square)
            for kc in range(8):
                mm(ps_bank[6].sub(0, 1, 0, 512), v_onesdiv, xtiles[kc], kc == 0, kc == 7)
            for kc in range(8):
                mm(ps_bank[7].sub(0, 1, 0, 512), v_onesdiv, sqv.sub(0, 128, kc * 512, 512),
                   kc == 0, kc == 7)
            act(r_mrow, ps_bank[6].sub(0, 1, 0, 512), AF.Copy)
            act(r_vrow, ps_bank[7].sub(0, 1, 0, 512), AF.Copy)
            act(r_msq, r_mrow, AF.Square)
            tt(r_varr, r_vrow, r_msq, ALU.subtract)
            act(r_lnv, r_varr, AF.Ln, bias=v_eps16.sub(0, 1, 0, 1))
            act(r_rrow, r_lnv, AF.Exp, scale=-0.5)
            tt(r_nmrow, r_mrow, r_rrow, ALU.mult)
            mm(ps_bank[4], v_bc1, r_rrow, True, True)
            mm(ps_bank[5], v_bc1n, r_nmrow, True, True)
            for kc in range(8):
                tt(st_t1, xtiles[kc], ps_bank[4], ALU.mult)
                tt(xlnouts[kc], st_t1, ps_bank[5], ALU.add)

        def ph_u1(with_ln):
            load_slab(wb_u1, 0, 8, 2, RK)
            for b in range(BLOC):
                for tt_i in range(L // TOKT):
                    t0 = tt_i * TOKT
                    xm = [b_xmov.sub(0, 128, kc * 512, 512) for kc in range(8)]
                    if with_ln:
                        xts = [st_x.sub(0, 128, kc * 512, 512) for kc in range(8)]
                        for kc in range(8):
                            dma(xts[kc].ap(), xT[b * D + kc * 128:b * D + (kc + 1) * 128,
                                                 t0:t0 + TOKT],
                                [wreg(xT)], [xts[kc]])
                        layernorm(xts, st_sq, xm)
                    else:
                        for kc in range(8):
                            dma(xm[kc].ap(), xlns[b * D + kc * 128:b * D + (kc + 1) * 128,
                                                  t0:t0 + TOKT],
                                [drg(xlns, b * D + kc * 128, b * D + (kc + 1) * 128)], [xm[kc]])
                    for m2 in range(2):
                        psv = ps_bank[m2]
                        for kc in range(8):
                            mm(psv, slab_tile(kc, m2, 2), xm[kc], kc == 0, kc == 7)
                        ov = b_work2.sub(0, 128, m2 * 512, 512)
                        op("dve", [psv], [ov],
                           lambda e, o=ov.ap(), i=psv.ap(): e.tensor_copy(out=o, in_=i))
                        dma(p1s[b * RK + m2 * 128:b * RK + (m2 + 1) * 128, t0:t0 + TOKT],
                            ov.ap(), [ov], [drg(p1s, b * RK + m2 * 128, b * RK + (m2 + 1) * 128)])

        def ph_v1gelu():
            load_slab(wb_v1, 0, 2, 32, DFF)
            for b in range(BLOC):
                for tt_i in range(L // TOKT):
                    t0 = tt_i * TOKT
                    pm = [b_xmov.sub(0, 128, r2 * 512, 512) for r2 in range(2)]
                    for r2 in range(2):
                        dma(pm[r2].ap(), p1s[b * RK + r2 * 128:b * RK + (r2 + 1) * 128,
                                             t0:t0 + TOKT],
                            [drg(p1s, b * RK + r2 * 128, b * RK + (r2 + 1) * 128)], [pm[r2]])
                    for fc in range(32):
                        psv = ps_bank[fc % 2]
                        for r2 in range(2):
                            mm(psv, slab_tile(r2, fc, 32), pm[r2], r2 == 0, r2 == 1)
                        act(st_u, psv, AF.Copy, bias=v_b1c.sub(0, 128, fc, 1))
                        act(st_u2, st_u, AF.Square)
                        ts(st_p1g, st_u2, GELU_A, 1.0, ALU.mult, ALU.add)
                        tt(st_p2, st_p1g, st_u, ALU.mult)
                        act(st_u2, st_p2, AF.Tanh, scale=GELU_C)
                        hv = b_work2.sub(0, 128, (fc % 4) * 512, 512)
                        stt(hv, st_u2, 1.0, st_u, ALU.add, ALU.mult)
                        dma(hs[b * DFF + fc * 128:b * DFF + (fc + 1) * 128, t0:t0 + TOKT],
                            hv.ap(), [hv], [drg(hs, b * DFF + fc * 128, b * DFF + (fc + 1) * 128)])

        def ph_u2():
            for kc in range(32):
                dstv = b_slab.sub(0, 128, kc * 256, 256)
                dma(dstv.ap(), wb_u2h[kc * 128:(kc + 1) * 128, :], [wreg(wb_u2h)], [dstv])
            for b in range(BLOC):
                for tt_i in range(L // TOKT):
                    t0 = tt_i * TOKT
                    for fg in range(4):
                        hm = [b_xmov.sub(0, 128, i * 512, 512) for i in range(8)]
                        for i in range(8):
                            fc = fg * 8 + i
                            dma(hm[i].ap(), hs[b * DFF + fc * 128:b * DFF + (fc + 1) * 128,
                                               t0:t0 + TOKT],
                                [drg(hs, b * DFF + fc * 128, b * DFF + (fc + 1) * 128)], [hm[i]])
                        for i in range(8):
                            fc = fg * 8 + i
                            for r2 in range(2):
                                mm(ps_bank[2 + r2], b_slab.sub(0, 128, fc * 256 + r2 * 128, 128),
                                   hm[i], fc == 0, fc == 31)
                    for r2 in range(2):
                        ov = b_work2.sub(0, 128, r2 * 512, 512)
                        op("dve", [ps_bank[2 + r2]], [ov],
                           lambda e, o=ov.ap(), i=ps_bank[2 + r2].ap(): e.tensor_copy(out=o, in_=i))
                        dma(q2s[b * RK + r2 * 128:b * RK + (r2 + 1) * 128, t0:t0 + TOKT],
                            ov.ap(), [ov], [drg(q2s, b * RK + r2 * 128, b * RK + (r2 + 1) * 128)])

        def ph_v2res(res_h, out_h, with_ln):
            load_slab(wb_v2, 0, 2, 8, D)
            for b in range(BLOC):
                for tt_i in range(L // TOKT):
                    t0 = tt_i * TOKT
                    qm = [b_xmov.sub(0, 128, r2 * 512, 512) for r2 in range(2)]
                    for r2 in range(2):
                        dma(qm[r2].ap(), q2s[b * RK + r2 * 128:b * RK + (r2 + 1) * 128,
                                             t0:t0 + TOKT],
                            [drg(q2s, b * RK + r2 * 128, b * RK + (r2 + 1) * 128)], [qm[r2]])
                    outs = [st_x.sub(0, 128, mc * 512, 512) for mc in range(8)]
                    for mc in range(8):
                        psv = ps_bank[mc % 2]
                        for r2 in range(2):
                            mm(psv, slab_tile(r2, mc, 8), qm[r2], r2 == 0, r2 == 1)
                        act(st_u, psv, AF.Copy, bias=v_b2hc.sub(0, 128, mc, 1))
                        rv = st_u2
                        dma(rv.ap(), res_h[b * D + mc * 128:b * D + (mc + 1) * 128, t0:t0 + TOKT],
                            [wreg(res_h) if id(res_h) not in rb else
                             drg(res_h, b * D + mc * 128, b * D + (mc + 1) * 128)], [rv])
                        tt(outs[mc], st_u, rv, ALU.add)
                        dma(out_h[b * D + mc * 128:b * D + (mc + 1) * 128, t0:t0 + TOKT],
                            outs[mc].ap(), [outs[mc]],
                            [drg(out_h, b * D + mc * 128, b * D + (mc + 1) * 128)
                             if id(out_h) in rb else wreg(out_h)])
                    if with_ln:
                        xm = [b_xmov.sub(0, 128, 2048 + kc * 512, 512) for kc in range(4)] + \
                             [b_work2.sub(0, 128, kc * 512, 512) for kc in range(4)]
                        layernorm(outs, st_sq, xm)
                        for kc in range(8):
                            dma(xlns[b * D + kc * 128:b * D + (kc + 1) * 128, t0:t0 + TOKT],
                                xm[kc].ap(), [xm[kc]],
                                [drg(xlns, b * D + kc * 128, b * D + (kc + 1) * 128)])

        def ph_qkv(mg):
            load_slab(wb_qkv, mg * 1024, 8, 8, 3 * D)
            if mg == 2:
                for kc in range(8):
                    dstv = b_work2.sub(0, 128, kc * 16, 16)
                    dma(dstv.ap(), wb_vgs[kc * 128:(kc + 1) * 128, :], [wreg(wb_vgs)], [dstv])
            for b in range(BLOC):
                for tt_i in range(L // TOKT):
                    t0 = tt_i * TOKT
                    xm = [b_xmov.sub(0, 128, kc * 512, 512) for kc in range(8)]
                    for kc in range(8):
                        dma(xm[kc].ap(), xlns[b * D + kc * 128:b * D + (kc + 1) * 128,
                                              t0:t0 + TOKT],
                            [drg(xlns, b * D + kc * 128, b * D + (kc + 1) * 128)], [xm[kc]])
                    if mg == 0:
                        for mc in range(8):
                            psv = ps_bank[mc % 2]
                            for kc in range(8):
                                mm(psv, slab_tile(kc, mc, 8), xm[kc], kc == 0, kc == 7)
                            act(st_u, psv, AF.Copy)
                            dma(qs[b * D + mc * 128:b * D + (mc + 1) * 128, t0:t0 + TOKT],
                                st_u.ap(), [st_u],
                                [drg(qs, b * D + mc * 128, b * D + (mc + 1) * 128)])
                    elif mg == 1:
                        kraw = [st_kraw.sub(0, 128, mc * 512, 512) for mc in range(8)]
                        for mc in range(8):
                            psv = ps_bank[mc % 2]
                            for kc in range(8):
                                mm(psv, slab_tile(kc, mc, 8), xm[kc], kc == 0, kc == 7)
                            act(kraw[mc], psv, AF.Copy)
                        act(st_sq, st_kraw, AF.Square)
                        for jk in range(8):
                            mm(ps_bank[6].sub(0, 16, 0, 512),
                               v_blk.sub(0, 128, jk * 16, 16), kraw[jk], jk == 0, jk == 7)
                        for jk in range(8):
                            mm(ps_bank[7].sub(0, 16, 0, 512),
                               v_blk.sub(0, 128, jk * 16, 16),
                               st_sq.sub(0, 128, jk * 512, 512), jk == 0, jk == 7)
                        act(k_m16, ps_bank[6].sub(0, 16, 0, 512), AF.Copy)
                        act(k_v16, ps_bank[7].sub(0, 16, 0, 512), AF.Copy)
                        act(k_msq, k_m16, AF.Square)
                        tt(k_var, k_v16, k_msq, ALU.subtract)
                        act(k_lnv, k_var, AF.Ln, bias=v_eps16)
                        act(k_r16, k_lnv, AF.Exp, scale=-0.5)
                        tt(k_nm16, k_m16, k_r16, ALU.mult)
                        for jk in range(8):
                            pr = ps_bank[4 if jk % 2 == 0 else 6]
                            pn = ps_bank[5 if jk % 2 == 0 else 7]
                            mm(pr, v_bc16.sub(0, 16, jk * 128, 128), k_r16, True, True)
                            mm(pn, v_bc16n.sub(0, 16, jk * 128, 128), k_nm16, True, True)
                            klv = st_x.sub(0, 128, 0, 512)
                            tt(st_t1, kraw[jk], pr, ALU.mult)
                            tt(klv, st_t1, pn, ALU.add)
                            dma(ks[b * D + jk * 128:b * D + (jk + 1) * 128, t0:t0 + TOKT],
                                klv.ap(), [klv],
                                [drg(ks, b * D + jk * 128, b * D + (jk + 1) * 128)])
                    else:
                        for mc in range(8):
                            psv = ps_bank[mc % 2]
                            for kc in range(8):
                                mm(psv, slab_tile(kc, mc, 8), xm[kc], kc == 0, kc == 7)
                            ts(st_u, psv, v_vpreK.sub(0, 128, mc, 1), None, ALU.mult)
                            dma(vs[b * D + mc * 128:b * D + (mc + 1) * 128, t0:t0 + TOKT],
                                st_u.ap(), [st_u],
                                [drg(vs, b * D + mc * 128, b * D + (mc + 1) * 128)])
                        for kc in range(8):
                            mm(ps_bank[6].sub(0, 16, 0, 512),
                               b_work2.sub(0, 128, kc * 16, 16), xm[kc], kc == 0, kc == 7)
                        sv = st_x.sub(0, 16, 512, 512)
                        act(sv, ps_bank[6].sub(0, 16, 0, 512), AF.Copy, bias=v_sb16)
                        dma(ss[b * Hn:(b + 1) * Hn, t0:t0 + TOKT], sv.ap(), [sv],
                            [drg(ss, b * Hn, (b + 1) * Hn)])

        def ph_attn():
            for bh in range(BLOC * Hn):
                b, h = bh // Hn, bh % Hn
                r0 = b * D + h * DK
                dma(v_krep.sub(0, 64, 0, 4096).ap(), ks[r0:r0 + 64, :],
                    [drg(ks, r0, r0 + 64)], [v_krep])
                dma(v_krep.sub(64, 64, 0, 4096).ap(), ks[r0:r0 + 64, :],
                    [drg(ks, r0, r0 + 64)], [v_krep])
                dma(v_qrep.sub(0, 64, 0, 4096).ap(), qs[r0:r0 + 64, :],
                    [drg(qs, r0, r0 + 64)], [v_qrep])
                dma(v_qrep.sub(64, 64, 0, 4096).ap(), qs[r0:r0 + 64, :],
                    [drg(qs, r0, r0 + 64)], [v_qrep])
                dma(v_sbc.ap(), ss[b * Hn + h:b * Hn + h + 1, :].partition_broadcast(128),
                    [drg(ss, b * Hn + h, b * Hn + h + 1)], [v_sbc])
                op("dve", [v_qrep], [b_qbf],
                   lambda e, o=b_qbf.ap(), i=v_qrep.sub(0, 64, 0, 4096).ap():
                   e.tensor_copy(out=o, in_=i))
                cexp_h = v_cexp.sub(0, 128, h, 1)
                tsc_h = v_tsc.sub(0, 128, h, 1)
                tsb_h = v_tsb.sub(0, 128, h, 1)
                wq_h = b_wq.sub(0, 64, h * 64, 64)
                it = 0
                for c in range(32):
                    wcol = v_wcols.sub(0, 128, h * 32 + c, 1)
                    w05 = v_w05.sub(0, 128, h * 32 + c, 1)
                    for th in range(2):
                        par = it % 2
                        sbc_t = v_sbc.sub(0, 128, th * T2, T2)
                        spv = v_sp[par]
                        Rv = v_R[par]
                        act(v_evr, sbc_t, AF.Exp, bias=wcol)
                        act(spv, v_evr, AF.Ln, bias=1.0)
                        act(v_tsig, sbc_t, AF.Tanh, bias=w05, scale=0.5)
                        act(v_sgn, spv, AF.Tanh, bias=tsb_h, scale=tsc_h)
                        act(Rv, spv, AF.Exp, scale=cexp_h)
                        for dd in range(2):
                            vrow = r0 + 2 * c + dd
                            dma(v_evr.sub(64 * dd, 64, 0, T2).ap(),
                                vs[vrow:vrow + 1,
                                   th * T2:th * T2 + T2].partition_broadcast(64),
                                [drg(vs, vrow, vrow + 1)], [v_evr])
                        krep_t = v_krep.sub(0, 128, th * T2, T2)
                        qrep_t = v_qrep.sub(0, 128, th * T2, T2)
                        tt(v_g1, krep_t, v_evr, ALU.mult)
                        stt(v_cc, v_tsig, 1.0, v_sgn, ALU.add, ALU.mult)
                        tt(v_cc, v_cc, spv, ALU.mult)
                        tt(v_g1, v_cc, v_g1, ALU.mult)
                        if th == 0:
                            op("dve", [Rv, v_g1], [v_ff],
                               lambda e, o=v_ff.ap(), r=Rv.ap(), z=v_g1.ap():
                               e.tensor_tensor_scan(out=o, data0=r, data1=z, initial=0.0,
                                                    op0=ALU.mult, op1=ALU.add))
                            op("dve", [v_ff], [v_carry],
                               lambda e, o=v_carry.sub(0, 128, 0, 1).ap(),
                               i=v_ff.sub(0, 128, T2 - 1, 1).ap(): e.tensor_copy(out=o, in_=i))
                        else:
                            op("dve", [Rv, v_g1, v_carry], [v_ff],
                               lambda e, o=v_ff.ap(), r=Rv.ap(), z=v_g1.ap(),
                               ini=v_carry.sub(0, 128, 0, 1).ap():
                               e.tensor_tensor_scan(out=o, data0=r, data1=z, initial=ini,
                                                    op0=ALU.mult, op1=ALU.add))
                        tt(b_m, v_ff, qrep_t, ALU.mult)
                        if c == 0:
                            for s4 in range(4):
                                mm(y_ps[th].sub(0, 64, s4 * 512, 512), wq_h,
                                   b_qbf.sub(0, 64, th * T2 + s4 * 512, 512), True, False)
                        sel_c = b_sel.sub(0, 128, c * 64, 64)
                        for s4 in range(4):
                            mm(y_ps[th].sub(0, 64, s4 * 512, 512), sel_c,
                               b_m.sub(0, 128, s4 * 512, 512), False, c == 31)
                        if c == 31:
                            ybv = b_ybf.sub(0, 64, th * T2, T2)
                            act(ybv, y_ps[th], AF.Copy)
                        it += 1
                dma(ys[r0:r0 + 64, :], b_ybf.ap(), [b_ybf], [drg(ys, r0, r0 + 64)])

        def ph_wo():
            for kc in range(8):
                dstv = b_slab.sub(0, 128, kc * 1024, 1024)
                dma(dstv.ap(), wb_wo[kc * 128:(kc + 1) * 128, :], [wreg(wb_wo)], [dstv])
            for b in range(BLOC):
                for tt_i in range(L // TOKT):
                    t0 = tt_i * TOKT
                    ym = [b_xmov.sub(0, 128, kc * 512, 512) for kc in range(8)]
                    for kc in range(8):
                        dma(ym[kc].ap(), ys[b * D + kc * 128:b * D + (kc + 1) * 128, t0:t0 + TOKT],
                            [drg(ys, b * D + kc * 128, b * D + (kc + 1) * 128)], [ym[kc]])
                    outs = [st_x2.sub(0, 128, mc * 512, 512) for mc in range(8)]
                    for mc in range(8):
                        psv = ps_bank[mc % 2]
                        for kc in range(8):
                            mm(psv, b_slab.sub(0, 128, kc * 1024 + mc * 128, 128), ym[kc],
                               kc == 0, kc == 7)
                        rv = st_u
                        dma(rv.ap(), x1s[b * D + mc * 128:b * D + (mc + 1) * 128, t0:t0 + TOKT],
                            [drg(x1s, b * D + mc * 128, b * D + (mc + 1) * 128)], [rv])
                        tt(outs[mc], rv, psv, ALU.add)
                        dma(x2s[b * D + mc * 128:b * D + (mc + 1) * 128, t0:t0 + TOKT],
                            outs[mc].ap(), [outs[mc]],
                            [drg(x2s, b * D + mc * 128, b * D + (mc + 1) * 128)])
                    xm = [b_xmov.sub(0, 128, 4096 - 4096 + kc * 512, 512) for kc in range(8)]
                    layernorm(outs, st_sq, xm)
                    for kc in range(8):
                        dma(xlns[b * D + kc * 128:b * D + (kc + 1) * 128, t0:t0 + TOKT],
                            xm[kc].ap(), [xm[kc]],
                            [drg(xlns, b * D + kc * 128, b * D + (kc + 1) * 128)])

        phset = set(phases)
        if 1 in phset: ph_u1(with_ln=True)
        if 2 in phset: ph_v1gelu()
        if 3 in phset: ph_u2()
        if 4 in phset: ph_v2res(xT, x1s, True)
        if 5 in phset: ph_qkv(0)
        if 6 in phset: ph_qkv(1)
        if 7 in phset: ph_qkv(2)
        if 8 in phset: ph_attn()
        if 9 in phset: ph_wo()
        if 10 in phset: ph_u1(with_ln=False)
        if 11 in phset: ph_v1gelu()
        if 12 in phset: ph_u2()
        if 13 in phset: ph_v2res(x2s, outT, False)

        bd.emit(nc, block, sems)

    return nc


_host_cache = {}


def kernel(**inputs):
    x = np.asarray(inputs["L0_attn_x"], np.float32)
    W_qkv = np.asarray(inputs["L0_attn_W_qkv"], np.float32)
    W_o = np.asarray(inputs["L0_attn_W_o"], np.float32)
    W_vgs = np.asarray(inputs["L0_attn_W_vgs"], np.float32)
    W_LTM = np.asarray(inputs["L0_attn_W_LTM"], np.float32)
    V_gs_base = np.asarray(inputs["L0_attn_V_gs_base"], np.float32)
    V_T0 = np.asarray(inputs["L0_attn_V_T0"], np.float32)
    beta_tau = np.asarray(inputs["L0_attn_beta_tau"], np.float32)
    beta_gm = np.asarray(inputs["L0_attn_beta_gm"], np.float32)
    C_ch = np.asarray(inputs["L0_attn_C_ch"], np.float32)
    gamma = np.asarray(inputs["L0_attn_gamma"], np.float32)
    alpha = np.asarray(inputs["L0_attn_alpha_ppd"], np.float32)
    ic = np.asarray(inputs["L0_attn_IC_threshold"], np.float32)
    U1 = np.asarray(inputs["L0_ffn_U1"], np.float32)
    V1 = np.asarray(inputs["L0_ffn_V1"], np.float32)
    b1 = np.asarray(inputs["L0_ffn_b1"], np.float32)
    U2 = np.asarray(inputs["L0_ffn_U2"], np.float32)
    V2 = np.asarray(inputs["L0_ffn_V2"], np.float32)
    b2 = np.asarray(inputs["L0_ffn_b2"], np.float32)

    selc = np.zeros((32, 128, 64), ml_dtypes.bfloat16)
    for c in range(32):
        for p in range(128):
            selc[c, p, 2 * c + p // 64] = 1.0
    blkc = np.zeros((8, 128, 16), np.float32)
    bc16c = np.zeros((8, 16, 128), np.float32)
    for jk in range(8):
        for p in range(128):
            hh = 2 * jk + p // 64
            blkc[jk, p, hh] = 1.0 / 64.0
            bc16c[jk, hh, p] = 1.0
    bc16nc = -bc16c
    common = dict(
        wqkvT=np.ascontiguousarray(W_qkv.T),
        woT=np.ascontiguousarray(W_o.T),
        u1=U1, v1=V1, u2=U2, v2=V2,
        wvgsT=np.ascontiguousarray(W_vgs.T),
        b1c=np.ascontiguousarray(b1.reshape(32, 128).T),
        b2c=np.ascontiguousarray(b2.reshape(8, 128).T),
        wcols=np.ascontiguousarray(
            W_LTM.reshape(Hn, 32, 128).transpose(2, 0, 1).reshape(128, 512)),
        wltmT=np.ascontiguousarray(W_LTM.transpose(0, 2, 1)),
        selc=selc, blkc=blkc, bc16c=bc16c, bc16nc=bc16nc,
        onesdiv=np.full((128, 1), 1.0 / 1024.0, np.float32),
        bc1=np.ones((1, 128), np.float32),
        bc1n=-np.ones((1, 128), np.float32),
        ones16=np.ones((1, 16), np.float32),
        betatau=beta_tau.reshape(1, 16), cch=C_ch.reshape(1, 16),
        gammav=gamma.reshape(1, 16), betagm=beta_gm.reshape(1, 16),
        alphap=alpha.reshape(1, 1), icth=ic.reshape(1, 1),
        vgsbase=V_gs_base, vt0=V_T0,
    )
    in_maps = []
    for ci in range(NC_):
        xs = np.ascontiguousarray(
            x[ci * BLOC:(ci + 1) * BLOC].transpose(0, 2, 1).reshape(BLOC * D, L))
        m = dict(common)
        m["xT"] = xs
        in_maps.append(m)

    nc = build_program()
    res = run_bass_kernel_spmd(nc, in_maps, core_ids=list(range(NC_)))
    outs = []
    for ci in range(NC_):
        o = res.results[ci]["outT"].reshape(BLOC, D, L).transpose(0, 2, 1)
        outs.append(o)
    return np.ascontiguousarray(np.concatenate(outs, axis=0)).astype(np.float32)


# version 05
# speedup vs baseline: 1.0972x; 1.0126x over previous
import sys
sys.path.insert(0, "/opt/trn_rl_repo")
import numpy as np
import ml_dtypes
import concourse.bass as bass
import concourse.mybir as mybir
from concourse.bass_utils import run_bass_kernel_spmd
from contextlib import ExitStack

F32 = mybir.dt.float32
BF16 = mybir.dt.bfloat16
AF = mybir.ActivationFunctionType
ALU = mybir.AluOpType

B, L, D, Hn, DK = 16, 4096, 1024, 16, 64
DFF, RK = 4096, 256
NC_, BLOC = 8, 2
TOKT = 512
NTT = (BLOC * L) // TOKT
T2 = 2048
GELU_C = 0.7978845608028654
GELU_A = 0.044715
EPS = 1e-5

ENGS = ["pe", "act", "dve", "sp", "pool"]
BLOCKNAME = {"pe": "tensor", "act": "scalar", "dve": "vector", "sp": "sync", "pool": "gpsimd"}
INC = {"pe": 1, "act": 1, "dve": 1, "sp": 16, "pool": 16}


class View:
    __slots__ = ("h", "p0", "np_", "c0", "nc_", "esz")

    def __init__(self, h, p0, np_, c0, nc_, esz):
        self.h, self.p0, self.np_, self.c0, self.nc_, self.esz = h, p0, np_, c0, nc_, esz

    def ap(self):
        return self.h[self.p0:self.p0 + self.np_, self.c0:self.c0 + self.nc_]

    def reg(self):
        return (id(self.h), self.p0, self.p0 + self.np_,
                self.c0 * self.esz, (self.c0 + self.nc_) * self.esz)

    def sub(self, dp0, np_, dc0, nc_):
        return View(self.h, self.p0 + dp0, np_, self.c0 + dc0, nc_, self.esz)


def dreg(h, r0, r1, rowbytes):
    return (id(h), 0, 1, r0 * rowbytes, r1 * rowbytes)


class Builder:
    def __init__(self):
        self.ops = []

    def op(self, eng, reads, writes, emit):
        self.ops.append((eng, emit, tuple(r.reg() if isinstance(r, View) else r for r in reads),
                         tuple(w.reg() if isinstance(w, View) else w for w in writes)))

    SERIALIZE = False
    SELF_WAITS = True
    NSLOT = 8

    def emit(self, nc, block, sems):
        n = len(self.ops)
        eng_seq = [None] * n
        seqs = {e: 0 for e in ENGS}
        events = {}
        deps = [dict() for _ in range(n)]

        def inter(a, b):
            return a[1] > b[0] and b[1] > a[0]

        for i, (eng, emit, reads, writes) in enumerate(self.ops):
            myseq = seqs[eng]
            seqs[eng] += 1
            eng_seq[i] = (eng, myseq)
            d = deps[i]

            def dep(ev):
                peng, pseq = ev[4], ev[5]
                if peng == eng and not Builder.SELF_WAITS:
                    return
                if peng == eng and pseq == myseq:
                    return
                if d.get(peng, -1) < pseq:
                    d[peng] = pseq

            for r in reads:
                evs = events.get(r[0])
                if evs:
                    pr = (r[1], r[2]); br = (r[3], r[4])
                    for ev in evs:
                        if ev[6] and inter((ev[0], ev[1]), pr) and inter((ev[2], ev[3]), br):
                            dep(ev)
            for w in writes:
                evs = events.get(w[0])
                if evs:
                    pr = (w[1], w[2]); br = (w[3], w[4])
                    for ev in evs:
                        if inter((ev[0], ev[1]), pr) and inter((ev[2], ev[3]), br):
                            dep(ev)
            for r in reads:
                evs = events.setdefault(r[0], [])
                evs[:] = [ev for ev in evs
                          if ev[6] or ev[4] != eng or
                          (ev[0], ev[1], ev[2], ev[3]) != (r[1], r[2], r[3], r[4])]
                evs.append([r[1], r[2], r[3], r[4], eng, myseq, False, i])
            for w in writes:
                evs = events.setdefault(w[0], [])
                pr = (w[1], w[2]); br = (w[3], w[4])
                evs[:] = [ev for ev in evs
                          if not (ev[0] >= w[1] and ev[1] <= w[2] and ev[2] >= w[3] and ev[3] <= w[4])]
                evs.append([w[1], w[2], w[3], w[4], eng, myseq, True, i])

        need_inc = [True] * n
        idx_of = {}
        for i in range(n):
            idx_of[eng_seq[i]] = i
        for i in range(n):
            for peng, pseq in deps[i].items():
                need_inc[idx_of[(peng, pseq)]] = True
        last_sp = None
        for i in range(n):
            if eng_seq[i][0] == "sp":
                last_sp = i
        if last_sp is not None:
            need_inc[last_sp] = True

        cum = {e: [] for e in ENGS}
        cnt = {e: 0 for e in ENGS}
        for i in range(n):
            eng, myseq = eng_seq[i]
            if need_inc[i]:
                cnt[eng] += 1
            cum[eng].append(cnt[eng])

        waits = [[] for _ in range(n)]
        waited = {e: {p: 0 for p in ENGS} for e in ENGS}
        if self.SERIALIZE:
            done = {e: 0 for e in ENGS}
            for i in range(n):
                eng = eng_seq[i][0]
                for peng in ENGS:
                    if peng == eng:
                        continue
                    val = done[peng] * INC[peng]
                    if val > 0 and waited[eng][peng] < val:
                        waited[eng][peng] = val
                        waits[i].append((peng, val))
                if need_inc[i]:
                    done[eng] += 1
        else:
            for i in range(n):
                eng = eng_seq[i][0]
                for peng, pseq in sorted(deps[i].items()):
                    val = cum[peng][pseq] * INC[peng]
                    assert cum[peng][pseq] > 0
                    if waited[eng][peng] < val:
                        waited[eng][peng] = val
                        waits[i].append((peng, val))

        engine_ops = {e: [] for e in ENGS}
        for i in range(n):
            engine_ops[eng_seq[i][0]].append(i)

        total_inc = {e: cnt[e] * INC[e] for e in ENGS}

        NSLOT = Builder.NSLOT

        def sem_of(eng, seq):
            if eng in ("sp", "pool"):
                return (eng, seq % NSLOT)
            return (eng, 0)

        def val_of(eng, seq):
            if eng in ("sp", "pool"):
                return 16 * (seq // NSLOT + 1)
            return cum[eng][seq] * INC[eng]

        waits = [[] for _ in range(n)]
        waited2 = {}
        if self.SERIALIZE:
            last = {}
            for i in range(n):
                eng, myseq = eng_seq[i]
                for key, val in list(last.items()):
                    if key[0] == eng and key[0] not in ("sp", "pool"):
                        continue
                    if sem_of(*eng_seq[i]) == key:
                        continue
                    wk = (eng, key)
                    if waited2.get(wk, 0) < val:
                        waited2[wk] = val
                        waits[i].append((key, val))
                last[sem_of(eng, myseq)] = val_of(eng, myseq)
        else:
            for i in range(n):
                eng = eng_seq[i][0]
                for peng, pseq in sorted(deps[i].items()):
                    key = sem_of(peng, pseq)
                    val = val_of(peng, pseq)
                    wk = (eng, key)
                    if waited2.get(wk, 0) < val:
                        waited2[wk] = val
                        waits[i].append((key, val))

        gate = [None] * n
        for i in range(n):
            eng, myseq = eng_seq[i]
            if eng in ("sp", "pool") and myseq >= NSLOT:
                gate[i] = (sem_of(eng, myseq), 16 * (myseq // NSLOT))

        for eng in ENGS:
            ops_list = engine_ops[eng]
            if not ops_list and eng != "sp":
                continue

            def make_body(eng, ops_list):
                def body(e):
                    for i in ops_list:
                        if gate[i] is not None:
                            key, val = gate[i]
                            e.wait_ge(sems[key], val)
                        for (key, val) in waits[i]:
                            e.wait_ge(sems[key], val)
                        inst = self.ops[i][1](e)
                        myeng, myseq = eng_seq[i]
                        if need_inc[i] or myeng in ("sp", "pool"):
                            inst.then_inc(sems[sem_of(myeng, myseq)], INC[myeng])
                    if eng in ("sp", "pool"):
                        nops = len(ops_list)
                        for sl in range(min(NSLOT, nops)):
                            cnt_s = (nops - sl + NSLOT - 1) // NSLOT
                            e.wait_ge(sems[(eng, sl)], 16 * cnt_s)
                return body

            getattr(block, BLOCKNAME[eng])(make_body(eng, ops_list))


def build_program(dbg=(), phases=tuple(range(1, 14))):
    nc = bass.Bass("TRN2", target_bir_lowering=False)
    dt4, dt2 = 4, 2

    def din(name, shape, dt=F32):
        return nc.dram_tensor(name, shape, dt, kind="ExternalInput")

    xT = din("xT", [BLOC * D, L])
    wqkvT = din("wqkvT", [D, 3 * D])
    woT = din("woT", [D, D])
    u1 = din("u1", [D, RK]); v1 = din("v1", [RK, DFF])
    u2 = din("u2", [DFF, RK]); v2 = din("v2", [RK, D])
    wvgsT = din("wvgsT", [D, Hn])
    b1c_in = din("b1c", [128, 32]); b2c_in = din("b2c", [128, 8])
    wcols_in = din("wcols", [128, 512])
    wltmT = din("wltmT", [Hn, DK, DK])
    sel_in = din("selc", [32, 128, 64], BF16)
    blk_in = din("blkc", [8, 128, 16])
    bc16_in = din("bc16c", [8, 16, 128]); bc16n_in = din("bc16nc", [8, 16, 128])
    onesdiv_in = din("onesdiv", [128, 1])
    bc1_in = din("bc1", [1, 128]); bc1n_in = din("bc1n", [1, 128])
    ones16_in = din("ones16", [1, 16])
    bt_in = din("betatau", [1, 16]); cc_in = din("cch", [1, 16])
    gm_in = din("gammav", [1, 16]); bg_in = din("betagm", [1, 16])
    al_in = din("alphap", [1, 1]); ic_in = din("icth", [1, 1])
    vgsb_in = din("vgsbase", [16]); vt0_in = din("vt0", [16])

    outT = nc.dram_tensor("outT", [BLOC * D, L], F32, kind="ExternalOutput")

    def scr(name, shape, dt=F32):
        kind = "ExternalOutput" if name in dbg else "Internal"
        return nc.dram_tensor(name, shape, dt, kind=kind)

    x1s = scr("x1s", [BLOC * D, L]); x2s = scr("x2s", [BLOC * D, L])
    qs = scr("qs", [BLOC * D, L]); ks = scr("ks", [BLOC * D, L]); vs = scr("vs", [BLOC * D, L])
    ss = scr("ss", [BLOC * Hn, L])
    ys = scr("ys", [BLOC * D, L], BF16)
    xlns = scr("xlns", [BLOC * D, L], BF16)
    p1s = scr("p1s", [BLOC * RK, L], BF16)
    q2s = scr("q2s", [BLOC * RK, L], BF16)
    hs = scr("hs", [BLOC * DFF, L], BF16)
    wb_qkv = scr("wb_qkv", [D, 3 * D], BF16)
    wb_wo = scr("wb_wo", [D, D], BF16)
    wb_u1 = scr("wb_u1", [D, RK], BF16)
    wb_v1 = scr("wb_v1", [RK, DFF], BF16)
    wb_u2h = scr("wb_u2h", [DFF, RK], BF16)
    wb_v2 = scr("wb_v2", [RK, D], BF16)
    wb_vgs = scr("wb_vgs", [D, Hn], BF16)
    cbounce = scr("cbounce", [4, 16])

    bd = Builder()

    with ExitStack() as ctx:
        ARF_h = ctx.enter_context(nc.sbuf_tensor("ARF", [128, 38912], F32))
        ARB_h = ctx.enter_context(nc.sbuf_tensor("ARB", [128, 27648], BF16))
        PS_h = ctx.enter_context(nc.psum_tensor("PS", [128, 4096], F32))
        sems = {}
        for e in ("pe", "act", "dve"):
            sems[(e, 0)] = ctx.enter_context(nc.semaphore("sem_" + e))
        for e in ("sp", "pool"):
            for sl in range(Builder.NSLOT):
                sems[(e, sl)] = ctx.enter_context(nc.semaphore(f"sem_{e}{sl}"))
        block = ctx.enter_context(nc.Block())

        def F(p0, np_, c0, nc_):
            return View(ARF_h, p0, np_, c0, nc_, 4)

        def Bv(p0, np_, c0, nc_):
            return View(ARB_h, p0, np_, c0, nc_, 2)

        def P(p0, np_, c0, nc_):
            return View(PS_h, p0, np_, c0, nc_, 4)

        v_sbc = F(0, 128, 0, 4096)
        v_krep = F(0, 128, 4096, 4096)
        v_qrep = F(0, 128, 8192, 4096)
        v_vh = F(64, 64, 12288, 4096)
        v_bc16 = F(0, 16, 12288, 1024)
        v_bc16n = F(0, 16, 13312, 1024)
        v_blk = F(0, 128, 14336, 128)
        CH = 16384
        v_sp = [F(0, 128, CH, T2), F(0, 128, CH + T2, T2)]
        v_tsig = F(0, 128, CH + 2 * T2, T2)
        v_sgn = F(0, 128, CH + 3 * T2, T2)
        v_R = [F(0, 128, CH + 4 * T2, T2), F(0, 128, CH + 5 * T2, T2)]
        v_evr = F(0, 128, CH + 6 * T2, T2)
        v_g1 = F(0, 128, CH + 7 * T2, T2)
        v_cc = F(0, 128, CH + 8 * T2, T2)
        v_ff = F(0, 128, CH + 9 * T2, T2)
        v_carry = F(0, 128, 36864, 8)
        PC = 36880
        v_wcols = F(0, 128, PC, 512)
        v_w05 = F(0, 128, PC + 512, 512)
        v_cexp = F(0, 128, PC + 1024, 16)
        v_tsc = F(0, 128, PC + 1040, 16)
        v_tsb = F(0, 128, PC + 1056, 16)
        v_b1c = F(0, 128, PC + 1072, 32)
        v_b2hc = F(0, 128, PC + 1104, 8)
        v_onesdiv = F(0, 128, PC + 1112, 1)
        v_bc1 = F(0, 1, PC + 1120, 128)
        v_bc1n = F(0, 1, PC + 1248, 128)
        v_vpreK = F(0, 128, PC + 1376, 8)
        v_sb16 = F(0, 16, PC + 1384, 1)
        v_wtmp = F(0, 128, PC + 1392, 16)
        v_row1 = F(0, 1, PC + 1408, 16)
        v_row2 = F(0, 1, PC + 1424, 16)
        v_row3 = F(0, 1, PC + 1440, 16)
        st_x = F(0, 128, 0, 4096)
        st_x2 = F(0, 128, 4096, 4096)
        st_ub = F(0, 128, 34816, 512)
        st_u2b = F(0, 128, 35328, 512)
        st_t1b = F(0, 128, 35840, 512)
        st_u = F(0, 128, 24576, 512)
        st_u2 = F(0, 128, 25088, 512)
        st_p1g = F(0, 128, 25600, 512)
        st_p2 = F(0, 128, 26112, 512)
        st_t1 = F(0, 128, 26624, 512)
        st_t2 = F(0, 128, 27136, 512)
        r_mrow = F(0, 1, 27648, 512)
        r_vrow = F(0, 1, 28160, 512)
        r_msq = F(0, 1, 28672, 512)
        r_varr = F(0, 1, 29184, 512)
        r_lnv = F(0, 1, 29696, 512)
        r_rrow = F(0, 1, 30208, 512)
        r_nmrow = F(0, 1, 30720, 512)
        k_m16 = F(0, 16, 31232, 512)
        k_v16 = F(0, 16, 31744, 512)
        k_msq = F(0, 16, 32256, 512)
        k_var = F(0, 16, 32768, 512)
        k_lnv = F(0, 16, 33280, 512)
        k_r16 = F(0, 16, 33792, 512)
        k_nm16 = F(0, 16, 34304, 512)
        st_kraw = F(0, 128, CH, 4096)
        st_sq = F(0, 128, CH + 4096, 4096)
        st_wf = F(0, 128, CH + 8192, 3072)

        b_m = Bv(0, 128, 0, T2)
        b_sel = Bv(0, 128, 2048, 2048)
        b_wq = Bv(0, 64, 4096, 1024)
        b_qbf = Bv(0, 64, 5120, 4096)
        b_ybf = Bv(0, 64, 9216, 4096)
        b_xmov = Bv(0, 128, 13312, 4096)
        b_xmov2 = Bv(0, 128, 5120, 4096)
        b_work2 = Bv(0, 128, 17408, 2048)
        b_slab = Bv(0, 128, 19456, 8192)
        b_wcast = Bv(0, 128, 13312, 3072)

        ps_bank = [P(0, 128, 512 * i, 512) for i in range(8)]
        y_ps = [P(0, 64, 0, 2048), P(0, 64, 2048, 2048)]

        op = bd.op

        def dma(dst_ap, src_ap, reads, writes, eng="sp"):
            op(eng, reads, writes, lambda e, d=dst_ap, s=src_ap: e.dma_start(out=d, in_=s))

        rb = {
            id(xT): L * 4, id(outT): L * 4, id(x1s): L * 4, id(x2s): L * 4,
            id(qs): L * 4, id(ks): L * 4, id(vs): L * 4, id(ss): L * 4,
            id(ys): L * 2, id(xlns): L * 2, id(p1s): L * 2, id(q2s): L * 2, id(hs): L * 2,
        }

        def drg(h, r0, r1):
            return dreg(h, r0, r1, rb[id(h)])

        def wreg(h):
            return (id(h), 0, 1, 0, 1 << 40)

        v_eps16 = F(0, 16, PC + 1600, 1)
        op("dve", [], [v_eps16],
           lambda e: e.memset(v_eps16.ap(), EPS))
        dma(v_onesdiv.ap(), onesdiv_in[:, :], [wreg(onesdiv_in)], [v_onesdiv])
        dma(v_bc1.ap(), bc1_in[:, :], [wreg(bc1_in)], [v_bc1])
        dma(v_bc1n.ap(), bc1n_in[:, :], [wreg(bc1n_in)], [v_bc1n])
        dma(v_b1c.ap(), b1c_in[:, :], [wreg(b1c_in)], [v_b1c])
        dma(v_wcols.ap(), wcols_in[:, :], [wreg(wcols_in)], [v_wcols])
        op("dve", [v_wcols], [v_w05],
           lambda e: e.tensor_scalar(out=v_w05.ap(), in0=v_wcols.ap(), scalar1=0.5,
                                     scalar2=None, op0=ALU.mult))
        tmpb2 = F(0, 128, PC + 1456, 8)
        dma(tmpb2.ap(), b2c_in[:, :], [wreg(b2c_in)], [tmpb2])
        op("dve", [tmpb2], [v_b2hc],
           lambda e: e.tensor_scalar(out=v_b2hc.ap(), in0=tmpb2.ap(), scalar1=0.5,
                                     scalar2=None, op0=ALU.mult))
        src = bass.AP(bc16_in, 0, [[128, 16], [2048, 8], [1, 128]])
        dma(v_bc16.ap(), src, [wreg(bc16_in)], [v_bc16])
        src = bass.AP(bc16n_in, 0, [[128, 16], [2048, 8], [1, 128]])
        dma(v_bc16n.ap(), src, [wreg(bc16n_in)], [v_bc16n])
        src = bass.AP(blk_in, 0, [[16, 128], [2048, 8], [1, 16]])
        dma(v_blk.ap(), src, [wreg(blk_in)], [v_blk])
        src = bass.AP(sel_in, 0, [[64, 128], [8192, 32], [1, 64]])
        dma(b_sel.ap(), src, [wreg(sel_in)], [b_sel])
        wq_f = F(0, 64, CH, 1024)
        src = bass.AP(wltmT, 0, [[64, 64], [4096, 16], [1, 64]])
        dma(wq_f.ap(), src, [wreg(wltmT)], [wq_f])
        op("dve", [wq_f], [b_wq],
           lambda e: e.tensor_copy(out=b_wq.ap(), in_=wq_f.ap()))
        rbt = F(0, 1, PC + 1472, 16); rcc = F(0, 1, PC + 1488, 16)
        rgm = F(0, 1, PC + 1504, 16); rbg = F(0, 1, PC + 1520, 16)
        ral = F(0, 1, PC + 1536, 1); ric = F(0, 1, PC + 1537, 1)
        r16o = F(0, 1, PC + 1538, 16)
        rt1 = F(0, 1, PC + 1554, 16); rt2 = F(0, 1, PC + 1570, 16)
        dma(rbt.ap(), bt_in[:, :], [wreg(bt_in)], [rbt])
        dma(rcc.ap(), cc_in[:, :], [wreg(cc_in)], [rcc])
        dma(rgm.ap(), gm_in[:, :], [wreg(gm_in)], [rgm])
        dma(rbg.ap(), bg_in[:, :], [wreg(bg_in)], [rbg])
        dma(ral.ap(), al_in[:, :], [wreg(al_in)], [ral])
        dma(ric.ap(), ic_in[:, :], [wreg(ic_in)], [ric])
        dma(r16o.ap(), ones16_in[:, :], [wreg(ones16_in)], [r16o])
        op("dve", [rcc], [rt1], lambda e: e.reciprocal(out=rt1.sub(0, 1, 0, 16).ap(), in_=rcc.ap()))
        op("dve", [rbt, rt1], [rt2],
           lambda e: e.tensor_tensor(out=rt2.ap(), in0=rbt.ap(), in1=rt1.ap(), op=ALU.mult))
        op("dve", [rt2], [rt2],
           lambda e: e.tensor_scalar(out=rt2.ap(), in0=rt2.ap(), scalar1=-1.0,
                                     scalar2=None, op0=ALU.mult))
        dma(cbounce[0:1, :], rt2.ap(), [rt2], [wreg(cbounce)])
        op("dve", [rbt, ral], [rt1],
           lambda e: e.tensor_scalar(out=rt1.ap(), in0=rbt.ap(), scalar1=ral.ap(),
                                     scalar2=None, op0=ALU.mult))
        dma(cbounce[1:2, :], rt1.ap(), [rt1, wreg(cbounce)], [wreg(cbounce)])
        op("dve", [r16o, ral, ric], [rt1],
           lambda e: e.tensor_scalar(out=rt1.ap(), in0=r16o.ap(), scalar1=ral.ap(),
                                     scalar2=ric.ap(), op0=ALU.mult, op1=ALU.mult))
        op("dve", [rt1], [rt1],
           lambda e: e.tensor_scalar(out=rt1.ap(), in0=rt1.ap(), scalar1=-1.0,
                                     scalar2=None, op0=ALU.mult))
        dma(cbounce[2:3, :], rt1.ap(), [rt1, wreg(cbounce)], [wreg(cbounce)])
        op("dve", [rgm, rbg], [rt1],
           lambda e: e.tensor_tensor(out=rt1.ap(), in0=rgm.ap(), in1=rbg.ap(), op=ALU.mult))
        op("dve", [rt1], [rt1],
           lambda e: e.tensor_scalar(out=rt1.ap(), in0=rt1.ap(), scalar1=0.5,
                                     scalar2=None, op0=ALU.mult))
        dma(cbounce[3:4, :], rt1.ap(), [rt1, wreg(cbounce)], [wreg(cbounce)])
        dma(v_cexp.ap(), cbounce[0:1, :].partition_broadcast(128), [wreg(cbounce)], [v_cexp])
        dma(v_tsc.ap(), cbounce[1:2, :].partition_broadcast(128), [wreg(cbounce)], [v_tsc])
        dma(v_tsb.ap(), cbounce[2:3, :].partition_broadcast(128), [wreg(cbounce)], [v_tsb])
        for dd in range(2):
            dst_ap = v_vpreK.sub(64 * dd, 64, 0, 8).ap()
            src_ap = bass.AP(cbounce, 3 * 16 + dd, [[0, 64], [2, 8]])
            op("sp", [wreg(cbounce)], [v_vpreK],
               lambda e, d=dst_ap, s=src_ap: e.dma_start(
                   out=d, in_=s, allow_slow_non_contiguous=True))
        sb_a = F(0, 16, PC + 1586, 1); sb_b = F(0, 16, PC + 1587, 1)
        dma(sb_a.ap(), bass.AP(vgsb_in, 0, [[1, 16], [1, 1]]), [wreg(vgsb_in)], [sb_a])
        dma(sb_b.ap(), bass.AP(vt0_in, 0, [[1, 16], [1, 1]]), [wreg(vt0_in)], [sb_b])
        op("dve", [sb_a, sb_b], [v_sb16],
           lambda e: e.tensor_tensor(out=v_sb16.ap(), in0=sb_a.ap(), in1=sb_b.ap(),
                                     op=ALU.subtract))

        def wcast(src_h, dst_h, rows, cols, scale=None):
            for r0 in range(0, rows, 128):
                for c0 in range(0, cols, 3072):
                    cw = min(3072, cols - c0)
                    fsrc = st_wf.sub(0, 128, 0, cw)
                    bdst = b_wcast.sub(0, 128, 0, cw)
                    dma(fsrc.ap(), src_h[r0:r0 + 128, c0:c0 + cw], [wreg(src_h)], [fsrc])
                    if scale is None:
                        op("dve", [fsrc], [bdst],
                           lambda e, a=bdst.ap(), b_=fsrc.ap(): e.tensor_copy(out=a, in_=b_))
                    else:
                        op("dve", [fsrc], [bdst],
                           lambda e, a=bdst.ap(), b_=fsrc.ap(), s=scale: e.tensor_scalar(
                               out=a, in0=b_, scalar1=s, scalar2=None, op0=ALU.mult))
                    dma(dst_h[r0:r0 + 128, c0:c0 + cw], bdst.ap(), [bdst], [wreg(dst_h)])

        wcast(wqkvT, wb_qkv, D, 3 * D)
        wcast(woT, wb_wo, D, D)
        wcast(u1, wb_u1, D, RK)
        wcast(v1, wb_v1, RK, DFF)
        wcast(u2, wb_u2h, DFF, RK, scale=0.25)
        wcast(v2, wb_v2, RK, D)
        wcast(wvgsT, wb_vgs, D, Hn)

        def load_slab(src_h, col0, nk, nm, src_cols):
            for kc in range(nk):
                dstv = b_slab.sub(0, 128, kc * nm * 128, nm * 128)
                dma(dstv.ap(), src_h[kc * 128:(kc + 1) * 128, col0:col0 + nm * 128],
                    [wreg(src_h)], [dstv])

        def slab_tile(kc, ml, nm):
            return b_slab.sub(0, 128, (kc * nm + ml) * 128, 128)

        def mm(psv, lhv, rhv, start, stop, extra_reads=()):
            op("pe", [lhv, rhv] + list(extra_reads), [psv],
               lambda e, o=psv.ap(), l=lhv.ap(), r=rhv.ap(), st=start, sp=stop:
               e.matmul(out=o, lhsT=l, rhs=r, start=st, stop=sp))

        def act(outv, inv, func, bias=0.0, scale=1.0, extra_reads=()):
            br = [bias] if isinstance(bias, View) else []
            sr = [scale] if isinstance(scale, View) else []
            bap = bias.ap() if isinstance(bias, View) else bias
            sap = scale.ap() if isinstance(scale, View) else scale
            if func == AF.Copy and br:
                assert not sr and scale == 1.0
                op("act", [inv] + br + list(extra_reads), [outv],
                   lambda e, o=outv.ap(), i=inv.ap(), b_=bap:
                   e.add(out=o, in_=i, add=b_))
            else:
                op("act", [inv] + br + sr + list(extra_reads), [outv],
                   lambda e, o=outv.ap(), i=inv.ap(), f=func, b_=bap, s=sap:
                   e.activation(out=o, in_=i, func=f, bias=b_, scale=s))

        def tt(outv, av, bv, alu):
            op("dve", [av, bv], [outv],
               lambda e, o=outv.ap(), a=av.ap(), b_=bv.ap(), u=alu:
               e.tensor_tensor(out=o, in0=a, in1=b_, op=u))

        def ts(outv, av, s1, s2, op0, op1=None):
            s1r = [s1] if isinstance(s1, View) else []
            s1ap = s1.ap() if isinstance(s1, View) else s1
            if op1 is None:
                op("dve", [av] + s1r, [outv],
                   lambda e, o=outv.ap(), a=av.ap(), s=s1ap, u=op0:
                   e.tensor_scalar(out=o, in0=a, scalar1=s, scalar2=None, op0=u))
            else:
                op("dve", [av] + s1r, [outv],
                   lambda e, o=outv.ap(), a=av.ap(), s=s1ap, s2_=s2, u0=op0, u1=op1:
                   e.tensor_scalar(out=o, in0=a, scalar1=s, scalar2=s2_, op0=u0, op1=u1))

        def stt(outv, av, sc, bv, op0, op1):
            op("dve", [av, bv], [outv],
               lambda e, o=outv.ap(), a=av.ap(), s=sc, b_=bv.ap(), u0=op0, u1=op1:
               e.scalar_tensor_tensor(out=o, in0=a, scalar=s, in1=b_, op0=u0, op1=u1))

        def layernorm(xtiles, sqv, xlnouts):
            whole = xtiles[0].sub(0, 128, 0, 4096)
            act(sqv, whole, AF.Square)
            for kc in range(8):
                mm(ps_bank[6].sub(0, 1, 0, 512), v_onesdiv, xtiles[kc], kc == 0, kc == 7)
            for kc in range(8):
                mm(ps_bank[7].sub(0, 1, 0, 512), v_onesdiv, sqv.sub(0, 128, kc * 512, 512),
                   kc == 0, kc == 7)
            act(r_mrow, ps_bank[6].sub(0, 1, 0, 512), AF.Copy)
            act(r_vrow, ps_bank[7].sub(0, 1, 0, 512), AF.Copy)
            act(r_msq, r_mrow, AF.Square)
            tt(r_varr, r_vrow, r_msq, ALU.subtract)
            act(r_lnv, r_varr, AF.Ln, bias=v_eps16.sub(0, 1, 0, 1))
            act(r_rrow, r_lnv, AF.Exp, scale=-0.5)
            tt(r_nmrow, r_mrow, r_rrow, ALU.mult)
            mm(ps_bank[4], v_bc1, r_rrow, True, True)
            mm(ps_bank[5], v_bc1n, r_nmrow, True, True)
            for kc in range(8):
                tt(st_t1, xtiles[kc], ps_bank[4], ALU.mult)
                tt(xlnouts[kc], st_t1, ps_bank[5], ALU.add)

        def ph_u1(with_ln):
            load_slab(wb_u1, 0, 8, 2, RK)
            for b in range(BLOC):
                for tt_i in range(L // TOKT):
                    t0 = tt_i * TOKT
                    mvb = b_xmov if (b * 8 + tt_i) % 2 == 0 else b_xmov2
                    xm = [mvb.sub(0, 128, kc * 512, 512) for kc in range(8)]
                    if with_ln:
                        xts = [st_x.sub(0, 128, kc * 512, 512) for kc in range(8)]
                        for kc in range(8):
                            dma(xts[kc].ap(), xT[b * D + kc * 128:b * D + (kc + 1) * 128,
                                                 t0:t0 + TOKT],
                                [wreg(xT)], [xts[kc]])
                        layernorm(xts, st_sq, xm)
                    else:
                        for kc in range(8):
                            dma(xm[kc].ap(), xlns[b * D + kc * 128:b * D + (kc + 1) * 128,
                                                  t0:t0 + TOKT],
                                [drg(xlns, b * D + kc * 128, b * D + (kc + 1) * 128)], [xm[kc]])
                    for m2 in range(2):
                        psv = ps_bank[m2]
                        for kc in range(8):
                            mm(psv, slab_tile(kc, m2, 2), xm[kc], kc == 0, kc == 7)
                        ov = b_work2.sub(0, 128, m2 * 512, 512)
                        op("dve", [psv], [ov],
                           lambda e, o=ov.ap(), i=psv.ap(): e.tensor_copy(out=o, in_=i))
                        dma(p1s[b * RK + m2 * 128:b * RK + (m2 + 1) * 128, t0:t0 + TOKT],
                            ov.ap(), [ov], [drg(p1s, b * RK + m2 * 128, b * RK + (m2 + 1) * 128)])

        def ph_v1gelu():
            load_slab(wb_v1, 0, 2, 32, DFF)
            for b in range(BLOC):
                for tt_i in range(L // TOKT):
                    t0 = tt_i * TOKT
                    mvb = b_xmov if (b * 8 + tt_i) % 2 == 0 else b_xmov2
                    pm = [mvb.sub(0, 128, r2 * 512, 512) for r2 in range(2)]
                    for r2 in range(2):
                        dma(pm[r2].ap(), p1s[b * RK + r2 * 128:b * RK + (r2 + 1) * 128,
                                             t0:t0 + TOKT],
                            [drg(p1s, b * RK + r2 * 128, b * RK + (r2 + 1) * 128)], [pm[r2]])
                    for fc in range(32):
                        psv = ps_bank[fc % 2]
                        uu = st_u if fc % 2 == 0 else st_ub
                        uu2 = st_u2 if fc % 2 == 0 else st_u2b
                        pg = st_p1g if fc % 2 == 0 else st_t1b
                        pp2 = st_p2 if fc % 2 == 0 else st_t2
                        for r2 in range(2):
                            mm(psv, slab_tile(r2, fc, 32), pm[r2], r2 == 0, r2 == 1)
                        act(uu, psv, AF.Copy, bias=v_b1c.sub(0, 128, fc, 1))
                        act(uu2, uu, AF.Square)
                        ts(pg, uu2, GELU_A, 1.0, ALU.mult, ALU.add)
                        tt(pp2, pg, uu, ALU.mult)
                        act(uu2, pp2, AF.Tanh, scale=GELU_C)
                        hv = b_work2.sub(0, 128, (fc % 4) * 512, 512)
                        stt(hv, uu2, 1.0, uu, ALU.add, ALU.mult)
                        dma(hs[b * DFF + fc * 128:b * DFF + (fc + 1) * 128, t0:t0 + TOKT],
                            hv.ap(), [hv], [drg(hs, b * DFF + fc * 128, b * DFF + (fc + 1) * 128)])

        def ph_u2():
            for kc in range(32):
                dstv = b_slab.sub(0, 128, kc * 256, 256)
                dma(dstv.ap(), wb_u2h[kc * 128:(kc + 1) * 128, :], [wreg(wb_u2h)], [dstv])
            for b in range(BLOC):
                for tt_i in range(L // TOKT):
                    t0 = tt_i * TOKT
                    for fg in range(4):
                        mvb = b_xmov if fg % 2 == 0 else b_xmov2
                        hm = [mvb.sub(0, 128, i * 512, 512) for i in range(8)]
                        for i in range(8):
                            fc = fg * 8 + i
                            dma(hm[i].ap(), hs[b * DFF + fc * 128:b * DFF + (fc + 1) * 128,
                                               t0:t0 + TOKT],
                                [drg(hs, b * DFF + fc * 128, b * DFF + (fc + 1) * 128)], [hm[i]])
                        for i in range(8):
                            fc = fg * 8 + i
                            for r2 in range(2):
                                mm(ps_bank[2 + r2], b_slab.sub(0, 128, fc * 256 + r2 * 128, 128),
                                   hm[i], fc == 0, fc == 31)
                    for r2 in range(2):
                        ov = b_work2.sub(0, 128, r2 * 512, 512)
                        op("dve", [ps_bank[2 + r2]], [ov],
                           lambda e, o=ov.ap(), i=ps_bank[2 + r2].ap(): e.tensor_copy(out=o, in_=i))
                        dma(q2s[b * RK + r2 * 128:b * RK + (r2 + 1) * 128, t0:t0 + TOKT],
                            ov.ap(), [ov], [drg(q2s, b * RK + r2 * 128, b * RK + (r2 + 1) * 128)])

        def ph_v2res(res_h, out_h, with_ln):
            load_slab(wb_v2, 0, 2, 8, D)
            for b in range(BLOC):
                for tt_i in range(L // TOKT):
                    t0 = tt_i * TOKT
                    mvb = b_xmov if (b * 8 + tt_i) % 2 == 0 else b_xmov2
                    qm = [mvb.sub(0, 128, r2 * 512, 512) for r2 in range(2)]
                    for r2 in range(2):
                        dma(qm[r2].ap(), q2s[b * RK + r2 * 128:b * RK + (r2 + 1) * 128,
                                             t0:t0 + TOKT],
                            [drg(q2s, b * RK + r2 * 128, b * RK + (r2 + 1) * 128)], [qm[r2]])
                    outs = [st_x.sub(0, 128, mc * 512, 512) for mc in range(8)]
                    for mc in range(8):
                        psv = ps_bank[mc % 2]
                        uu = st_u if mc % 2 == 0 else st_ub
                        rv = st_u2 if mc % 2 == 0 else st_u2b
                        for r2 in range(2):
                            mm(psv, slab_tile(r2, mc, 8), qm[r2], r2 == 0, r2 == 1)
                        act(uu, psv, AF.Copy, bias=v_b2hc.sub(0, 128, mc, 1))
                        dma(rv.ap(), res_h[b * D + mc * 128:b * D + (mc + 1) * 128, t0:t0 + TOKT],
                            [wreg(res_h) if id(res_h) not in rb else
                             drg(res_h, b * D + mc * 128, b * D + (mc + 1) * 128)], [rv])
                        tt(outs[mc], uu, rv, ALU.add)
                        dma(out_h[b * D + mc * 128:b * D + (mc + 1) * 128, t0:t0 + TOKT],
                            outs[mc].ap(), [outs[mc]],
                            [drg(out_h, b * D + mc * 128, b * D + (mc + 1) * 128)
                             if id(out_h) in rb else wreg(out_h)])
                    if with_ln:
                        xm = [b_xmov.sub(0, 128, 2048 + kc * 512, 512) for kc in range(4)] + \
                             [b_work2.sub(0, 128, kc * 512, 512) for kc in range(4)]
                        layernorm(outs, st_sq, xm)
                        for kc in range(8):
                            dma(xlns[b * D + kc * 128:b * D + (kc + 1) * 128, t0:t0 + TOKT],
                                xm[kc].ap(), [xm[kc]],
                                [drg(xlns, b * D + kc * 128, b * D + (kc + 1) * 128)])

        def ph_qkv(mg):
            load_slab(wb_qkv, mg * 1024, 8, 8, 3 * D)
            if mg == 2:
                for kc in range(8):
                    dstv = b_work2.sub(0, 128, kc * 16, 16)
                    dma(dstv.ap(), wb_vgs[kc * 128:(kc + 1) * 128, :], [wreg(wb_vgs)], [dstv])
            for b in range(BLOC):
                for tt_i in range(L // TOKT):
                    t0 = tt_i * TOKT
                    mvb = b_xmov if (b * 8 + tt_i) % 2 == 0 else b_xmov2
                    xm = [mvb.sub(0, 128, kc * 512, 512) for kc in range(8)]
                    for kc in range(8):
                        dma(xm[kc].ap(), xlns[b * D + kc * 128:b * D + (kc + 1) * 128,
                                              t0:t0 + TOKT],
                            [drg(xlns, b * D + kc * 128, b * D + (kc + 1) * 128)], [xm[kc]])
                    if mg == 0:
                        for mc in range(8):
                            psv = ps_bank[mc % 2]
                            uu = st_u if mc % 2 == 0 else st_ub
                            for kc in range(8):
                                mm(psv, slab_tile(kc, mc, 8), xm[kc], kc == 0, kc == 7)
                            act(uu, psv, AF.Copy)
                            dma(qs[b * D + mc * 128:b * D + (mc + 1) * 128, t0:t0 + TOKT],
                                uu.ap(), [uu],
                                [drg(qs, b * D + mc * 128, b * D + (mc + 1) * 128)])
                    elif mg == 1:
                        kraw = [st_kraw.sub(0, 128, mc * 512, 512) for mc in range(8)]
                        for mc in range(8):
                            psv = ps_bank[mc % 2]
                            for kc in range(8):
                                mm(psv, slab_tile(kc, mc, 8), xm[kc], kc == 0, kc == 7)
                            act(kraw[mc], psv, AF.Copy)
                        act(st_sq, st_kraw, AF.Square)
                        for jk in range(8):
                            mm(ps_bank[6].sub(0, 16, 0, 512),
                               v_blk.sub(0, 128, jk * 16, 16), kraw[jk], jk == 0, jk == 7)
                        for jk in range(8):
                            mm(ps_bank[7].sub(0, 16, 0, 512),
                               v_blk.sub(0, 128, jk * 16, 16),
                               st_sq.sub(0, 128, jk * 512, 512), jk == 0, jk == 7)
                        act(k_m16, ps_bank[6].sub(0, 16, 0, 512), AF.Copy)
                        act(k_v16, ps_bank[7].sub(0, 16, 0, 512), AF.Copy)
                        act(k_msq, k_m16, AF.Square)
                        tt(k_var, k_v16, k_msq, ALU.subtract)
                        act(k_lnv, k_var, AF.Ln, bias=v_eps16)
                        act(k_r16, k_lnv, AF.Exp, scale=-0.5)
                        tt(k_nm16, k_m16, k_r16, ALU.mult)
                        for jk in range(8):
                            pr = ps_bank[4 if jk % 2 == 0 else 6]
                            pn = ps_bank[5 if jk % 2 == 0 else 7]
                            mm(pr, v_bc16.sub(0, 16, jk * 128, 128), k_r16, True, True)
                            mm(pn, v_bc16n.sub(0, 16, jk * 128, 128), k_nm16, True, True)
                            klv = st_x.sub(0, 128, 0, 512)
                            tt(st_t1, kraw[jk], pr, ALU.mult)
                            tt(klv, st_t1, pn, ALU.add)
                            dma(ks[b * D + jk * 128:b * D + (jk + 1) * 128, t0:t0 + TOKT],
                                klv.ap(), [klv],
                                [drg(ks, b * D + jk * 128, b * D + (jk + 1) * 128)])
                    else:
                        for mc in range(8):
                            psv = ps_bank[mc % 2]
                            uu = st_u if mc % 2 == 0 else st_ub
                            for kc in range(8):
                                mm(psv, slab_tile(kc, mc, 8), xm[kc], kc == 0, kc == 7)
                            ts(uu, psv, v_vpreK.sub(0, 128, mc, 1), None, ALU.mult)
                            dma(vs[b * D + mc * 128:b * D + (mc + 1) * 128, t0:t0 + TOKT],
                                uu.ap(), [uu],
                                [drg(vs, b * D + mc * 128, b * D + (mc + 1) * 128)])
                        for kc in range(8):
                            mm(ps_bank[6].sub(0, 16, 0, 512),
                               b_work2.sub(0, 128, kc * 16, 16), xm[kc], kc == 0, kc == 7)
                        sv = st_x.sub(0, 16, 512, 512)
                        act(sv, ps_bank[6].sub(0, 16, 0, 512), AF.Copy, bias=v_sb16)
                        dma(ss[b * Hn:(b + 1) * Hn, t0:t0 + TOKT], sv.ap(), [sv],
                            [drg(ss, b * Hn, (b + 1) * Hn)])

        def ph_attn():
            for bh in range(BLOC * Hn):
                b, h = bh // Hn, bh % Hn
                r0 = b * D + h * DK
                dma(v_krep.sub(0, 64, 0, 4096).ap(), ks[r0:r0 + 64, :],
                    [drg(ks, r0, r0 + 64)], [v_krep])
                dma(v_krep.sub(64, 64, 0, 4096).ap(), ks[r0:r0 + 64, :],
                    [drg(ks, r0, r0 + 64)], [v_krep])
                dma(v_qrep.sub(0, 64, 0, 4096).ap(), qs[r0:r0 + 64, :],
                    [drg(qs, r0, r0 + 64)], [v_qrep])
                dma(v_qrep.sub(64, 64, 0, 4096).ap(), qs[r0:r0 + 64, :],
                    [drg(qs, r0, r0 + 64)], [v_qrep])
                dma(v_sbc.ap(), ss[b * Hn + h:b * Hn + h + 1, :].partition_broadcast(128),
                    [drg(ss, b * Hn + h, b * Hn + h + 1)], [v_sbc])
                op("dve", [v_qrep], [b_qbf],
                   lambda e, o=b_qbf.ap(), i=v_qrep.sub(0, 64, 0, 4096).ap():
                   e.tensor_copy(out=o, in_=i))
                cexp_h = v_cexp.sub(0, 128, h, 1)
                tsc_h = v_tsc.sub(0, 128, h, 1)
                tsb_h = v_tsb.sub(0, 128, h, 1)
                wq_h = b_wq.sub(0, 64, h * 64, 64)
                it = 0
                for c in range(32):
                    wcol = v_wcols.sub(0, 128, h * 32 + c, 1)
                    w05 = v_w05.sub(0, 128, h * 32 + c, 1)
                    for th in range(2):
                        par = it % 2
                        sbc_t = v_sbc.sub(0, 128, th * T2, T2)
                        spv = v_sp[par]
                        Rv = v_R[par]
                        act(v_evr, sbc_t, AF.Exp, bias=wcol)
                        act(spv, v_evr, AF.Ln, bias=1.0)
                        act(v_tsig, sbc_t, AF.Tanh, bias=w05, scale=0.5)
                        act(v_sgn, spv, AF.Tanh, bias=tsb_h, scale=tsc_h)
                        act(Rv, spv, AF.Exp, scale=cexp_h)
                        for dd in range(2):
                            vrow = r0 + 2 * c + dd
                            dma(v_evr.sub(64 * dd, 64, 0, T2).ap(),
                                vs[vrow:vrow + 1,
                                   th * T2:th * T2 + T2].partition_broadcast(64),
                                [drg(vs, vrow, vrow + 1)], [v_evr])
                        krep_t = v_krep.sub(0, 128, th * T2, T2)
                        qrep_t = v_qrep.sub(0, 128, th * T2, T2)
                        tt(v_g1, krep_t, v_evr, ALU.mult)
                        stt(v_cc, v_tsig, 1.0, v_sgn, ALU.add, ALU.mult)
                        tt(v_cc, v_cc, spv, ALU.mult)
                        tt(v_g1, v_cc, v_g1, ALU.mult)
                        if th == 0:
                            op("dve", [Rv, v_g1], [v_ff],
                               lambda e, o=v_ff.ap(), r=Rv.ap(), z=v_g1.ap():
                               e.tensor_tensor_scan(out=o, data0=r, data1=z, initial=0.0,
                                                    op0=ALU.mult, op1=ALU.add))
                            op("dve", [v_ff], [v_carry],
                               lambda e, o=v_carry.sub(0, 128, 0, 1).ap(),
                               i=v_ff.sub(0, 128, T2 - 1, 1).ap(): e.tensor_copy(out=o, in_=i))
                        else:
                            op("dve", [Rv, v_g1, v_carry], [v_ff],
                               lambda e, o=v_ff.ap(), r=Rv.ap(), z=v_g1.ap(),
                               ini=v_carry.sub(0, 128, 0, 1).ap():
                               e.tensor_tensor_scan(out=o, data0=r, data1=z, initial=ini,
                                                    op0=ALU.mult, op1=ALU.add))
                        tt(b_m, v_ff, qrep_t, ALU.mult)
                        if c == 0:
                            for s4 in range(4):
                                mm(y_ps[th].sub(0, 64, s4 * 512, 512), wq_h,
                                   b_qbf.sub(0, 64, th * T2 + s4 * 512, 512), True, False)
                        sel_c = b_sel.sub(0, 128, c * 64, 64)
                        for s4 in range(4):
                            mm(y_ps[th].sub(0, 64, s4 * 512, 512), sel_c,
                               b_m.sub(0, 128, s4 * 512, 512), False, c == 31)
                        if c == 31:
                            ybv = b_ybf.sub(0, 64, th * T2, T2)
                            act(ybv, y_ps[th], AF.Copy)
                        it += 1
                dma(ys[r0:r0 + 64, :], b_ybf.ap(), [b_ybf], [drg(ys, r0, r0 + 64)])

        def ph_wo():
            for kc in range(8):
                dstv = b_slab.sub(0, 128, kc * 1024, 1024)
                dma(dstv.ap(), wb_wo[kc * 128:(kc + 1) * 128, :], [wreg(wb_wo)], [dstv])
            for b in range(BLOC):
                for tt_i in range(L // TOKT):
                    t0 = tt_i * TOKT
                    mvb = b_xmov if (b * 8 + tt_i) % 2 == 0 else b_xmov2
                    ym = [mvb.sub(0, 128, kc * 512, 512) for kc in range(8)]
                    for kc in range(8):
                        dma(ym[kc].ap(), ys[b * D + kc * 128:b * D + (kc + 1) * 128, t0:t0 + TOKT],
                            [drg(ys, b * D + kc * 128, b * D + (kc + 1) * 128)], [ym[kc]])
                    outs = [st_x2.sub(0, 128, mc * 512, 512) for mc in range(8)]
                    for mc in range(8):
                        psv = ps_bank[mc % 2]
                        rv = st_u if mc % 2 == 0 else st_ub
                        for kc in range(8):
                            mm(psv, b_slab.sub(0, 128, kc * 1024 + mc * 128, 128), ym[kc],
                               kc == 0, kc == 7)
                        dma(rv.ap(), x1s[b * D + mc * 128:b * D + (mc + 1) * 128, t0:t0 + TOKT],
                            [drg(x1s, b * D + mc * 128, b * D + (mc + 1) * 128)], [rv])
                        tt(outs[mc], rv, psv, ALU.add)
                        dma(x2s[b * D + mc * 128:b * D + (mc + 1) * 128, t0:t0 + TOKT],
                            outs[mc].ap(), [outs[mc]],
                            [drg(x2s, b * D + mc * 128, b * D + (mc + 1) * 128)])
                    xm = [mvb.sub(0, 128, kc * 512, 512) for kc in range(8)]
                    layernorm(outs, st_sq, xm)
                    for kc in range(8):
                        dma(xlns[b * D + kc * 128:b * D + (kc + 1) * 128, t0:t0 + TOKT],
                            xm[kc].ap(), [xm[kc]],
                            [drg(xlns, b * D + kc * 128, b * D + (kc + 1) * 128)])

        phset = set(phases)
        if 1 in phset: ph_u1(with_ln=True)
        if 2 in phset: ph_v1gelu()
        if 3 in phset: ph_u2()
        if 4 in phset: ph_v2res(xT, x1s, True)
        if 5 in phset: ph_qkv(0)
        if 6 in phset: ph_qkv(1)
        if 7 in phset: ph_qkv(2)
        if 8 in phset: ph_attn()
        if 9 in phset: ph_wo()
        if 10 in phset: ph_u1(with_ln=False)
        if 11 in phset: ph_v1gelu()
        if 12 in phset: ph_u2()
        if 13 in phset: ph_v2res(x2s, outT, False)

        bd.emit(nc, block, sems)

    return nc


_host_cache = {}


def kernel(**inputs):
    x = np.asarray(inputs["L0_attn_x"], np.float32)
    W_qkv = np.asarray(inputs["L0_attn_W_qkv"], np.float32)
    W_o = np.asarray(inputs["L0_attn_W_o"], np.float32)
    W_vgs = np.asarray(inputs["L0_attn_W_vgs"], np.float32)
    W_LTM = np.asarray(inputs["L0_attn_W_LTM"], np.float32)
    V_gs_base = np.asarray(inputs["L0_attn_V_gs_base"], np.float32)
    V_T0 = np.asarray(inputs["L0_attn_V_T0"], np.float32)
    beta_tau = np.asarray(inputs["L0_attn_beta_tau"], np.float32)
    beta_gm = np.asarray(inputs["L0_attn_beta_gm"], np.float32)
    C_ch = np.asarray(inputs["L0_attn_C_ch"], np.float32)
    gamma = np.asarray(inputs["L0_attn_gamma"], np.float32)
    alpha = np.asarray(inputs["L0_attn_alpha_ppd"], np.float32)
    ic = np.asarray(inputs["L0_attn_IC_threshold"], np.float32)
    U1 = np.asarray(inputs["L0_ffn_U1"], np.float32)
    V1 = np.asarray(inputs["L0_ffn_V1"], np.float32)
    b1 = np.asarray(inputs["L0_ffn_b1"], np.float32)
    U2 = np.asarray(inputs["L0_ffn_U2"], np.float32)
    V2 = np.asarray(inputs["L0_ffn_V2"], np.float32)
    b2 = np.asarray(inputs["L0_ffn_b2"], np.float32)

    selc = np.zeros((32, 128, 64), ml_dtypes.bfloat16)
    for c in range(32):
        for p in range(128):
            selc[c, p, 2 * c + p // 64] = 1.0
    blkc = np.zeros((8, 128, 16), np.float32)
    bc16c = np.zeros((8, 16, 128), np.float32)
    for jk in range(8):
        for p in range(128):
            hh = 2 * jk + p // 64
            blkc[jk, p, hh] = 1.0 / 64.0
            bc16c[jk, hh, p] = 1.0
    bc16nc = -bc16c
    common = dict(
        wqkvT=np.ascontiguousarray(W_qkv.T),
        woT=np.ascontiguousarray(W_o.T),
        u1=U1, v1=V1, u2=U2, v2=V2,
        wvgsT=np.ascontiguousarray(W_vgs.T),
        b1c=np.ascontiguousarray(b1.reshape(32, 128).T),
        b2c=np.ascontiguousarray(b2.reshape(8, 128).T),
        wcols=np.ascontiguousarray(
            W_LTM.reshape(Hn, 32, 128).transpose(2, 0, 1).reshape(128, 512)),
        wltmT=np.ascontiguousarray(W_LTM.transpose(0, 2, 1)),
        selc=selc, blkc=blkc, bc16c=bc16c, bc16nc=bc16nc,
        onesdiv=np.full((128, 1), 1.0 / 1024.0, np.float32),
        bc1=np.ones((1, 128), np.float32),
        bc1n=-np.ones((1, 128), np.float32),
        ones16=np.ones((1, 16), np.float32),
        betatau=beta_tau.reshape(1, 16), cch=C_ch.reshape(1, 16),
        gammav=gamma.reshape(1, 16), betagm=beta_gm.reshape(1, 16),
        alphap=alpha.reshape(1, 1), icth=ic.reshape(1, 1),
        vgsbase=V_gs_base, vt0=V_T0,
    )
    in_maps = []
    for ci in range(NC_):
        xs = np.ascontiguousarray(
            x[ci * BLOC:(ci + 1) * BLOC].transpose(0, 2, 1).reshape(BLOC * D, L))
        m = dict(common)
        m["xT"] = xs
        in_maps.append(m)

    nc = build_program()
    res = run_bass_kernel_spmd(nc, in_maps, core_ids=list(range(NC_)))
    outs = []
    for ci in range(NC_):
        o = res.results[ci]["outT"].reshape(BLOC, D, L).transpose(0, 2, 1)
        outs.append(o)
    return np.ascontiguousarray(np.concatenate(outs, axis=0)).astype(np.float32)


# version 07
# speedup vs baseline: 1.1649x; 1.0618x over previous
import sys
sys.path.insert(0, "/opt/trn_rl_repo")
import numpy as np
import ml_dtypes
import concourse.bass as bass
import concourse.mybir as mybir
from concourse.bass_utils import run_bass_kernel_spmd
from contextlib import ExitStack

F32 = mybir.dt.float32
BF16 = mybir.dt.bfloat16
AF = mybir.ActivationFunctionType
ALU = mybir.AluOpType

B, L, D, Hn, DK = 16, 4096, 1024, 16, 64
DFF, RK = 4096, 256
NC_, BLOC = 8, 2
TOKT = 512
NTT = (BLOC * L) // TOKT
T2 = 2048
GELU_C = 0.7978845608028654
GELU_A = 0.044715
EPS = 1e-5

ENGS = ["pe", "act", "dve", "sp", "pool"]
BLOCKNAME = {"pe": "tensor", "act": "scalar", "dve": "vector", "sp": "sync", "pool": "gpsimd"}
INC = {"pe": 1, "act": 1, "dve": 1, "sp": 16, "pool": 16}


class View:
    __slots__ = ("h", "p0", "np_", "c0", "nc_", "esz")

    def __init__(self, h, p0, np_, c0, nc_, esz):
        self.h, self.p0, self.np_, self.c0, self.nc_, self.esz = h, p0, np_, c0, nc_, esz

    def ap(self):
        return self.h[self.p0:self.p0 + self.np_, self.c0:self.c0 + self.nc_]

    def reg(self):
        return (id(self.h), self.p0, self.p0 + self.np_,
                self.c0 * self.esz, (self.c0 + self.nc_) * self.esz)

    def sub(self, dp0, np_, dc0, nc_):
        return View(self.h, self.p0 + dp0, np_, self.c0 + dc0, nc_, self.esz)


def dreg(h, r0, r1, rowbytes):
    return (id(h), 0, 1, r0 * rowbytes, r1 * rowbytes)


class Builder:
    def __init__(self):
        self.ops = []

    def op(self, eng, reads, writes, emit):
        self.ops.append((eng, emit, tuple(r.reg() if hasattr(r, "reg") else r for r in reads),
                         tuple(w.reg() if hasattr(w, "reg") else w for w in writes)))

    SERIALIZE = False
    SELF_WAITS = True
    NSLOT = 8

    def emit(self, nc, block, sems):
        n = len(self.ops)
        eng_seq = [None] * n
        seqs = {e: 0 for e in ENGS}
        events = {}
        deps = [dict() for _ in range(n)]

        def inter(a, b):
            return a[1] > b[0] and b[1] > a[0]

        for i, (eng, emit, reads, writes) in enumerate(self.ops):
            myseq = seqs[eng]
            seqs[eng] += 1
            eng_seq[i] = (eng, myseq)
            d = deps[i]

            def dep(ev):
                peng, pseq = ev[4], ev[5]
                if peng == eng and not Builder.SELF_WAITS:
                    return
                if peng == eng and pseq == myseq:
                    return
                if d.get(peng, -1) < pseq:
                    d[peng] = pseq

            for r in reads:
                evs = events.get(r[0])
                if evs:
                    pr = (r[1], r[2]); br = (r[3], r[4])
                    for ev in evs:
                        if ev[6] and inter((ev[0], ev[1]), pr) and inter((ev[2], ev[3]), br):
                            dep(ev)
            for w in writes:
                evs = events.get(w[0])
                if evs:
                    pr = (w[1], w[2]); br = (w[3], w[4])
                    for ev in evs:
                        if inter((ev[0], ev[1]), pr) and inter((ev[2], ev[3]), br):
                            dep(ev)
            for r in reads:
                evs = events.setdefault(r[0], [])
                evs[:] = [ev for ev in evs
                          if ev[6] or ev[4] != eng or
                          (ev[0], ev[1], ev[2], ev[3]) != (r[1], r[2], r[3], r[4])]
                evs.append([r[1], r[2], r[3], r[4], eng, myseq, False, i])
            for w in writes:
                evs = events.setdefault(w[0], [])
                pr = (w[1], w[2]); br = (w[3], w[4])
                evs[:] = [ev for ev in evs
                          if not (ev[0] >= w[1] and ev[1] <= w[2] and ev[2] >= w[3] and ev[3] <= w[4])]
                evs.append([w[1], w[2], w[3], w[4], eng, myseq, True, i])

        need_inc = [True] * n
        idx_of = {}
        for i in range(n):
            idx_of[eng_seq[i]] = i
        for i in range(n):
            for peng, pseq in deps[i].items():
                need_inc[idx_of[(peng, pseq)]] = True
        last_sp = None
        for i in range(n):
            if eng_seq[i][0] == "sp":
                last_sp = i
        if last_sp is not None:
            need_inc[last_sp] = True

        cum = {e: [] for e in ENGS}
        cnt = {e: 0 for e in ENGS}
        for i in range(n):
            eng, myseq = eng_seq[i]
            if need_inc[i]:
                cnt[eng] += 1
            cum[eng].append(cnt[eng])

        waits = [[] for _ in range(n)]
        waited = {e: {p: 0 for p in ENGS} for e in ENGS}
        if self.SERIALIZE:
            done = {e: 0 for e in ENGS}
            for i in range(n):
                eng = eng_seq[i][0]
                for peng in ENGS:
                    if peng == eng:
                        continue
                    val = done[peng] * INC[peng]
                    if val > 0 and waited[eng][peng] < val:
                        waited[eng][peng] = val
                        waits[i].append((peng, val))
                if need_inc[i]:
                    done[eng] += 1
        else:
            for i in range(n):
                eng = eng_seq[i][0]
                for peng, pseq in sorted(deps[i].items()):
                    val = cum[peng][pseq] * INC[peng]
                    assert cum[peng][pseq] > 0
                    if waited[eng][peng] < val:
                        waited[eng][peng] = val
                        waits[i].append((peng, val))

        engine_ops = {e: [] for e in ENGS}
        for i in range(n):
            engine_ops[eng_seq[i][0]].append(i)

        total_inc = {e: cnt[e] * INC[e] for e in ENGS}

        NSLOT = Builder.NSLOT

        def sem_of(eng, seq):
            if eng in ("sp", "pool"):
                return (eng, seq % NSLOT)
            return (eng, 0)

        def val_of(eng, seq):
            if eng in ("sp", "pool"):
                return 16 * (seq // NSLOT + 1)
            return cum[eng][seq] * INC[eng]

        waits = [[] for _ in range(n)]
        waited2 = {}
        if self.SERIALIZE:
            last = {}
            for i in range(n):
                eng, myseq = eng_seq[i]
                for key, val in list(last.items()):
                    if key[0] == eng and key[0] not in ("sp", "pool"):
                        continue
                    if sem_of(*eng_seq[i]) == key:
                        continue
                    wk = (eng, key)
                    if waited2.get(wk, 0) < val:
                        waited2[wk] = val
                        waits[i].append((key, val))
                last[sem_of(eng, myseq)] = val_of(eng, myseq)
        else:
            for i in range(n):
                eng = eng_seq[i][0]
                for peng, pseq in sorted(deps[i].items()):
                    key = sem_of(peng, pseq)
                    val = val_of(peng, pseq)
                    wk = (eng, key)
                    if waited2.get(wk, 0) < val:
                        waited2[wk] = val
                        waits[i].append((key, val))

        gate = [None] * n
        for i in range(n):
            eng, myseq = eng_seq[i]
            if eng in ("sp", "pool") and myseq >= NSLOT:
                gate[i] = (sem_of(eng, myseq), 16 * (myseq // NSLOT))

        for eng in ENGS:
            ops_list = engine_ops[eng]
            if not ops_list and eng != "sp":
                continue

            def make_body(eng, ops_list):
                def body(e):
                    for i in ops_list:
                        if gate[i] is not None:
                            key, val = gate[i]
                            e.wait_ge(sems[key], val)
                        for (key, val) in waits[i]:
                            e.wait_ge(sems[key], val)
                        inst = self.ops[i][1](e)
                        myeng, myseq = eng_seq[i]
                        if need_inc[i] or myeng in ("sp", "pool"):
                            inst.then_inc(sems[sem_of(myeng, myseq)], INC[myeng])
                    if eng in ("sp", "pool"):
                        nops = len(ops_list)
                        for sl in range(min(NSLOT, nops)):
                            cnt_s = (nops - sl + NSLOT - 1) // NSLOT
                            e.wait_ge(sems[(eng, sl)], 16 * cnt_s)
                return body

            getattr(block, BLOCKNAME[eng])(make_body(eng, ops_list))


def build_program(dbg=(), phases=tuple(range(1, 14))):
    nc = bass.Bass("TRN2", target_bir_lowering=False)
    dt4, dt2 = 4, 2

    def din(name, shape, dt=F32):
        return nc.dram_tensor(name, shape, dt, kind="ExternalInput")

    xT = din("xT", [BLOC * D, L])
    wqkvT = din("wqkvT", [D, 3 * D])
    woT = din("woT", [D, D])
    u1 = din("u1", [D, RK]); v1 = din("v1", [RK, DFF])
    u2 = din("u2", [DFF, RK]); v2 = din("v2", [RK, D])
    wvgsT = din("wvgsT", [D, Hn])
    b1c_in = din("b1c", [128, 32]); b2c_in = din("b2c", [128, 8])
    wcols_in = din("wcols", [128, 512])
    wltmT = din("wltmT", [Hn, DK, DK])
    sel_in = din("selc", [32, 128, 64], BF16)
    blk_in = din("blkc", [8, 128, 16])
    bc16_in = din("bc16c", [8, 16, 128]); bc16n_in = din("bc16nc", [8, 16, 128])
    onesdiv_in = din("onesdiv", [128, 1])
    bc1_in = din("bc1", [1, 128]); bc1n_in = din("bc1n", [1, 128])
    ones16_in = din("ones16", [1, 16])
    bt_in = din("betatau", [1, 16]); cc_in = din("cch", [1, 16])
    gm_in = din("gammav", [1, 16]); bg_in = din("betagm", [1, 16])
    al_in = din("alphap", [1, 1]); ic_in = din("icth", [1, 1])
    vgsb_in = din("vgsbase", [16]); vt0_in = din("vt0", [16])

    outT = nc.dram_tensor("outT", [BLOC * D, L], F32, kind="ExternalOutput")

    def scr(name, shape, dt=F32):
        kind = "ExternalOutput" if name in dbg else "Internal"
        return nc.dram_tensor(name, shape, dt, kind=kind)

    x1s = scr("x1s", [BLOC * D, L]); x2s = scr("x2s", [BLOC * D, L])
    qs = scr("qs", [BLOC * D, L]); ks = scr("ks", [BLOC * D, L]); vs = scr("vs", [BLOC * D, L])
    ss = scr("ss", [BLOC * Hn, L])
    ys = scr("ys", [BLOC * D, L], BF16)
    xlns = scr("xlns", [BLOC * D, L], BF16)
    p1s = scr("p1s", [BLOC * RK, L], BF16)
    q2s = scr("q2s", [BLOC * RK, L], BF16)
    hs = scr("hs", [BLOC * DFF, L], BF16)
    wb_qkv = scr("wb_qkv", [D, 3 * D], BF16)
    wb_wo = scr("wb_wo", [D, D], BF16)
    wb_u1 = scr("wb_u1", [D, RK], BF16)
    wb_v1 = scr("wb_v1", [RK, DFF], BF16)
    wb_u2h = scr("wb_u2h", [DFF, RK], BF16)
    wb_v2 = scr("wb_v2", [RK, D], BF16)
    wb_vgs = scr("wb_vgs", [D, Hn], BF16)
    cbounce = scr("cbounce", [4, 16])

    bd = Builder()

    with ExitStack() as ctx:
        ARF_h = ctx.enter_context(nc.sbuf_tensor("ARF", [128, 38912], F32))
        ARB_h = ctx.enter_context(nc.sbuf_tensor("ARB", [128, 27648], BF16))
        PS_h = ctx.enter_context(nc.psum_tensor("PS", [128, 4096], F32))
        sems = {}
        for e in ("pe", "act", "dve"):
            sems[(e, 0)] = ctx.enter_context(nc.semaphore("sem_" + e))
        for e in ("sp", "pool"):
            for sl in range(Builder.NSLOT):
                sems[(e, sl)] = ctx.enter_context(nc.semaphore(f"sem_{e}{sl}"))
        block = ctx.enter_context(nc.Block())

        def F(p0, np_, c0, nc_):
            return View(ARF_h, p0, np_, c0, nc_, 4)

        def Bv(p0, np_, c0, nc_):
            return View(ARB_h, p0, np_, c0, nc_, 2)

        def P(p0, np_, c0, nc_):
            return View(PS_h, p0, np_, c0, nc_, 4)

        v_sbc = F(0, 128, 0, 4096)
        v_krep = F(0, 128, 4096, 4096)
        v_qrep = F(0, 128, 8192, 4096)
        v_vh = F(64, 64, 12288, 4096)
        v_bc16 = F(0, 16, 12288, 1024)
        v_bc16n = F(0, 16, 13312, 1024)
        v_blk = F(0, 128, 14336, 128)
        CH = 16384
        v_sp = [F(0, 128, CH, T2), F(0, 128, CH + T2, T2)]
        v_tsig = F(0, 128, CH + 2 * T2, T2)
        v_sgn = F(0, 128, CH + 3 * T2, T2)
        v_R = [F(0, 128, CH + 4 * T2, T2), F(0, 128, CH + 5 * T2, T2)]
        v_evr = F(0, 128, CH + 6 * T2, T2)
        v_g1 = F(0, 128, CH + 7 * T2, T2)
        v_cc = F(0, 128, CH + 8 * T2, T2)
        v_ff = F(0, 128, CH + 9 * T2, T2)
        v_carry = F(0, 128, 36864, 8)
        PC = 36880
        v_wcols = F(0, 128, PC, 512)
        v_w05 = F(0, 128, PC + 512, 512)
        v_cexp = F(0, 128, PC + 1024, 16)
        v_tsc = F(0, 128, PC + 1040, 16)
        v_tsb = F(0, 128, PC + 1056, 16)
        v_b1c = F(0, 128, PC + 1072, 32)
        v_b2hc = F(0, 128, PC + 1104, 8)
        v_onesdiv = F(0, 128, PC + 1112, 1)
        v_bc1 = F(0, 1, PC + 1120, 128)
        v_bc1n = F(0, 1, PC + 1248, 128)
        v_vpreK = F(0, 128, PC + 1376, 8)
        v_sb16 = F(0, 16, PC + 1384, 1)
        v_wtmp = F(0, 128, PC + 1392, 16)
        v_row1 = F(0, 1, PC + 1408, 16)
        v_row2 = F(0, 1, PC + 1424, 16)
        v_row3 = F(0, 1, PC + 1440, 16)
        st_x = F(0, 128, 0, 4096)
        st_x2 = F(0, 128, 4096, 4096)
        st_ub = F(0, 128, 34816, 512)
        st_u2b = F(0, 128, 35328, 512)
        st_t1b = F(0, 128, 35840, 512)
        st_u = F(0, 128, 24576, 512)
        st_u2 = F(0, 128, 25088, 512)
        st_p1g = F(0, 128, 25600, 512)
        st_p2 = F(0, 128, 26112, 512)
        st_t1 = F(0, 128, 26624, 512)
        st_t2 = F(0, 128, 27136, 512)
        r_mrow = F(0, 1, 27648, 512)
        r_vrow = F(0, 1, 28160, 512)
        r_msq = F(0, 1, 28672, 512)
        r_varr = F(0, 1, 29184, 512)
        r_lnv = F(0, 1, 29696, 512)
        r_rrow = F(0, 1, 30208, 512)
        r_nmrow = F(0, 1, 30720, 512)
        k_m16 = F(0, 16, 31232, 512)
        k_v16 = F(0, 16, 31744, 512)
        k_msq = F(0, 16, 32256, 512)
        k_var = F(0, 16, 32768, 512)
        k_lnv = F(0, 16, 33280, 512)
        k_r16 = F(0, 16, 33792, 512)
        k_nm16 = F(0, 16, 34304, 512)
        st_kraw = F(0, 128, CH, 4096)
        st_sq = F(0, 128, CH + 4096, 4096)
        st_wf = F(0, 128, CH + 8192, 3072)

        b_m = Bv(0, 128, 0, T2)
        b_sel = Bv(0, 128, 2048, 2048)
        b_wq = Bv(0, 64, 4096, 1024)
        b_qbf = Bv(0, 64, 5120, 4096)
        b_ybf = Bv(0, 64, 9216, 4096)
        b_xmov = Bv(0, 128, 13312, 4096)
        b_xmov2 = Bv(0, 128, 5120, 4096)
        b_work2 = Bv(0, 128, 17408, 2048)
        b_slab = Bv(0, 128, 19456, 8192)
        b_wcast = Bv(0, 128, 13312, 3072)

        class BCView:
            def __init__(self, c0, nc_f32):
                self.c0, self.ncf = c0, nc_f32

            def ap(self):
                return ARB_h[0:128, self.c0:self.c0 + 2 * self.ncf].bitcast(F32)

            def reg(self):
                return (id(ARB_h), 0, 128, self.c0 * 2, (self.c0 + 2 * self.ncf) * 2)

        v_vrep = [BCView(19456, T2), BCView(19456 + 2 * T2, T2)]

        ps_bank = [P(0, 128, 512 * i, 512) for i in range(8)]
        y_ps = [P(0, 64, 0, 2048), P(0, 64, 2048, 2048)]

        op = bd.op

        def dma(dst_ap, src_ap, reads, writes, eng="sp"):
            op(eng, reads, writes, lambda e, d=dst_ap, s=src_ap: e.dma_start(out=d, in_=s))

        rb = {
            id(xT): L * 4, id(outT): L * 4, id(x1s): L * 4, id(x2s): L * 4,
            id(qs): L * 4, id(ks): L * 4, id(vs): L * 4, id(ss): L * 4,
            id(ys): L * 2, id(xlns): L * 2, id(p1s): L * 2, id(q2s): L * 2, id(hs): L * 2,
        }

        def drg(h, r0, r1):
            return dreg(h, r0, r1, rb[id(h)])

        def wreg(h):
            return (id(h), 0, 1, 0, 1 << 40)

        v_eps16 = F(0, 16, PC + 1600, 1)
        op("dve", [], [v_eps16],
           lambda e: e.memset(v_eps16.ap(), EPS))
        dma(v_onesdiv.ap(), onesdiv_in[:, :], [wreg(onesdiv_in)], [v_onesdiv])
        dma(v_bc1.ap(), bc1_in[:, :], [wreg(bc1_in)], [v_bc1])
        dma(v_bc1n.ap(), bc1n_in[:, :], [wreg(bc1n_in)], [v_bc1n])
        dma(v_b1c.ap(), b1c_in[:, :], [wreg(b1c_in)], [v_b1c])
        dma(v_wcols.ap(), wcols_in[:, :], [wreg(wcols_in)], [v_wcols])
        op("dve", [v_wcols], [v_w05],
           lambda e: e.tensor_scalar(out=v_w05.ap(), in0=v_wcols.ap(), scalar1=0.5,
                                     scalar2=None, op0=ALU.mult))
        tmpb2 = F(0, 128, PC + 1456, 8)
        dma(tmpb2.ap(), b2c_in[:, :], [wreg(b2c_in)], [tmpb2])
        op("dve", [tmpb2], [v_b2hc],
           lambda e: e.tensor_scalar(out=v_b2hc.ap(), in0=tmpb2.ap(), scalar1=0.5,
                                     scalar2=None, op0=ALU.mult))
        src = bass.AP(bc16_in, 0, [[128, 16], [2048, 8], [1, 128]])
        dma(v_bc16.ap(), src, [wreg(bc16_in)], [v_bc16])
        src = bass.AP(bc16n_in, 0, [[128, 16], [2048, 8], [1, 128]])
        dma(v_bc16n.ap(), src, [wreg(bc16n_in)], [v_bc16n])
        src = bass.AP(blk_in, 0, [[16, 128], [2048, 8], [1, 16]])
        dma(v_blk.ap(), src, [wreg(blk_in)], [v_blk])
        src = bass.AP(sel_in, 0, [[64, 128], [8192, 32], [1, 64]])
        dma(b_sel.ap(), src, [wreg(sel_in)], [b_sel])
        wq_f = F(0, 64, CH, 1024)
        src = bass.AP(wltmT, 0, [[64, 64], [4096, 16], [1, 64]])
        dma(wq_f.ap(), src, [wreg(wltmT)], [wq_f])
        op("dve", [wq_f], [b_wq],
           lambda e: e.tensor_copy(out=b_wq.ap(), in_=wq_f.ap()))
        rbt = F(0, 1, PC + 1472, 16); rcc = F(0, 1, PC + 1488, 16)
        rgm = F(0, 1, PC + 1504, 16); rbg = F(0, 1, PC + 1520, 16)
        ral = F(0, 1, PC + 1536, 1); ric = F(0, 1, PC + 1537, 1)
        r16o = F(0, 1, PC + 1538, 16)
        rt1 = F(0, 1, PC + 1554, 16); rt2 = F(0, 1, PC + 1570, 16)
        dma(rbt.ap(), bt_in[:, :], [wreg(bt_in)], [rbt])
        dma(rcc.ap(), cc_in[:, :], [wreg(cc_in)], [rcc])
        dma(rgm.ap(), gm_in[:, :], [wreg(gm_in)], [rgm])
        dma(rbg.ap(), bg_in[:, :], [wreg(bg_in)], [rbg])
        dma(ral.ap(), al_in[:, :], [wreg(al_in)], [ral])
        dma(ric.ap(), ic_in[:, :], [wreg(ic_in)], [ric])
        dma(r16o.ap(), ones16_in[:, :], [wreg(ones16_in)], [r16o])
        op("dve", [rcc], [rt1], lambda e: e.reciprocal(out=rt1.sub(0, 1, 0, 16).ap(), in_=rcc.ap()))
        op("dve", [rbt, rt1], [rt2],
           lambda e: e.tensor_tensor(out=rt2.ap(), in0=rbt.ap(), in1=rt1.ap(), op=ALU.mult))
        op("dve", [rt2], [rt2],
           lambda e: e.tensor_scalar(out=rt2.ap(), in0=rt2.ap(), scalar1=-1.0,
                                     scalar2=None, op0=ALU.mult))
        dma(cbounce[0:1, :], rt2.ap(), [rt2], [wreg(cbounce)])
        op("dve", [rbt, ral], [rt1],
           lambda e: e.tensor_scalar(out=rt1.ap(), in0=rbt.ap(), scalar1=ral.ap(),
                                     scalar2=None, op0=ALU.mult))
        dma(cbounce[1:2, :], rt1.ap(), [rt1, wreg(cbounce)], [wreg(cbounce)])
        op("dve", [r16o, ral, ric], [rt1],
           lambda e: e.tensor_scalar(out=rt1.ap(), in0=r16o.ap(), scalar1=ral.ap(),
                                     scalar2=ric.ap(), op0=ALU.mult, op1=ALU.mult))
        op("dve", [rt1], [rt1],
           lambda e: e.tensor_scalar(out=rt1.ap(), in0=rt1.ap(), scalar1=-1.0,
                                     scalar2=None, op0=ALU.mult))
        dma(cbounce[2:3, :], rt1.ap(), [rt1, wreg(cbounce)], [wreg(cbounce)])
        op("dve", [rgm, rbg], [rt1],
           lambda e: e.tensor_tensor(out=rt1.ap(), in0=rgm.ap(), in1=rbg.ap(), op=ALU.mult))
        op("dve", [rt1], [rt1],
           lambda e: e.tensor_scalar(out=rt1.ap(), in0=rt1.ap(), scalar1=0.5,
                                     scalar2=None, op0=ALU.mult))
        dma(cbounce[3:4, :], rt1.ap(), [rt1, wreg(cbounce)], [wreg(cbounce)])
        dma(v_cexp.ap(), cbounce[0:1, :].partition_broadcast(128), [wreg(cbounce)], [v_cexp])
        dma(v_tsc.ap(), cbounce[1:2, :].partition_broadcast(128), [wreg(cbounce)], [v_tsc])
        dma(v_tsb.ap(), cbounce[2:3, :].partition_broadcast(128), [wreg(cbounce)], [v_tsb])
        for dd in range(2):
            dst_ap = v_vpreK.sub(64 * dd, 64, 0, 8).ap()
            src_ap = bass.AP(cbounce, 3 * 16 + dd, [[0, 64], [2, 8]])
            op("sp", [wreg(cbounce)], [v_vpreK],
               lambda e, d=dst_ap, s=src_ap: e.dma_start(
                   out=d, in_=s, allow_slow_non_contiguous=True))
        sb_a = F(0, 16, PC + 1586, 1); sb_b = F(0, 16, PC + 1587, 1)
        dma(sb_a.ap(), bass.AP(vgsb_in, 0, [[1, 16], [1, 1]]), [wreg(vgsb_in)], [sb_a])
        dma(sb_b.ap(), bass.AP(vt0_in, 0, [[1, 16], [1, 1]]), [wreg(vt0_in)], [sb_b])
        op("dve", [sb_a, sb_b], [v_sb16],
           lambda e: e.tensor_tensor(out=v_sb16.ap(), in0=sb_a.ap(), in1=sb_b.ap(),
                                     op=ALU.subtract))

        def wcast(src_h, dst_h, rows, cols, scale=None):
            for r0 in range(0, rows, 128):
                for c0 in range(0, cols, 3072):
                    cw = min(3072, cols - c0)
                    fsrc = st_wf.sub(0, 128, 0, cw)
                    bdst = b_wcast.sub(0, 128, 0, cw)
                    dma(fsrc.ap(), src_h[r0:r0 + 128, c0:c0 + cw], [wreg(src_h)], [fsrc])
                    if scale is None:
                        op("dve", [fsrc], [bdst],
                           lambda e, a=bdst.ap(), b_=fsrc.ap(): e.tensor_copy(out=a, in_=b_))
                    else:
                        op("dve", [fsrc], [bdst],
                           lambda e, a=bdst.ap(), b_=fsrc.ap(), s=scale: e.tensor_scalar(
                               out=a, in0=b_, scalar1=s, scalar2=None, op0=ALU.mult))
                    dma(dst_h[r0:r0 + 128, c0:c0 + cw], bdst.ap(), [bdst], [wreg(dst_h)])

        wcast(wqkvT, wb_qkv, D, 3 * D)
        wcast(woT, wb_wo, D, D)
        wcast(u1, wb_u1, D, RK)
        wcast(v1, wb_v1, RK, DFF)
        wcast(u2, wb_u2h, DFF, RK, scale=0.25)
        wcast(v2, wb_v2, RK, D)
        wcast(wvgsT, wb_vgs, D, Hn)

        def load_slab(src_h, col0, nk, nm, src_cols):
            for kc in range(nk):
                dstv = b_slab.sub(0, 128, kc * nm * 128, nm * 128)
                dma(dstv.ap(), src_h[kc * 128:(kc + 1) * 128, col0:col0 + nm * 128],
                    [wreg(src_h)], [dstv])

        def slab_tile(kc, ml, nm):
            return b_slab.sub(0, 128, (kc * nm + ml) * 128, 128)

        def mm(psv, lhv, rhv, start, stop, extra_reads=()):
            op("pe", [lhv, rhv] + list(extra_reads), [psv],
               lambda e, o=psv.ap(), l=lhv.ap(), r=rhv.ap(), st=start, sp=stop:
               e.matmul(out=o, lhsT=l, rhs=r, start=st, stop=sp))

        def act(outv, inv, func, bias=0.0, scale=1.0, extra_reads=()):
            br = [bias] if isinstance(bias, View) else []
            sr = [scale] if isinstance(scale, View) else []
            bap = bias.ap() if isinstance(bias, View) else bias
            sap = scale.ap() if isinstance(scale, View) else scale
            if func == AF.Copy and br:
                assert not sr and scale == 1.0
                op("act", [inv] + br + list(extra_reads), [outv],
                   lambda e, o=outv.ap(), i=inv.ap(), b_=bap:
                   e.add(out=o, in_=i, add=b_))
            else:
                op("act", [inv] + br + sr + list(extra_reads), [outv],
                   lambda e, o=outv.ap(), i=inv.ap(), f=func, b_=bap, s=sap:
                   e.activation(out=o, in_=i, func=f, bias=b_, scale=s))

        def tt(outv, av, bv, alu):
            op("dve", [av, bv], [outv],
               lambda e, o=outv.ap(), a=av.ap(), b_=bv.ap(), u=alu:
               e.tensor_tensor(out=o, in0=a, in1=b_, op=u))

        def ts(outv, av, s1, s2, op0, op1=None):
            s1r = [s1] if isinstance(s1, View) else []
            s1ap = s1.ap() if isinstance(s1, View) else s1
            if op1 is None:
                op("dve", [av] + s1r, [outv],
                   lambda e, o=outv.ap(), a=av.ap(), s=s1ap, u=op0:
                   e.tensor_scalar(out=o, in0=a, scalar1=s, scalar2=None, op0=u))
            else:
                op("dve", [av] + s1r, [outv],
                   lambda e, o=outv.ap(), a=av.ap(), s=s1ap, s2_=s2, u0=op0, u1=op1:
                   e.tensor_scalar(out=o, in0=a, scalar1=s, scalar2=s2_, op0=u0, op1=u1))

        def stt(outv, av, sc, bv, op0, op1):
            op("dve", [av, bv], [outv],
               lambda e, o=outv.ap(), a=av.ap(), s=sc, b_=bv.ap(), u0=op0, u1=op1:
               e.scalar_tensor_tensor(out=o, in0=a, scalar=s, in1=b_, op0=u0, op1=u1))

        def layernorm(xtiles, sqv, xlnouts):
            whole = xtiles[0].sub(0, 128, 0, 4096)
            act(sqv, whole, AF.Square)
            for kc in range(8):
                mm(ps_bank[6].sub(0, 1, 0, 512), v_onesdiv, xtiles[kc], kc == 0, kc == 7)
            for kc in range(8):
                mm(ps_bank[7].sub(0, 1, 0, 512), v_onesdiv, sqv.sub(0, 128, kc * 512, 512),
                   kc == 0, kc == 7)
            act(r_mrow, ps_bank[6].sub(0, 1, 0, 512), AF.Copy)
            act(r_vrow, ps_bank[7].sub(0, 1, 0, 512), AF.Copy)
            act(r_msq, r_mrow, AF.Square)
            tt(r_varr, r_vrow, r_msq, ALU.subtract)
            act(r_lnv, r_varr, AF.Ln, bias=v_eps16.sub(0, 1, 0, 1))
            act(r_rrow, r_lnv, AF.Exp, scale=-0.5)
            tt(r_nmrow, r_mrow, r_rrow, ALU.mult)
            mm(ps_bank[4], v_bc1, r_rrow, True, True)
            mm(ps_bank[5], v_bc1n, r_nmrow, True, True)
            for kc in range(8):
                tt(st_t1, xtiles[kc], ps_bank[4], ALU.mult)
                tt(xlnouts[kc], st_t1, ps_bank[5], ALU.add)

        def ph_u1(with_ln):
            load_slab(wb_u1, 0, 8, 2, RK)
            for b in range(BLOC):
                for tt_i in range(L // TOKT):
                    t0 = tt_i * TOKT
                    mvb = b_xmov if (b * 8 + tt_i) % 2 == 0 else b_xmov2
                    xm = [mvb.sub(0, 128, kc * 512, 512) for kc in range(8)]
                    if with_ln:
                        xts = [st_x.sub(0, 128, kc * 512, 512) for kc in range(8)]
                        for kc in range(8):
                            dma(xts[kc].ap(), xT[b * D + kc * 128:b * D + (kc + 1) * 128,
                                                 t0:t0 + TOKT],
                                [wreg(xT)], [xts[kc]])
                        layernorm(xts, st_sq, xm)
                    else:
                        for kc in range(8):
                            dma(xm[kc].ap(), xlns[b * D + kc * 128:b * D + (kc + 1) * 128,
                                                  t0:t0 + TOKT],
                                [drg(xlns, b * D + kc * 128, b * D + (kc + 1) * 128)], [xm[kc]])
                    for m2 in range(2):
                        psv = ps_bank[m2]
                        for kc in range(8):
                            mm(psv, slab_tile(kc, m2, 2), xm[kc], kc == 0, kc == 7)
                        ov = b_work2.sub(0, 128, m2 * 512, 512)
                        op("dve", [psv], [ov],
                           lambda e, o=ov.ap(), i=psv.ap(): e.tensor_copy(out=o, in_=i))
                        dma(p1s[b * RK + m2 * 128:b * RK + (m2 + 1) * 128, t0:t0 + TOKT],
                            ov.ap(), [ov], [drg(p1s, b * RK + m2 * 128, b * RK + (m2 + 1) * 128)])

        def ph_v1gelu():
            load_slab(wb_v1, 0, 2, 32, DFF)
            for b in range(BLOC):
                for tt_i in range(L // TOKT):
                    t0 = tt_i * TOKT
                    mvb = b_xmov if (b * 8 + tt_i) % 2 == 0 else b_xmov2
                    pm = [mvb.sub(0, 128, r2 * 512, 512) for r2 in range(2)]
                    for r2 in range(2):
                        dma(pm[r2].ap(), p1s[b * RK + r2 * 128:b * RK + (r2 + 1) * 128,
                                             t0:t0 + TOKT],
                            [drg(p1s, b * RK + r2 * 128, b * RK + (r2 + 1) * 128)], [pm[r2]])
                    for fc in range(32):
                        psv = ps_bank[fc % 2]
                        uu = st_u if fc % 2 == 0 else st_ub
                        uu2 = st_u2 if fc % 2 == 0 else st_u2b
                        pg = st_p1g if fc % 2 == 0 else st_t1b
                        pp2 = st_p2 if fc % 2 == 0 else st_t2
                        for r2 in range(2):
                            mm(psv, slab_tile(r2, fc, 32), pm[r2], r2 == 0, r2 == 1)
                        act(uu, psv, AF.Copy, bias=v_b1c.sub(0, 128, fc, 1))
                        act(uu2, uu, AF.Square)
                        ts(pg, uu2, GELU_A, 1.0, ALU.mult, ALU.add)
                        tt(pp2, pg, uu, ALU.mult)
                        act(uu2, pp2, AF.Tanh, scale=GELU_C)
                        hv = b_work2.sub(0, 128, (fc % 4) * 512, 512)
                        stt(hv, uu2, 1.0, uu, ALU.add, ALU.mult)
                        dma(hs[b * DFF + fc * 128:b * DFF + (fc + 1) * 128, t0:t0 + TOKT],
                            hv.ap(), [hv], [drg(hs, b * DFF + fc * 128, b * DFF + (fc + 1) * 128)])

        def ph_u2():
            for kc in range(32):
                dstv = b_slab.sub(0, 128, kc * 256, 256)
                dma(dstv.ap(), wb_u2h[kc * 128:(kc + 1) * 128, :], [wreg(wb_u2h)], [dstv])
            for b in range(BLOC):
                for tt_i in range(L // TOKT):
                    t0 = tt_i * TOKT
                    for fg in range(4):
                        mvb = b_xmov if fg % 2 == 0 else b_xmov2
                        hm = [mvb.sub(0, 128, i * 512, 512) for i in range(8)]
                        for i in range(8):
                            fc = fg * 8 + i
                            dma(hm[i].ap(), hs[b * DFF + fc * 128:b * DFF + (fc + 1) * 128,
                                               t0:t0 + TOKT],
                                [drg(hs, b * DFF + fc * 128, b * DFF + (fc + 1) * 128)], [hm[i]])
                        for i in range(8):
                            fc = fg * 8 + i
                            for r2 in range(2):
                                mm(ps_bank[2 + r2], b_slab.sub(0, 128, fc * 256 + r2 * 128, 128),
                                   hm[i], fc == 0, fc == 31)
                    for r2 in range(2):
                        ov = b_work2.sub(0, 128, r2 * 512, 512)
                        op("dve", [ps_bank[2 + r2]], [ov],
                           lambda e, o=ov.ap(), i=ps_bank[2 + r2].ap(): e.tensor_copy(out=o, in_=i))
                        dma(q2s[b * RK + r2 * 128:b * RK + (r2 + 1) * 128, t0:t0 + TOKT],
                            ov.ap(), [ov], [drg(q2s, b * RK + r2 * 128, b * RK + (r2 + 1) * 128)])

        def ph_v2res(res_h, out_h, with_ln):
            load_slab(wb_v2, 0, 2, 8, D)
            for b in range(BLOC):
                for tt_i in range(L // TOKT):
                    t0 = tt_i * TOKT
                    mvb = b_xmov if (b * 8 + tt_i) % 2 == 0 else b_xmov2
                    qm = [mvb.sub(0, 128, r2 * 512, 512) for r2 in range(2)]
                    for r2 in range(2):
                        dma(qm[r2].ap(), q2s[b * RK + r2 * 128:b * RK + (r2 + 1) * 128,
                                             t0:t0 + TOKT],
                            [drg(q2s, b * RK + r2 * 128, b * RK + (r2 + 1) * 128)], [qm[r2]])
                    outs = [st_x.sub(0, 128, mc * 512, 512) for mc in range(8)]
                    for mc in range(8):
                        psv = ps_bank[mc % 2]
                        uu = st_u if mc % 2 == 0 else st_ub
                        rv = st_u2 if mc % 2 == 0 else st_u2b
                        for r2 in range(2):
                            mm(psv, slab_tile(r2, mc, 8), qm[r2], r2 == 0, r2 == 1)
                        act(uu, psv, AF.Copy, bias=v_b2hc.sub(0, 128, mc, 1))
                        dma(rv.ap(), res_h[b * D + mc * 128:b * D + (mc + 1) * 128, t0:t0 + TOKT],
                            [wreg(res_h) if id(res_h) not in rb else
                             drg(res_h, b * D + mc * 128, b * D + (mc + 1) * 128)], [rv])
                        tt(outs[mc], uu, rv, ALU.add)
                        dma(out_h[b * D + mc * 128:b * D + (mc + 1) * 128, t0:t0 + TOKT],
                            outs[mc].ap(), [outs[mc]],
                            [drg(out_h, b * D + mc * 128, b * D + (mc + 1) * 128)
                             if id(out_h) in rb else wreg(out_h)])
                    if with_ln:
                        xm = [b_xmov.sub(0, 128, 2048 + kc * 512, 512) for kc in range(4)] + \
                             [b_work2.sub(0, 128, kc * 512, 512) for kc in range(4)]
                        layernorm(outs, st_sq, xm)
                        for kc in range(8):
                            dma(xlns[b * D + kc * 128:b * D + (kc + 1) * 128, t0:t0 + TOKT],
                                xm[kc].ap(), [xm[kc]],
                                [drg(xlns, b * D + kc * 128, b * D + (kc + 1) * 128)])

        def ph_qkv(mg):
            load_slab(wb_qkv, mg * 1024, 8, 8, 3 * D)
            if mg == 2:
                for kc in range(8):
                    dstv = b_work2.sub(0, 128, kc * 16, 16)
                    dma(dstv.ap(), wb_vgs[kc * 128:(kc + 1) * 128, :], [wreg(wb_vgs)], [dstv])
            for b in range(BLOC):
                for tt_i in range(L // TOKT):
                    t0 = tt_i * TOKT
                    mvb = b_xmov if (b * 8 + tt_i) % 2 == 0 else b_xmov2
                    xm = [mvb.sub(0, 128, kc * 512, 512) for kc in range(8)]
                    for kc in range(8):
                        dma(xm[kc].ap(), xlns[b * D + kc * 128:b * D + (kc + 1) * 128,
                                              t0:t0 + TOKT],
                            [drg(xlns, b * D + kc * 128, b * D + (kc + 1) * 128)], [xm[kc]])
                    if mg == 0:
                        for mc in range(8):
                            psv = ps_bank[mc % 2]
                            uu = st_u if mc % 2 == 0 else st_ub
                            for kc in range(8):
                                mm(psv, slab_tile(kc, mc, 8), xm[kc], kc == 0, kc == 7)
                            act(uu, psv, AF.Copy)
                            dma(qs[b * D + mc * 128:b * D + (mc + 1) * 128, t0:t0 + TOKT],
                                uu.ap(), [uu],
                                [drg(qs, b * D + mc * 128, b * D + (mc + 1) * 128)])
                    elif mg == 1:
                        kraw = [st_kraw.sub(0, 128, mc * 512, 512) for mc in range(8)]
                        for mc in range(8):
                            psv = ps_bank[mc % 2]
                            for kc in range(8):
                                mm(psv, slab_tile(kc, mc, 8), xm[kc], kc == 0, kc == 7)
                            act(kraw[mc], psv, AF.Copy)
                        act(st_sq, st_kraw, AF.Square)
                        for jk in range(8):
                            mm(ps_bank[6].sub(0, 16, 0, 512),
                               v_blk.sub(0, 128, jk * 16, 16), kraw[jk], jk == 0, jk == 7)
                        for jk in range(8):
                            mm(ps_bank[7].sub(0, 16, 0, 512),
                               v_blk.sub(0, 128, jk * 16, 16),
                               st_sq.sub(0, 128, jk * 512, 512), jk == 0, jk == 7)
                        act(k_m16, ps_bank[6].sub(0, 16, 0, 512), AF.Copy)
                        act(k_v16, ps_bank[7].sub(0, 16, 0, 512), AF.Copy)
                        act(k_msq, k_m16, AF.Square)
                        tt(k_var, k_v16, k_msq, ALU.subtract)
                        act(k_lnv, k_var, AF.Ln, bias=v_eps16)
                        act(k_r16, k_lnv, AF.Exp, scale=-0.5)
                        tt(k_nm16, k_m16, k_r16, ALU.mult)
                        for jk in range(8):
                            pr = ps_bank[4 if jk % 2 == 0 else 6]
                            pn = ps_bank[5 if jk % 2 == 0 else 7]
                            mm(pr, v_bc16.sub(0, 16, jk * 128, 128), k_r16, True, True)
                            mm(pn, v_bc16n.sub(0, 16, jk * 128, 128), k_nm16, True, True)
                            klv = st_x.sub(0, 128, 0, 512)
                            tt(st_t1, kraw[jk], pr, ALU.mult)
                            tt(klv, st_t1, pn, ALU.add)
                            dma(ks[b * D + jk * 128:b * D + (jk + 1) * 128, t0:t0 + TOKT],
                                klv.ap(), [klv],
                                [drg(ks, b * D + jk * 128, b * D + (jk + 1) * 128)])
                    else:
                        for mc in range(8):
                            psv = ps_bank[mc % 2]
                            uu = st_u if mc % 2 == 0 else st_ub
                            for kc in range(8):
                                mm(psv, slab_tile(kc, mc, 8), xm[kc], kc == 0, kc == 7)
                            ts(uu, psv, v_vpreK.sub(0, 128, mc, 1), None, ALU.mult)
                            dma(vs[b * D + mc * 128:b * D + (mc + 1) * 128, t0:t0 + TOKT],
                                uu.ap(), [uu],
                                [drg(vs, b * D + mc * 128, b * D + (mc + 1) * 128)])
                        for kc in range(8):
                            mm(ps_bank[6].sub(0, 16, 0, 512),
                               b_work2.sub(0, 128, kc * 16, 16), xm[kc], kc == 0, kc == 7)
                        sv = st_x.sub(0, 16, 512, 512)
                        act(sv, ps_bank[6].sub(0, 16, 0, 512), AF.Copy, bias=v_sb16)
                        dma(ss[b * Hn:(b + 1) * Hn, t0:t0 + TOKT], sv.ap(), [sv],
                            [drg(ss, b * Hn, (b + 1) * Hn)])

        def ph_attn():
            for bh in range(BLOC * Hn):
                b, h = bh // Hn, bh % Hn
                r0 = b * D + h * DK
                dma(v_krep.sub(0, 64, 0, 4096).ap(), ks[r0:r0 + 64, :],
                    [drg(ks, r0, r0 + 64)], [v_krep])
                dma(v_krep.sub(64, 64, 0, 4096).ap(), ks[r0:r0 + 64, :],
                    [drg(ks, r0, r0 + 64)], [v_krep])
                dma(v_qrep.sub(0, 64, 0, 4096).ap(), qs[r0:r0 + 64, :],
                    [drg(qs, r0, r0 + 64)], [v_qrep])
                dma(v_qrep.sub(64, 64, 0, 4096).ap(), qs[r0:r0 + 64, :],
                    [drg(qs, r0, r0 + 64)], [v_qrep])
                dma(v_sbc.ap(), ss[b * Hn + h:b * Hn + h + 1, :].partition_broadcast(128),
                    [drg(ss, b * Hn + h, b * Hn + h + 1)], [v_sbc])
                op("dve", [v_qrep], [b_qbf],
                   lambda e, o=b_qbf.ap(), i=v_qrep.sub(0, 64, 0, 4096).ap():
                   e.tensor_copy(out=o, in_=i))
                cexp_h = v_cexp.sub(0, 128, h, 1)
                tsc_h = v_tsc.sub(0, 128, h, 1)
                tsb_h = v_tsb.sub(0, 128, h, 1)
                wq_h = b_wq.sub(0, 64, h * 64, 64)

                def vrep_dma(c_, th_, buf):
                    for dd in range(2):
                        vrow = r0 + 2 * c_ + dd
                        dma(buf.ap()[64 * dd:64 * (dd + 1), :],
                            vs[vrow:vrow + 1,
                               th_ * T2:th_ * T2 + T2].partition_broadcast(64),
                            [drg(vs, vrow, vrow + 1)], [buf])

                vrep_dma(0, 0, v_vrep[0])
                it = 0
                for c in range(32):
                    wcol = v_wcols.sub(0, 128, h * 32 + c, 1)
                    w05 = v_w05.sub(0, 128, h * 32 + c, 1)
                    for th in range(2):
                        par = it % 2
                        sbc_t = v_sbc.sub(0, 128, th * T2, T2)
                        spv = v_sp[par]
                        Rv = v_R[par]
                        act(v_evr, sbc_t, AF.Exp, bias=wcol)
                        act(spv, v_evr, AF.Ln, bias=1.0)
                        act(v_tsig, sbc_t, AF.Tanh, bias=w05, scale=0.5)
                        act(v_sgn, spv, AF.Tanh, bias=tsb_h, scale=tsc_h)
                        act(Rv, spv, AF.Exp, scale=cexp_h)
                        if it + 1 < 64:
                            nc_, nth = (c, 1) if th == 0 else (c + 1, 0)
                            vrep_dma(nc_, nth, v_vrep[(it + 1) % 2])
                        krep_t = v_krep.sub(0, 128, th * T2, T2)
                        qrep_t = v_qrep.sub(0, 128, th * T2, T2)
                        tt(v_g1, krep_t, v_vrep[it % 2], ALU.mult)
                        stt(v_cc, v_tsig, 1.0, v_sgn, ALU.add, ALU.mult)
                        tt(v_cc, v_cc, spv, ALU.mult)
                        tt(v_g1, v_cc, v_g1, ALU.mult)
                        if th == 0:
                            op("dve", [Rv, v_g1], [v_ff],
                               lambda e, o=v_ff.ap(), r=Rv.ap(), z=v_g1.ap():
                               e.tensor_tensor_scan(out=o, data0=r, data1=z, initial=0.0,
                                                    op0=ALU.mult, op1=ALU.add))
                            op("dve", [v_ff], [v_carry],
                               lambda e, o=v_carry.sub(0, 128, 0, 1).ap(),
                               i=v_ff.sub(0, 128, T2 - 1, 1).ap(): e.tensor_copy(out=o, in_=i))
                        else:
                            op("dve", [Rv, v_g1, v_carry], [v_ff],
                               lambda e, o=v_ff.ap(), r=Rv.ap(), z=v_g1.ap(),
                               ini=v_carry.sub(0, 128, 0, 1).ap():
                               e.tensor_tensor_scan(out=o, data0=r, data1=z, initial=ini,
                                                    op0=ALU.mult, op1=ALU.add))
                        tt(b_m, v_ff, qrep_t, ALU.mult)
                        if c == 0:
                            for s4 in range(4):
                                mm(y_ps[th].sub(0, 64, s4 * 512, 512), wq_h,
                                   b_qbf.sub(0, 64, th * T2 + s4 * 512, 512), True, False)
                        sel_c = b_sel.sub(0, 128, c * 64, 64)
                        for s4 in range(4):
                            mm(y_ps[th].sub(0, 64, s4 * 512, 512), sel_c,
                               b_m.sub(0, 128, s4 * 512, 512), False, c == 31)
                        if c == 31:
                            ybv = b_ybf.sub(0, 64, th * T2, T2)
                            act(ybv, y_ps[th], AF.Copy)
                        it += 1
                dma(ys[r0:r0 + 64, :], b_ybf.ap(), [b_ybf], [drg(ys, r0, r0 + 64)])

        def ph_wo():
            for kc in range(8):
                dstv = b_slab.sub(0, 128, kc * 1024, 1024)
                dma(dstv.ap(), wb_wo[kc * 128:(kc + 1) * 128, :], [wreg(wb_wo)], [dstv])
            for b in range(BLOC):
                for tt_i in range(L // TOKT):
                    t0 = tt_i * TOKT
                    mvb = b_xmov if (b * 8 + tt_i) % 2 == 0 else b_xmov2
                    ym = [mvb.sub(0, 128, kc * 512, 512) for kc in range(8)]
                    for kc in range(8):
                        dma(ym[kc].ap(), ys[b * D + kc * 128:b * D + (kc + 1) * 128, t0:t0 + TOKT],
                            [drg(ys, b * D + kc * 128, b * D + (kc + 1) * 128)], [ym[kc]])
                    outs = [st_x2.sub(0, 128, mc * 512, 512) for mc in range(8)]
                    for mc in range(8):
                        psv = ps_bank[mc % 2]
                        rv = st_u if mc % 2 == 0 else st_ub
                        for kc in range(8):
                            mm(psv, b_slab.sub(0, 128, kc * 1024 + mc * 128, 128), ym[kc],
                               kc == 0, kc == 7)
                        dma(rv.ap(), x1s[b * D + mc * 128:b * D + (mc + 1) * 128, t0:t0 + TOKT],
                            [drg(x1s, b * D + mc * 128, b * D + (mc + 1) * 128)], [rv])
                        tt(outs[mc], rv, psv, ALU.add)
                        dma(x2s[b * D + mc * 128:b * D + (mc + 1) * 128, t0:t0 + TOKT],
                            outs[mc].ap(), [outs[mc]],
                            [drg(x2s, b * D + mc * 128, b * D + (mc + 1) * 128)])
                    xm = [mvb.sub(0, 128, kc * 512, 512) for kc in range(8)]
                    layernorm(outs, st_sq, xm)
                    for kc in range(8):
                        dma(xlns[b * D + kc * 128:b * D + (kc + 1) * 128, t0:t0 + TOKT],
                            xm[kc].ap(), [xm[kc]],
                            [drg(xlns, b * D + kc * 128, b * D + (kc + 1) * 128)])

        phset = set(phases)
        if 1 in phset: ph_u1(with_ln=True)
        if 2 in phset: ph_v1gelu()
        if 3 in phset: ph_u2()
        if 4 in phset: ph_v2res(xT, x1s, True)
        if 5 in phset: ph_qkv(0)
        if 6 in phset: ph_qkv(1)
        if 7 in phset: ph_qkv(2)
        if 8 in phset: ph_attn()
        if 9 in phset: ph_wo()
        if 10 in phset: ph_u1(with_ln=False)
        if 11 in phset: ph_v1gelu()
        if 12 in phset: ph_u2()
        if 13 in phset: ph_v2res(x2s, outT, False)

        bd.emit(nc, block, sems)

    return nc


_host_cache = {}


def kernel(**inputs):
    x = np.asarray(inputs["L0_attn_x"], np.float32)
    W_qkv = np.asarray(inputs["L0_attn_W_qkv"], np.float32)
    W_o = np.asarray(inputs["L0_attn_W_o"], np.float32)
    W_vgs = np.asarray(inputs["L0_attn_W_vgs"], np.float32)
    W_LTM = np.asarray(inputs["L0_attn_W_LTM"], np.float32)
    V_gs_base = np.asarray(inputs["L0_attn_V_gs_base"], np.float32)
    V_T0 = np.asarray(inputs["L0_attn_V_T0"], np.float32)
    beta_tau = np.asarray(inputs["L0_attn_beta_tau"], np.float32)
    beta_gm = np.asarray(inputs["L0_attn_beta_gm"], np.float32)
    C_ch = np.asarray(inputs["L0_attn_C_ch"], np.float32)
    gamma = np.asarray(inputs["L0_attn_gamma"], np.float32)
    alpha = np.asarray(inputs["L0_attn_alpha_ppd"], np.float32)
    ic = np.asarray(inputs["L0_attn_IC_threshold"], np.float32)
    U1 = np.asarray(inputs["L0_ffn_U1"], np.float32)
    V1 = np.asarray(inputs["L0_ffn_V1"], np.float32)
    b1 = np.asarray(inputs["L0_ffn_b1"], np.float32)
    U2 = np.asarray(inputs["L0_ffn_U2"], np.float32)
    V2 = np.asarray(inputs["L0_ffn_V2"], np.float32)
    b2 = np.asarray(inputs["L0_ffn_b2"], np.float32)

    selc = np.zeros((32, 128, 64), ml_dtypes.bfloat16)
    for c in range(32):
        for p in range(128):
            selc[c, p, 2 * c + p // 64] = 1.0
    blkc = np.zeros((8, 128, 16), np.float32)
    bc16c = np.zeros((8, 16, 128), np.float32)
    for jk in range(8):
        for p in range(128):
            hh = 2 * jk + p // 64
            blkc[jk, p, hh] = 1.0 / 64.0
            bc16c[jk, hh, p] = 1.0
    bc16nc = -bc16c
    common = dict(
        wqkvT=np.ascontiguousarray(W_qkv.T),
        woT=np.ascontiguousarray(W_o.T),
        u1=U1, v1=V1, u2=U2, v2=V2,
        wvgsT=np.ascontiguousarray(W_vgs.T),
        b1c=np.ascontiguousarray(b1.reshape(32, 128).T),
        b2c=np.ascontiguousarray(b2.reshape(8, 128).T),
        wcols=np.ascontiguousarray(
            W_LTM.reshape(Hn, 32, 128).transpose(2, 0, 1).reshape(128, 512)),
        wltmT=np.ascontiguousarray(W_LTM.transpose(0, 2, 1)),
        selc=selc, blkc=blkc, bc16c=bc16c, bc16nc=bc16nc,
        onesdiv=np.full((128, 1), 1.0 / 1024.0, np.float32),
        bc1=np.ones((1, 128), np.float32),
        bc1n=-np.ones((1, 128), np.float32),
        ones16=np.ones((1, 16), np.float32),
        betatau=beta_tau.reshape(1, 16), cch=C_ch.reshape(1, 16),
        gammav=gamma.reshape(1, 16), betagm=beta_gm.reshape(1, 16),
        alphap=alpha.reshape(1, 1), icth=ic.reshape(1, 1),
        vgsbase=V_gs_base, vt0=V_T0,
    )
    in_maps = []
    for ci in range(NC_):
        xs = np.ascontiguousarray(
            x[ci * BLOC:(ci + 1) * BLOC].transpose(0, 2, 1).reshape(BLOC * D, L))
        m = dict(common)
        m["xT"] = xs
        in_maps.append(m)

    nc = build_program()
    res = run_bass_kernel_spmd(nc, in_maps, core_ids=list(range(NC_)))
    outs = []
    for ci in range(NC_):
        o = res.results[ci]["outT"].reshape(BLOC, D, L).transpose(0, 2, 1)
        outs.append(o)
    return np.ascontiguousarray(np.concatenate(outs, axis=0)).astype(np.float32)
